# version 02
# baseline (speedup 1.0000x reference)
import numpy as np
import ml_dtypes
from contextlib import ExitStack

import concourse.bass as bass
import concourse.mybir as mybir
from concourse.bass_utils import run_bass_kernel_spmd

F32 = mybir.dt.float32
BF16 = mybir.dt.bfloat16
AF = mybir.ActivationFunctionType
ALU = mybir.AluOpType
AXX = mybir.AxisListType.X

L = 2048
DM = 1024
DI = 2048
NS = 16
SEG = 512
NSEG = L // SEG
NCH = SEG // 128
NE = 8
H = 1024
FF = 2048
BIG = 1.0e30
EPS = 1.0e-5

TAPS = []
RUN_NSEG = NSEG
RUN_FFN = 5
RUN_WIN = True
RUN_CHUNKS = NCH
RUN_LN = 3
RUN_LNT = 16

ENG = ("tensor", "vector", "scalar", "gpsimd", "sync")
STREAM_MOD = {"const": 1, "wout": 1, "xin": 4, "outw": 1,
              "cfl": 1, "rfl": 1, "gfl": 1, "tap": 1}


class Buf:
    __slots__ = ("name", "writers", "readers")

    def __init__(self, name):
        self.name = name
        self.writers = {}
        self.readers = {}


class Plan:
    def __init__(self):
        self.ops = []
        self.per_eng = {e: [] for e in ENG}
        self.stream_counts = {}

    def op(self, eng, fn, reads=(), writes=(), dma_stream=None, tag="",
           small=False):
        gid = len(self.ops)
        if dma_stream is not None:
            k = self.stream_counts.get(dma_stream, 0)
            self.stream_counts[dma_stream] = k + 1
            chan = ("D", dma_stream, k % STREAM_MOD.get(dma_stream, 2))
        else:
            chan = eng
        deps = set()

        def dep_on(gid2):
            p = self.ops[gid2]
            if p["dma"] is None and p["eng"] == eng:
                return
            deps.add(gid2)

        for b in reads:
            for g2 in b.writers.values():
                dep_on(g2)
        for b in writes:
            for g2 in b.writers.values():
                dep_on(g2)
            for g2 in b.readers.values():
                dep_on(g2)
        o = dict(gid=gid, eng=eng, fn=fn, deps=deps, dma=dma_stream,
                 chan=chan, tag=tag, need_inc=False, sem=None, val=None,
                 small=small)
        self.ops.append(o)
        self.per_eng[eng].append(o)
        for b in reads:
            b.readers[chan] = gid
        for b in writes:
            b.writers[chan] = gid
        return gid

    def barrier(self, B):
        bb = B(f"__bar{len(self.ops)}")
        for e in ENG:
            self.op(e, lambda eng: eng.nop(), reads=[], writes=[bb],
                    tag="bar_w")
        for e in ENG:
            self.op(e, None, reads=[bb], writes=[], tag="bar_r")

    def finalize(self, nc, ctx):
        for o in self.ops:
            if o["eng"] == "tensor" and o["dma"] is None and o["fn"] is not None:
                o["need_inc"] = True
            for d in o["deps"]:
                p = self.ops[d]
                assert p["fn"] is not None or p["dma"] is not None, (
                    "dep on no-op " + p["tag"])
                p["need_inc"] = True
        comp_sems = {e: ctx.enter_context(nc.semaphore(f"sem_{e}"))
                     for e in ENG}
        stream_sems = {}
        counters = {e: 0 for e in ENG}
        for o in self.ops:
            if o["dma"] is not None:
                key = o["chan"]
                if key not in stream_sems:
                    stream_sems[key] = [ctx.enter_context(nc.semaphore(
                        f"dma_{key[1]}_{key[2]}")), 0]
                stream_sems[key][1] += 16
                o["need_inc"] = True
                o["sem"] = stream_sems[key][0]
                o["val"] = stream_sems[key][1]
                self.stream_final = getattr(self, "stream_final", {})
                self.stream_final[key] = stream_sems[key][1]
            elif o["need_inc"]:
                counters[o["eng"]] += 1
                o["sem"] = comp_sems[o["eng"]]
                o["val"] = counters[o["eng"]]

    def emit_engine(self, ename, eng):
        waited = {}
        for o in self.per_eng[ename]:
            pass
        for o in self.per_eng[ename]:
            for d in sorted(o["deps"]):
                p = self.ops[d]
                sem, val = p["sem"], p["val"]
                assert sem is not None, f"{o['tag']} dep {p['tag']} no sem"
                if waited.get(sem.name, -1) < val:
                    eng.wait_ge(sem, val)
                    waited[sem.name] = val
            if o["fn"] is None:
                continue
            inst = o["fn"](eng)
            if o.get("small"):
                inst = eng.drain()
            if o["need_inc"]:
                inst.then_inc(o["sem"], 16 if o["dma"] is not None else 1)
        for o in self.per_eng[ename]:
            if o["dma"] is not None:
                sem, val = o["sem"], self.stream_final[o["chan"]]
                if waited.get(sem.name, -1) < val:
                    eng.wait_ge(sem, val)
                    waited[sem.name] = val


def host_prep(inp):
    f = np.float32
    ac = np.ascontiguousarray

    def g(n):
        return np.asarray(inp[n], f)
    W_inT = ac(g("L0_attn_W_in").T)
    winS = ac(W_inT.reshape(DM, 16, 256).transpose(1, 0, 2))
    woutT = ac(g("L0_attn_W_out").T)
    wxdtT = ac(np.concatenate([g("L0_attn_W_x"), g("L0_attn_W_dt")], 0).T)
    convw = ac(g("L0_attn_conv_w").reshape(DI, 4)
               .reshape(16, 128, 4).transpose(1, 0, 2).reshape(128, 64))
    convb = ac(g("L0_attn_conv_b").reshape(16, 128).T)
    dvec = ac(g("L0_attn_D").reshape(16, 128).T)
    alog = ac(g("L0_attn_A_log").reshape(NS, 1))
    bdt = ac(g("L0_attn_b_dt").reshape(NS, 1))
    w1S = ac(g("L0_ffn_W1").reshape(NE, H, 16, 128).transpose(0, 2, 1, 3))
    w2S = ac(g("L0_ffn_W2").reshape(NE, FF, 8, 128).transpose(0, 2, 1, 3))
    b1h = ac(g("L0_ffn_b1").reshape(NE, 16, 128)
             .transpose(2, 0, 1).reshape(128, NE * 16))
    b2 = ac(g("L0_ffn_b2"))
    wrS = ac(g("L0_ffn_Wr").reshape(8, 128, 8).transpose(1, 0, 2))
    br = ac(g("L0_ffn_br").reshape(NE, 1))
    i128 = np.eye(128, dtype=f)
    i16 = np.eye(16, dtype=f)
    tau = np.arange(128)
    m1 = (tau[:, None] <= tau[None, :]).astype(f)
    m2 = (tau[:, None] > tau[None, :]).astype(f)
    ub = (-BIG * (tau[None, :] > tau[:, None])).astype(f)
    ones128 = np.ones((1, 128), dtype=f)
    maskrep = np.tile((tau[:, None] <= tau[None, :]).astype(f), (1, 16))
    return dict(maskrep=maskrep, winS=winS, woutT=woutT, wxdtT=wxdtT, convw=convw,
                convb=convb, dvec=dvec, alog=alog, bdt=bdt, w1S=w1S,
                w2S=w2S, b1h=b1h, b2=b2, wrS=wrS, br=br, i128=i128,
                i16=i16, m1=m1, m2=m2, ub=ub, ones128=ones128)


def build_program(taps=()):
    nc = bass.Bass("TRN2", target_bir_lowering=False,
                   detect_race_conditions=False)
    P = Plan()
    bufs = {}

    def B(name):
        if name not in bufs:
            bufs[name] = Buf(name)
        return bufs[name]

    dx = nc.dram_tensor("x", [L, DM], F32, kind="ExternalInput").ap()
    dwinS = nc.dram_tensor("winS", [16, DM, 256], F32, kind="ExternalInput").ap()
    dwoutT = nc.dram_tensor("woutT", [DI, DM], F32, kind="ExternalInput").ap()
    dwxdtT = nc.dram_tensor("wxdtT", [DI, 48], F32, kind="ExternalInput").ap()
    dconvw = nc.dram_tensor("convw", [128, 64], F32, kind="ExternalInput").ap()
    dconvb = nc.dram_tensor("convb", [128, 16], F32, kind="ExternalInput").ap()
    ddvec = nc.dram_tensor("dvec", [128, 16], F32, kind="ExternalInput").ap()
    dalog = nc.dram_tensor("alog", [NS, 1], F32, kind="ExternalInput").ap()
    dbdt = nc.dram_tensor("bdt", [NS, 1], F32, kind="ExternalInput").ap()
    dw1S = nc.dram_tensor("w1S", [NE, 16, H, 128], F32, kind="ExternalInput").ap()
    dw2S = nc.dram_tensor("w2S", [NE, 8, FF, 128], F32, kind="ExternalInput").ap()
    db1h = nc.dram_tensor("b1h", [128, NE * 16], F32, kind="ExternalInput").ap()
    db2 = nc.dram_tensor("b2", [NE, H], F32, kind="ExternalInput").ap()
    dwrS = nc.dram_tensor("wrS", [128, 8, 8], F32, kind="ExternalInput").ap()
    dbr = nc.dram_tensor("br", [NE, 1], F32, kind="ExternalInput").ap()
    di128 = nc.dram_tensor("i128", [128, 128], F32, kind="ExternalInput").ap()
    di16 = nc.dram_tensor("i16", [16, 16], F32, kind="ExternalInput").ap()
    dm1 = nc.dram_tensor("m1", [128, 128], F32, kind="ExternalInput").ap()
    dm2 = nc.dram_tensor("m2", [128, 128], F32, kind="ExternalInput").ap()
    dub = nc.dram_tensor("ub", [128, 128], F32, kind="ExternalInput").ap()
    dones = nc.dram_tensor("ones128", [1, 128], F32, kind="ExternalInput").ap()
    dmaskrep = nc.dram_tensor("maskrep", [128, 2048], F32, kind="ExternalInput").ap()
    dout = nc.dram_tensor("out", [L, DM], F32, kind="ExternalOutput").ap()
    dx1scr = nc.dram_tensor("x1scr", [L, DM], F32, kind="Internal").ap()
    dtaps = {}
    for (nm, shape, dt) in taps:
        dtaps[nm] = nc.dram_tensor("tap_" + nm, shape, dt,
                                   kind="ExternalOutput").ap()

    def tap(nm, tile, rbuf):
        if nm in dtaps:
            ap_ = tile if isinstance(tile, bass.AP) else tile[:]
            dma("sync", "tap", dtaps[nm], ap_, [rbuf], [B("tapout")],
                tag=f"tap_{nm}")

    gctx = ExitStack()

    def SB(c, n, s, d):
        return c.enter_context(nc.sbuf_tensor(n, s, d))

    def mm(out_ap, lhsT_ap, rhs_ap, pbufs, rbufs, start, stop, tag=""):
        P.op("tensor",
             lambda t, o=out_ap, l=lhsT_ap, r=rhs_ap, s0=start, s1=stop:
                 t.matmul(o, l, r, start=s0, stop=s1),
             reads=rbufs, writes=pbufs, tag=tag)

    def act(out_ap, in_ap, func, rbufs, wbufs, bias=0.0, scale=1.0, tag="",
            small=False):
        P.op("scalar",
             lambda s, o=out_ap, i=in_ap, f=func, b=bias, sc=scale:
                 s.activation(o, i, f, bias=b, scale=sc),
             reads=rbufs, writes=wbufs, tag=tag, small=small)

    def vop(fn, rbufs, wbufs, tag="", small=False):
        P.op("vector", fn, reads=rbufs, writes=wbufs, tag=tag, small=small)

    def dma(eng, stream, out_ap, in_ap, rbufs, wbufs, tag=""):
        P.op(eng, lambda e, o=out_ap, i=in_ap: e.dma_start(o, i),
             reads=rbufs, writes=wbufs, dma_stream=stream, tag=tag)

    def pb_A(lo, hi):
        return [B(f"psA_b{i}") for i in range(lo // 512, (hi - 1) // 512 + 1)]

    def pb_B(lo, hi):
        return [B(f"psB_b{i}") for i in range(lo // 512, (hi - 1) // 512 + 1)]

    c_i128b = SB(gctx, "c_i128b", [128, 128], BF16)
    c_i128f = SB(gctx, "c_i128f", [128, 128], F32)
    c_i16f = SB(gctx, "c_i16f", [16, 16], F32)
    c_m1 = SB(gctx, "c_m1", [128, 128], F32)
    c_m2 = SB(gctx, "c_m2", [128, 128], F32)
    c_ub = SB(gctx, "c_ub", [128, 128], F32)
    c_onesf = SB(gctx, "c_onesf", [1, 128], F32)
    c_onesb = SB(gctx, "c_onesb", [1, 128], BF16)
    c_mrep = SB(gctx, "c_mrep", [128, 2048], BF16)
    c_convw = SB(gctx, "c_convw", [128, 64], F32)
    c_convb = SB(gctx, "c_convb", [128, 16], F32)
    c_dvec = SB(gctx, "c_dvec", [128, 16], F32)
    c_alog = SB(gctx, "c_alog", [NS, 1], F32)
    c_acoef = SB(gctx, "c_acoef", [NS, 8], F32)
    c_bdt = SB(gctx, "c_bdt", [NS, 1], F32)
    c_wxdt = SB(gctx, "c_wxdt", [128, 16, 48], BF16)
    c_wrS = SB(gctx, "c_wrS", [128, 8, 8], BF16)
    c_br = SB(gctx, "c_br", [8, 1], F32)
    c_b1h = SB(gctx, "c_b1h", [128, NE * 16], F32)
    c_b2 = SB(gctx, "c_b2", [NE, H], BF16)
    lnst = SB(gctx, "lnst", [128, 12], F32)
    lnmv = SB(gctx, "lnmv", [128, 2], F32)
    lnrs = SB(gctx, "lnrs", [128, 8], F32)
    psA = gctx.enter_context(nc.psum_tensor("psA", [128, 2048], F32))
    psB = gctx.enter_context(nc.psum_tensor("psB", [128, 2048], F32))

    CB = B("consts")
    for dst, src in [
        (c_i128b[:], di128), (c_i128f[:], di128), (c_i16f[:], di16),
        (c_m1[:], dm1), (c_m2[:], dm2), (c_ub[:], dub),
        (c_onesf[:], dones), (c_onesb[:], dones), (c_mrep[:], dmaskrep),
        (c_convw[:], dconvw), (c_convb[:], dconvb), (c_dvec[:], ddvec),
        (c_alog[:], dalog), (c_bdt[:], dbdt),
        (c_wxdt[:], dwxdtT.rearrange("(a p) k -> p a k", p=128)),
        (c_wrS[:], dwrS), (c_br[:], dbr), (c_b1h[:], db1h), (c_b2[:], db2),
    ]:
        dma("gpsimd", "const", dst, src, [], [CB], tag="const")
    ACF = B("acoef")
    act(lnrs[0:1, 0:8], lnrs[0:1, 0:8], AF.Exp, [], [B("lnrs")], tag="warm_e", small=True)
    act(lnrs[0:1, 0:8], lnrs[0:1, 0:8], AF.Ln, [B("lnrs")], [B("lnrs")],
        tag="warm_l", small=True)
    act(lnrs[0:1, 0:8], lnrs[0:1, 0:8], AF.Silu, [B("lnrs")], [B("lnrs")],
        tag="warm_s", small=True)
    act(lnrs[0:1, 0:8], lnrs[0:1, 0:8], AF.Ln, [B("lnrs")], [B("lnrs")],
        tag="warm_l2", small=True)
    act(c_acoef[:], c_alog[:].broadcast_to([NS, 8]), AF.Exp, [CB], [ACF],
        tag="acoef_exp", small=True)
    vop(lambda v: v.tensor_scalar_mul(c_acoef[:], c_acoef[:], -1.0),
        [ACF], [ACF], tag="acoef_neg", small=True)

    mctx = ExitStack()
    x1t = SB(mctx, "x1t", [128, 2, DM], F32)
    woutT = SB(mctx, "sb_woutT", [128, 16, DM], BF16)
    winSl = SB(mctx, "winSl", [128, 2, 8, 256], BF16)
    x_seg = SB(mctx, "x_seg", [128, NCH, DM], F32)
    ln1tm = SB(mctx, "ln1tm", [128, 2, DM], BF16)
    ln1fm = SB(mctx, "ln1fm", [128, 8, SEG], BF16)
    xi = SB(mctx, "xi", [128, 16, SEG + 3], BF16)
    xi_sv = SB(mctx, "xi_sv", [128, 16, 3], BF16)
    zfm = SB(mctx, "zfm", [128, 16, SEG], BF16)
    ufm = SB(mctx, "ufm", [128, 16, SEG], BF16)
    t1c = SB(mctx, "t1c", [128, SEG], F32)
    utm = SB(mctx, "utm", [128, 2, DI], BF16)
    crow = SB(mctx, "crow", [128, DI], BF16)
    ebuf = SB(mctx, "ebuf", [128, DI], BF16)
    t1e = SB(mctx, "t1e", [128, DI], BF16)
    ybar = SB(mctx, "ybar", [128, 16, 128], BF16)
    bmt = SB(mctx, "bmt", [NS, SEG], F32)
    cmt = SB(mctx, "cmt", [NS, SEG], F32)
    cmtb = SB(mctx, "cmtb", [NS, SEG], BF16)
    delt = SB(mctx, "delt", [NS, SEG], F32)
    lat = SB(mctx, "lat", [NS, SEG], F32)
    dbt = SB(mctx, "dbt", [NS, SEG], F32)
    latm = SB(mctx, "latm", [128, 2, NS], F32)
    dbtm = SB(mctx, "dbtm", [128, 2, NS], F32)
    cumtm = SB(mctx, "cumtm", [128, 2, NS], F32)
    negtm = SB(mctx, "negtm", [128, 2, NS], F32)
    revtm = SB(mctx, "revtm", [128, 2, NS], F32)
    erev = SB(mctx, "erev", [128, 2, NS], F32)
    dbbtm = SB(mctx, "dbbtm", [128, 2, NS], BF16)
    cumt = SB(mctx, "cumt", [NS, 2, 128], F32)
    ect = SB(mctx, "ect", [NS, 2, 128], F32)
    cbart = SB(mctx, "cbart", [NS, 2, 128], BF16)
    daq = SB(mctx, "daq", [NS, 2, 8], F32)
    cumfl = SB(mctx, "cumfl", [1, DI], F32)
    rowfl = SB(mctx, "rowfl", [1, DI], BF16)
    hfp = SB(mctx, "hfp", [NS, DI], F32)
    hbf = SB(mctx, "hbf", [NS, DI], BF16)

    WOUT = B("woutT")
    for ec in range(16):
        dma("gpsimd", "wout", woutT[:, ec, :],
            dwoutT[ec * 128:(ec + 1) * 128, :], [], [WOUT], tag="woutd")
    HB, HBB = B("hfp"), B("hbf")
    vop(lambda v: v.memset(hfp[:], 0.0), [], [HB], tag="h0")
    vop(lambda v: v.memset(hbf[:], 0.0), [], [HBB], tag="hbf0")
    XIB = B("xi")
    vop(lambda v: v.memset(xi[:], 0.0), [], [XIB], tag="xipad0")

    XSEG, LT, LF = B("x_seg"), B("ln1tm"), B("ln1fm")
    ZB, UB_, UD = B("zfm"), B("ufm"), B("xi")
    LS, MV, RS = B("lnst"), B("lnmv"), B("lnrs")

    for s in range(RUN_NSEG):
        sbase = s * SEG
        for ch in range(NCH):
            dma("sync", "xin", x_seg[:, ch, :],
                dx[sbase + ch * 128: sbase + (ch + 1) * 128, :],
                [], [XSEG], tag=f"xin{s}_{ch}")
        for ch in range(NCH if RUN_LN >= 1 else 0):
            vop(lambda v, c=ch: v.bn_stats(lnst[:, 0:6], x_seg[:, c, 0:512]),
                [XSEG], [LS], tag="bn1", small=True)
            vop(lambda v, c=ch: v.bn_stats(lnst[:, 6:12],
                                           x_seg[:, c, 512:1024]),
                [XSEG], [LS], tag="bn2", small=True)
            vop(lambda v: v.bn_aggr(lnmv[:], lnst[:]), [LS], [MV], tag="agg", small=True)
            vop(lambda v: v.tensor_scalar_add(
                lnrs[:], lnmv[:, 1:2].broadcast_to([128, 8]), EPS),
                [MV], [RS], tag="veps", small=True)
            vop(lambda v: v.tensor_scalar_add(lnrs[:], lnrs[:], 0.0),
                [RS], [RS], tag="vbounce", small=True)
            act(lnrs[:], lnrs[:], AF.Ln, [RS], [RS], tag="lnv", small=True)
            vop(lambda v: v.tensor_scalar_mul(lnrs[:], lnrs[:], -0.5),
                [RS], [RS], tag="lnvm", small=True)
            act(lnrs[:], lnrs[:], AF.Exp, [RS], [RS], tag="rstd", small=True)
            if RUN_LN < 2:
                continue
            vop(lambda v, c=ch: v.tensor_scalar(
                ln1tm[:, c % 2, :], x_seg[:, c, :], lnmv[:, 0:1], None,
                ALU.subtract), [XSEG, MV], [LT], tag="ln1sub")
            vop(lambda v, c=ch: v.tensor_scalar_mul(
                ln1tm[:, c % 2, :], ln1tm[:, c % 2, :], lnrs[:, 0:1]),
                [LT, RS], [LT], tag="ln1mul")
            for w in range(2 if RUN_LN >= 3 else 0):
                c0 = w * 512
                pb = pb_A(c0, c0 + 512)
                for j in range(4):
                    dc = w * 4 + j
                    mm(psA[:, c0 + j * 128:c0 + (j + 1) * 128],
                       ln1tm[:, ch % 2, dc * 128:(dc + 1) * 128], c_i128b[:],
                       pb, [LT, CB], start=(j == 0), stop=(j == 3),
                       tag="ln1T")
                vop(lambda v, c=ch, ww=w, cc0=c0: v.tensor_copy(
                    ln1fm[:, ww * 4:(ww + 1) * 4, c * 128:(c + 1) * 128],
                    psA[:, cc0:cc0 + 512].rearrange("p (a n) -> p a n", a=4)),
                    pb, [LF], tag="ln1ev")
        for g in range(16 if RUN_WIN else 0):
            sl = g % 2
            WS = B(f"winSl{sl}")
            dma("gpsimd", "winS", winSl[:, sl, :, :],
                dwinS[g].rearrange("(a p) n -> p a n", p=128),
                [], [WS], tag=f"winS{s}_{g}")
            b0 = sl * 2
            pb0 = pb_A(b0 * 512, b0 * 512 + 512)
            pb1 = pb_A((b0 + 1) * 512, (b0 + 1) * 512 + 512)
            for dc in range(8):
                mm(psA[:, b0 * 512:(b0 + 1) * 512], winSl[:, sl, dc, 0:128],
                   ln1fm[:, dc, :], pb0, [WS, LF],
                   start=(dc == 0), stop=(dc == 7), tag="xzA")
                mm(psA[:, (b0 + 1) * 512:(b0 + 2) * 512],
                   winSl[:, sl, dc, 128:256], ln1fm[:, dc, :], pb1, [WS, LF],
                   start=(dc == 0), stop=(dc == 7), tag="xzB")
            if s > 0 and g == 0:
                vop(lambda v: v.tensor_copy(xi[:, :, 0:3], xi_sv[:]),
                    [B("xi_sv")], [XIB], tag="xipad", small=True)
            if g < 8:
                act(xi[:, 2 * g, 3:SEG + 3], psA[:, b0 * 512:(b0 + 1) * 512],
                    AF.Copy, pb0, [XIB], tag="evxi")
                act(xi[:, 2 * g + 1, 3:SEG + 3],
                    psA[:, (b0 + 1) * 512:(b0 + 2) * 512], AF.Copy, pb1,
                    [XIB], tag="evxi")
            else:
                act(zfm[:, 2 * (g - 8), :], psA[:, b0 * 512:(b0 + 1) * 512],
                    AF.Copy, pb0, [ZB], tag="evz")
                act(zfm[:, 2 * (g - 8) + 1, :],
                    psA[:, (b0 + 1) * 512:(b0 + 2) * 512], AF.Copy, pb1,
                    [ZB], tag="evz")
        T1 = B("t1c")
        for ic in range(16 if RUN_WIN else 0):
            vop(lambda v, i=ic: v.tensor_scalar_mul(
                t1c[:], xi[:, i, 0:SEG], c_convw[:, 4 * i:4 * i + 1]),
                [XIB, CB], [T1], tag="cv0")
            for k in range(1, 4):
                vop(lambda v, i=ic, kk=k: v.scalar_tensor_tensor(
                    t1c[:], xi[:, i, kk:SEG + kk],
                    c_convw[:, 4 * i + kk:4 * i + kk + 1], t1c[:],
                    ALU.mult, ALU.add), [XIB, CB, T1], [T1], tag="cvk")
            act(ufm[:, ic, :], t1c[:], AF.Silu, [T1, CB], [UB_],
                bias=c_convb[:, ic:ic + 1], tag="silu_u")
        if RUN_WIN:
            vop(lambda v: v.tensor_copy(xi_sv[:], xi[:, :, SEG:SEG + 3]),
                [XIB], [B("xi_sv")], tag="xisave", small=True)

        for ic in range(16 if RUN_WIN else 0):
            act(zfm[:, ic, :], zfm[:, ic, :], AF.Silu, [ZB], [ZB],
                tag="silu_z")
        for ic in range(16 if RUN_WIN else 0):
            vop(lambda v, i=ic: v.tensor_scalar_mul(
                xi[:, i, 3:SEG + 3], ufm[:, i, :], c_dvec[:, i:i + 1]),
                [UB_, CB], [UD], tag="uD")
        pWx = pb_A(0, 1536)
        for ec in range(16 if RUN_WIN else 0):
            for j in range(3):
                mm(psA[0:16, j * 512:(j + 1) * 512],
                   c_wxdt[:, ec, j * 16:(j + 1) * 16], ufm[:, ec, :], pWx,
                   [CB, UB_], start=(ec == 0), stop=(ec == 15), tag="wxdt")
        BM, CM, CMB = B("bmt"), B("cmt"), B("cmtb")
        DL, LA, DB = B("delt"), B("lat"), B("dbt")
        if not RUN_WIN:
            continue
        vop(lambda v: v.tensor_copy(bmt[:], psA[0:16, 0:512]), pWx, [BM],
            tag="evbm")
        vop(lambda v: v.tensor_copy(cmt[:], psA[0:16, 512:1024]), pWx, [CM],
            tag="evcm")
        vop(lambda v: v.tensor_copy(cmtb[:], cmt[:]), [CM], [CMB], tag="cmtb")
        act(delt[:], psA[0:16, 1024:1536], AF.Exp, pWx + [CB], [DL],
            bias=c_bdt[:], tag="d_exp")
        vop(lambda v: v.tensor_scalar_add(delt[:], delt[:], 1.0), [DL], [DL],
            tag="d_p1")
        act(delt[:], delt[:], AF.Ln, [DL], [DL], tag="d_ln")
        vop(lambda v: v.tensor_scalar_mul(lat[:], delt[:], c_acoef[:, 0:1]),
            [DL, ACF], [LA], tag="la")
        vop(lambda v: v.tensor_mul(dbt[:], delt[:], bmt[:]), [DL, BM], [DB],
            tag="db")

        for cc in range(min(NCH, RUN_CHUNKS)):
            ci = cc % 2
            ut = cc % 2
            gch = s * NCH + cc
            pb3 = pb_B(1536, 2048)
            mm(psB[:, 1536:1552], lat[:, cc * 128:(cc + 1) * 128], c_i16f[:],
               pb3, [LA, CB], start=True, stop=True, tag="latmM")
            mm(psB[:, 1552:1568], dbt[:, cc * 128:(cc + 1) * 128], c_i16f[:],
               pb3, [DB, CB], start=True, stop=True, tag="dbtmM")
            LTM, DTM = B("latm"), B("dbtm")
            vop(lambda v, i=ci: v.tensor_copy(latm[:, i, :], psB[:, 1536:1552]),
                pb3, [LTM], tag="latmev", small=True)
            vop(lambda v, i=ci: v.tensor_copy(dbtm[:, i, :], psB[:, 1552:1568]),
                pb3, [DTM], tag="dbtmev", small=True)
            mm(psB[:, 1568:1584], c_m1[:], latm[:, ci, :], pb3, [CB, LTM],
               start=True, stop=True, tag="cumM")
            mm(psB[:, 1584:1600], c_m2[:], latm[:, ci, :], pb3, [CB, LTM],
               start=True, stop=True, tag="revM")
            mm(psB[0:16, 1600:1728], latm[:, ci, :], c_m1[:], pb3, [LTM, CB],
               start=True, stop=True, tag="cumTM")
            CUM, REV, CT = B("cumtm"), B("revtm"), B("cumt")
            vop(lambda v, i=ci: v.tensor_copy(cumtm[:, i, :], psB[:, 1568:1584]),
                pb3, [CUM], tag="cumev", small=True)
            NEG = B("negtm")
            vop(lambda v, i=ci: v.tensor_scalar_mul(
                negtm[:, i, :], cumtm[:, i, :], -1.0), [CUM], [NEG],
                tag="negtm", small=True)
            vop(lambda v, i=ci: v.tensor_copy(revtm[:, i, :], psB[:, 1584:1600]),
                pb3, [REV], tag="revev", small=True)
            vop(lambda v, i=ci: v.tensor_copy(cumt[:, i, :],
                                              psB[0:16, 1600:1728]),
                pb3, [CT], tag="cumtev")
            ER, ECT, DAQ = B("erev"), B("ect"), B("daq")
            act(erev[:, ci, :], revtm[:, ci, :], AF.Exp, [REV], [ER], tag="er")
            act(ect[:, ci, :], cumt[:, ci, :], AF.Exp, [CT], [ECT], tag="ec")
            act(daq[:, ci, :], cumt[:, ci, 127:128].broadcast_to([NS, 8]),
                AF.Exp, [CT], [DAQ], tag="daq", small=True)
            DBB, CBT = B("dbbtm"), B("cbart")
            vop(lambda v, i=ci: v.tensor_mul(
                dbbtm[:, i, :], dbtm[:, i, :], erev[:, i, :]), [DTM, ER],
                [DBB], tag="dbb", small=True)
            vop(lambda v, i=ci, c=cc: v.tensor_mul(
                cbart[:, i, :], cmt[:, c * 128:(c + 1) * 128], ect[:, i, :]),
                [CM, ECT], [CBT], tag="cbart")
            CFL, RFL = B("cumfl"), B("rowfl")
            dma("gpsimd", "cfl", cumfl[:], cumt[:, ci, :], [CT], [CFL],
                tag="cfl")
            dma("gpsimd", "rfl", rowfl[:], cmtb[:, cc * 128:(cc + 1) * 128],
                [CMB], [RFL], tag="rfl")
            pD = pb_A(0, 2048)
            for j in range(4):
                mm(psA[:, j * 512:(j + 1) * 512], c_onesf[:],
                   cumfl[:, j * 512:(j + 1) * 512], pD, [CB, CFL],
                   start=True, stop=True, tag="diff")
            EB = B("ebuf")
            for n in range(NS):
                act(ebuf[:, n * 128:(n + 1) * 128],
                    psA[:, n * 128:(n + 1) * 128], AF.Relu, pD + [CUM], [EB],
                    bias=cumtm[:, ci, n:n + 1], scale=-1.0, tag="erelu")
            for n in range(NS):
                act(ebuf[:, n * 128:(n + 1) * 128],
                    ebuf[:, n * 128:(n + 1) * 128], AF.Exp, [EB], [EB],
                    scale=-1.0, tag="eexp")
            vop(lambda v: v.tensor_mul(ebuf[:], ebuf[:], c_mrep[:]),
                [EB, CB], [EB], tag="emask")
            for n in range(NS):
                vop(lambda v, nn=n, i=ci: v.tensor_scalar_mul(
                    ebuf[:, nn * 128:(nn + 1) * 128],
                    ebuf[:, nn * 128:(nn + 1) * 128],
                    dbtm[:, i, nn:nn + 1]), [EB, DTM], [EB], tag="edb")
            UT = B(f"utm{ut}")
            for w in range(4):
                c0 = 1024 + (w % 2) * 512
                pbu = pb_B(c0, c0 + 512)
                for j in range(4):
                    ic = w * 4 + j
                    mm(psB[:, c0 + j * 128:c0 + (j + 1) * 128],
                       ufm[:, ic, cc * 128:(cc + 1) * 128], c_i128b[:], pbu,
                       [UB_, CB], start=(j == 0), stop=(j == 3), tag="uT")
                vop(lambda v, u=ut, ww=w, cc0=c0: v.tensor_copy(
                    utm[:, u, ww * 512:(ww + 1) * 512], psB[:, cc0:cc0 + 512]),
                    pbu, [UT], tag="uTev")
            pC = pD
            for j in range(4):
                mm(psA[:, j * 512:(j + 1) * 512], c_onesb[:],
                   rowfl[:, j * 512:(j + 1) * 512], pC, [CB, RFL],
                   start=True, stop=True, tag="crowM")
            CR = B("crow")
            act(crow[:], psA[:, 0:2048], AF.Copy, pC, [CR], tag="crowev")
            vop(lambda v: v.tensor_mul(ebuf[:], ebuf[:], crow[:]), [EB, CR],
                [EB], tag="wmul")
            for lo, hi in ((1024, 2048), (512, 1024), (256, 512), (128, 256)):
                vop(lambda v, l=lo, h=hi: v.tensor_add(
                    ebuf[:, 0:l], ebuf[:, 0:l], ebuf[:, l:h]), [EB], [EB],
                    tag="fold")
            pY = pb_B(0, 2048)
            for ec in range(16):
                mm(psB[:, ec * 128:(ec + 1) * 128],
                   utm[:, ut, ec * 128:(ec + 1) * 128], ebuf[:, 0:128], pY,
                   [UT, EB], start=(ec % 4 == 0), stop=False, tag="intra")
            for ec in range(16):
                mm(psB[:, ec * 128:(ec + 1) * 128],
                   hbf[:, ec * 128:(ec + 1) * 128], cbart[:, ci, :], pY,
                   [HBB, CBT], start=False, stop=(ec % 4 == 3), tag="inter")
            T1E, YB = B("t1e"), B("ybar")
            vop(lambda v, c=cc: v.tensor_add(
                t1e[:].rearrange("p (a n) -> p a n", a=16),
                psB[:, 0:2048].rearrange("p (a n) -> p a n", a=16),
                xi[:, :, 3 + c * 128:3 + (c + 1) * 128]), pY + [UD], [T1E],
                tag="epi1")
            vop(lambda v, c=cc: v.tensor_mul(
                ybar[:], t1e[:].rearrange("p (a n) -> p a n", a=16),
                zfm[:, :, c * 128:(c + 1) * 128]), [T1E, ZB], [YB], tag="epi2")
            pS = pD
            for j in range(4):
                mm(psA[0:16, j * 512:(j + 1) * 512], dbbtm[:, ci, :],
                   utm[:, ut, j * 512:(j + 1) * 512], pS, [DBB, UT],
                   start=True, stop=True, tag="state")

            vop(lambda v, i=ci: v.scalar_tensor_tensor(
                hfp[:], hfp[:], daq[:, i, 0:1], psA[0:16, 0:2048],
                ALU.mult, ALU.add), [HB, DAQ] + pS, [HB], tag="hupd")

            vop(lambda v: v.tensor_copy(hbf[:], hfp[:]), [HB], [HBB],
                tag="hcast")
            pW = pb_B(0, 1024)
            for ec in range(16):
                for dh in range(2):
                    mm(psB[:, dh * 512:(dh + 1) * 512], ybar[:, ec, :],
                       woutT[:, ec, dh * 512:(dh + 1) * 512], pW, [YB, WOUT],
                       start=(ec == 0), stop=(ec == 15), tag="wout")
            X1C = B("x1t")
            sl = cc % 2
            vop(lambda v, c=cc, s_=sl: v.tensor_add(
                x1t[:, s_, :], psB[:, 0:1024], x_seg[:, c, :]),
                pW + [XSEG], [X1C], tag="resid")
            dma("sync", "x1w", dx1scr[gch * 128:(gch + 1) * 128, :],
                x1t[:, sl, :], [X1C], [B("x1scr")], tag="x1w")

    tap("ufm", ufm, UB_)
    tap("zfm", zfm, ZB)
    tap("lat", lat, B("lat"))
    tap("dbt", dbt, B("dbt"))
    tap("cmt", cmt, B("cmt"))
    tap("cumt", cumt, B("cumt"))
    tap("latm", latm, B("latm"))
    tap("dbbtm", dbbtm, B("dbbtm"))
    tap("cbart", cbart, B("cbart"))
    tap("daq", daq, B("daq"))
    tap("utm", utm, B("utm0"))
    tap("acoef", c_acoef, B("acoef"))
    tap("crow", crow, B("crow"))
    tap("delt", delt, B("delt"))
    tap("hfp", hfp, HB)
    tap("ebuf", ebuf, B("ebuf"))
    tap("ybar", ybar, B("ybar"))
    tap("x1scr", dx1scr, B("x1scr"))
    P.barrier(B)
    mctx.close()

    fctx = ExitStack()
    x1t2 = SB(fctx, "x1t2", [128, 2, DM], F32)
    outf = SB(fctx, "outf", [128, 2, DM], F32)
    ln2fm = SB(fctx, "ln2fm", [128, 8, L], BF16) if RUN_FFN >= 1 else None
    ln2tm = SB(fctx, "ln2tm", [128, 2, DM], BF16)
    logt = SB(fctx, "logt", [8, L], F32)
    lgc = SB(fctx, "lgc", [128, 8], F32)
    g_m1 = SB(fctx, "g_m1", [128, 1], F32)
    g_m2 = SB(fctx, "g_m2", [128, 1], F32)
    g_e1 = SB(fctx, "g_e1", [128, 8], F32)
    g_e2 = SB(fctx, "g_e2", [128, 8], F32)
    g_msk = SB(fctx, "g_msk", [128, 8], F32)
    g_w = SB(fctx, "g_w", [128, 32], F32)
    g_t = SB(fctx, "g_t", [128, 8], F32)
    gatec = SB(fctx, "gatec", [128, 8], BF16)
    gatet = SB(fctx, "gatet", [8, L], BF16)
    gfl = SB(fctx, "gfl", [1, L], BF16)
    if RUN_FFN >= 4:
        grep = SB(fctx, "grep", [128, L], BF16)
        w1fc = SB(fctx, "w1fc", [128, 2, 8, 128], BF16)
        w2dc = SB(fctx, "w2dc", [128, 2, 16, 128], BF16)
        hscl = SB(fctx, "hscl", [128, 16, L], BF16)
        htmp = SB(fctx, "htmp", [128, L], BF16)
        outac = SB(fctx, "outac", [128, 8, L], BF16)
    else:
        grep = w1fc = w2dc = hscl = htmp = outac = None

    if RUN_FFN >= 4:
        P.op("gpsimd", None, reads=[B("x1scr")], writes=[], tag="gfence")
    X1R, LT2, LN2 = B("x1r"), B("ln2tm"), B("ln2fm")
    for tc in range(16 if RUN_FFN >= 1 else 0):
        sl = tc % 2
        dma("sync", "x1r", x1t2[:, sl, :], dx1scr[tc * 128:(tc + 1) * 128, :],
            [B("x1scr")], [X1R], tag="x1r")
        if RUN_LN < 1:
            continue
        vop(lambda v, s_=sl: v.bn_stats(lnst[:, 0:6], x1t2[:, s_, 0:512]),
            [X1R], [LS], tag="bn1f", small=True)
        vop(lambda v, s_=sl: v.bn_stats(lnst[:, 6:12], x1t2[:, s_, 512:1024]),
            [X1R], [LS], tag="bn2f", small=True)
        vop(lambda v: v.bn_aggr(lnmv[:], lnst[:]), [LS], [MV], tag="aggf", small=True)
        vop(lambda v: v.tensor_scalar_add(
            lnrs[:], lnmv[:, 1:2].broadcast_to([128, 8]), EPS),
            [MV], [RS], tag="vepsf", small=True)
        vop(lambda v: v.tensor_scalar_add(lnrs[:], lnrs[:], 0.0),
            [RS], [RS], tag="vbouncef", small=True)
        act(lnrs[:], lnrs[:], AF.Ln, [RS], [RS], tag="lnvf", small=True)
        vop(lambda v: v.tensor_scalar_mul(lnrs[:], lnrs[:], -0.5),
            [RS], [RS], tag="lnvmf", small=True)
        act(lnrs[:], lnrs[:], AF.Exp, [RS], [RS], tag="rstdf", small=True)
        if RUN_LN < 2:
            continue
        vop(lambda v, s_=sl: v.tensor_scalar(
            ln2tm[:, s_, :], x1t2[:, s_, :], lnmv[:, 0:1], None,
            ALU.subtract), [X1R, MV], [LT2], tag="ln2sub")
        vop(lambda v, s_=sl: v.tensor_scalar_mul(
            ln2tm[:, s_, :], ln2tm[:, s_, :], lnrs[:, 0:1]),
            [LT2, RS], [LT2], tag="ln2mul")
        for w in range(2 if (RUN_LN >= 3 and tc < RUN_LNT) else 0):
            c0 = w * 512
            pb = pb_A(c0, c0 + 512)
            for j in range(4):
                dc = w * 4 + j
                mm(psA[:, c0 + j * 128:c0 + (j + 1) * 128],
                   ln2tm[:, sl, dc * 128:(dc + 1) * 128], c_i128b[:], pb,
                   [LT2, CB], start=(j == 0), stop=(j == 3), tag="ln2T")
            vop(lambda v, t_=tc, ww=w, cc0=c0: v.tensor_copy(
                ln2fm[:, ww * 4:(ww + 1) * 4, t_ * 128:(t_ + 1) * 128],
                psA[:, cc0:cc0 + 512].rearrange("p (a n) -> p a n", a=4)),
                pb, [LN2], tag="ln2ev")
    pR = pb_A(0, 2048)
    for dc in range(8 if RUN_FFN >= 2 else 0):
        for ts_ in range(4):
            mm(psA[0:8, ts_ * 512:(ts_ + 1) * 512], c_wrS[:, dc, :],
               ln2fm[:, dc, ts_ * 512:(ts_ + 1) * 512], pR, [CB, LN2],
               start=(dc == 0), stop=(dc == 7), tag="rout")
    LOG = B("logt")
    if RUN_FFN >= 2:
        act(logt[:], psA[0:8, 0:2048], AF.Copy, pR, [LOG], tag="routev")
    if RUN_FFN >= 2:
        vop(lambda v: v.tensor_scalar_add(logt[:], logt[:], c_br[:]),
            [LOG, CB], [LOG], tag="routbr")
    GT = B("gatet")
    for tc in range(16 if RUN_FFN >= 3 else 0):
        pb3 = pb_B(1536, 2048)
        mm(psB[0:128, 1536:1544], logt[:, tc * 128:(tc + 1) * 128],
           c_i16f[0:8, 0:8], pb3, [LOG, CB], start=True, stop=True,
           tag="gT")
        LGC = B("lgc")
        vop(lambda v: v.tensor_copy(lgc[:], psB[:, 1536:1544]), pb3, [LGC],
            tag="lgcev", small=True)
        M1_, M2_ = B("g_m1"), B("g_m2")
        E1, E2, MSK = B("g_e1"), B("g_e2"), B("g_msk")
        W_, T_, GC = B("g_w"), B("g_t"), B("gatec")
        vop(lambda v: v.reduce_max(g_m1[:], lgc[:], axis=AXX), [LGC], [M1_],
            tag="gm1", small=True)
        vop(lambda v: v.tensor_scalar(g_e1[:], lgc[:], g_m1[:], None,
                                      ALU.is_equal), [LGC, M1_], [E1],
            tag="ge1", small=True)
        vop(lambda v: v.scalar_tensor_tensor(g_msk[:], g_e1[:], -BIG, lgc[:],
                                             ALU.mult, ALU.add), [E1, LGC],
            [MSK], tag="gmsk", small=True)
        vop(lambda v: v.reduce_max(g_m2[:], g_msk[:], axis=AXX), [MSK],
            [M2_], tag="gm2", small=True)
        vop(lambda v: v.tensor_scalar(g_e2[:], lgc[:], g_m2[:], None,
                                      ALU.is_equal), [LGC, M2_], [E2],
            tag="ge2", small=True)
        vop(lambda v: v.tensor_tensor(
            g_w[:, 0:8], g_m2[:].broadcast_to([128, 8]),
            g_m1[:].broadcast_to([128, 8]), ALU.subtract), [M1_, M2_], [W_],
            tag="gd", small=True)
        act(g_w[:, 8:16], g_w[:, 0:8], AF.Exp, [W_], [W_], scale=-1.0,
            tag="gexp", small=True)
        vop(lambda v: v.tensor_scalar_add(g_w[:, 8:16], g_w[:, 8:16], 1.0),
            [W_], [W_], tag="gp1", small=True)
        vop(lambda v: v.reciprocal(g_w[:, 16:24], g_w[:, 8:16]), [W_], [W_],
            tag="grec", small=True)
        vop(lambda v: v.tensor_scalar(g_w[:, 24:32], g_w[:, 16:24], -1.0,
                                      1.0, ALU.mult, ALU.add), [W_], [W_],
            tag="gw1", small=True)
        vop(lambda v: v.tensor_scalar_mul(g_t[:], g_e2[:], g_w[:, 16:17]),
            [E2, W_], [T_], tag="gt2", small=True)
        vop(lambda v: v.scalar_tensor_tensor(gatec[:], g_e1[:], g_w[:, 24:25],
                                             g_t[:], ALU.mult, ALU.add),
            [E1, W_, T_], [GC], tag="gate", small=True)
        mm(psB[0:8, 1544:1672], gatec[:], c_i128b[:], pb3, [GC, CB],
           start=True, stop=True, tag="gTT")
        vop(lambda v, t_=tc: v.tensor_copy(gatet[:, t_ * 128:(t_ + 1) * 128],
                                           psB[0:8, 1544:1672]), pb3, [GT],
            tag="gatetev")
    tap("logt", logt, LOG)
    tap("gatet", gatet, GT)
    OUT = B("outac")
    GFL, GR, HS, HT = B("gfl"), B("grep"), B("hscl"), B("htmp")
    for e in range(NE if RUN_FFN >= 4 else 0):
        dma("gpsimd", "gfl", gfl[:], gatet[e:e + 1, :], [GT], [GFL],
            tag="gfl")
        pG = pb_A(0, 2048)
        for j in range(4):
            mm(psA[:, j * 512:(j + 1) * 512], c_onesb[:],
               gfl[:, j * 512:(j + 1) * 512], pG, [CB, GFL],
               start=True, stop=True, tag="grepM")
        act(grep[:], psA[:, 0:2048], AF.Copy, pG, [GR], tag="grepev")
        for fc in range(16):
            sl = fc % 2
            WF = B(f"w1fc{sl}")
            dma("gpsimd", "w1", w1fc[:, sl, :, :],
                dw1S[e, fc].rearrange("(a p) n -> p a n", p=128), [], [WF],
                tag="w1d")
            ps, pbank = (psA, pb_A) if fc % 2 == 0 else (psB, pb_B)
            pH = pbank(0, 2048)
            for dc in range(8):
                for ts_ in range(4):
                    mm(ps[:, ts_ * 512:(ts_ + 1) * 512], w1fc[:, sl, dc, :],
                       ln2fm[:, dc, ts_ * 512:(ts_ + 1) * 512], pH, [WF, LN2],
                       start=(dc == 0), stop=(dc == 7), tag="w1mm")
            act(htmp[:], ps[:, 0:2048], AF.Relu, pH + [CB], [HT],
                bias=c_b1h[:, e * 16 + fc:e * 16 + fc + 1], tag="hrelu")
            vop(lambda v, f_=fc: v.tensor_mul(hscl[:, f_, :], htmp[:],
                                              grep[:]), [HT, GR], [HS],
                tag="hscale")
        for dc in range(8):
            sl = dc % 2
            WD = B(f"w2dc{sl}")
            dma("gpsimd", "w2", w2dc[:, sl, :, :],
                dw2S[e, dc].rearrange("(a p) n -> p a n", p=128), [], [WD],
                tag="w2d")
            ps, pbank = (psA, pb_A) if dc % 2 == 0 else (psB, pb_B)
            pO = pbank(0, 2048)
            if e == 0:
                for ts_ in range(4):
                    mm(ps[:, ts_ * 512:(ts_ + 1) * 512],
                       c_b2[:, dc * 128:(dc + 1) * 128],
                       gatet[:, ts_ * 512:(ts_ + 1) * 512], pO, [CB, GT],
                       start=True, stop=False, tag="b2mm")
            for fc in range(16):
                for ts_ in range(4):
                    mm(ps[:, ts_ * 512:(ts_ + 1) * 512], w2dc[:, sl, fc, :],
                       hscl[:, fc, ts_ * 512:(ts_ + 1) * 512], pO, [WD, HS],
                       start=(e != 0 and fc == 0), stop=(fc == 15),
                       tag="w2mm")
            if e == 0:
                act(outac[:, dc, :], ps[:, 0:2048], AF.Copy, pO, [OUT],
                    tag="oev")
            else:
                vop(lambda v, d=dc, p=ps: v.tensor_add(
                    outac[:, d, :], outac[:, d, :], p[:, 0:2048]),
                    pO + [OUT], [OUT], tag="oadd")
    tap("hscl", hscl, HS)
    tap("outac", outac, OUT)
    OUTS, FO = B("outstore"), B("outf")
    for tc in range(16 if RUN_FFN >= 5 else 0):
        sl = tc % 2
        dma("sync", "x1r2", x1t2[:, sl, :], dx1scr[tc * 128:(tc + 1) * 128, :],
            [X1R, FO], [X1R], tag="x1rr")
        ps, pbank = (psA, pb_A) if tc % 2 == 0 else (psB, pb_B)
        pF = pbank(0, 1024)
        for dc in range(8):
            mm(ps[:, dc * 128:(dc + 1) * 128],
               outac[:, dc, tc * 128:(tc + 1) * 128], c_i128b[:], pF,
               [OUT, CB], start=(dc % 4 == 0), stop=(dc % 4 == 3),
               tag="foutT")
        vop(lambda v, s_=sl, p=ps: v.tensor_add(
            outf[:, s_, :], p[:, 0:1024], x1t2[:, s_, :]), pF + [X1R], [FO],
            tag="fadd")
        dma("sync", "outw", dout[tc * 128:(tc + 1) * 128, :], outf[:, sl, :],
            [FO], [OUTS], tag="outw")
    P.op("sync", None, reads=[OUTS], writes=[], tag="fence")
    if dtaps:
        P.op("sync", None, reads=[B("tapout")], writes=[], tag="tapfence")

    P.finalize(nc, gctx)
    with nc.Block() as block:
        block.tensor(lambda t: P.emit_engine("tensor", t))
        block.vector(lambda v: P.emit_engine("vector", v))
        block.scalar(lambda s: P.emit_engine("scalar", s))
        block.gpsimd(lambda g: P.emit_engine("gpsimd", g))
        block.sync(lambda q: P.emit_engine("sync", q))
    fctx.close()
    gctx.close()
    return nc


_tap_shapes = {
    "ufm": ([128, 16, SEG], BF16),
    "lat": ([NS, SEG], F32),
    "dbt": ([NS, SEG], F32),
    "cmt": ([NS, SEG], F32),
    "cumt": ([NS, 2, 128], F32),
    "latm": ([128, 2, NS], F32),
    "dbbtm": ([128, 2, NS], BF16),
    "cbart": ([NS, 2, 128], BF16),
    "daq": ([NS, 2, 8], F32),
    "utm": ([128, 2, DI], BF16),
    "acoef": ([NS, 8], F32),
    "crow": ([128, DI], BF16),
    "zfm": ([128, 16, SEG], BF16),
    "delt": ([NS, SEG], F32),
    "hfp": ([NS, DI], F32),
    "ebuf": ([128, DI], BF16),
    "ybar": ([128, 16, 128], BF16),
    "x1scr": ([L, DM], F32),
    "logt": ([8, L], F32),
    "gatet": ([8, L], BF16),
    "hscl": ([128, 16, L], BF16),
    "outac": ([128, 8, L], BF16),
}


def kernel(**inputs):
    x_full = np.asarray(inputs["L0_attn_x"], np.float32)
    W = host_prep(inputs)
    taps = [(nm, *(_tap_shapes[nm])) for nm in TAPS]
    nc = build_program(taps=taps)
    in_maps = []
    for c in range(8):
        m = dict(x=np.ascontiguousarray(x_full[c]))
        m.update(W)
        in_maps.append(m)
    res = run_bass_kernel_spmd(nc, in_maps, core_ids=list(range(8)))
    out = np.stack([np.asarray(r["out"], np.float32) for r in res.results])
    if TAPS:
        kernel.tap_results = [{nm: np.asarray(r["tap_" + nm]) for nm in TAPS}
                              for r in res.results]
    return out


# version 03
# speedup vs baseline: 1.0359x; 1.0359x over previous
import numpy as np
import ml_dtypes
from contextlib import ExitStack

import concourse.bass as bass
import concourse.mybir as mybir
from concourse.bass_utils import run_bass_kernel_spmd

F32 = mybir.dt.float32
BF16 = mybir.dt.bfloat16
AF = mybir.ActivationFunctionType
ALU = mybir.AluOpType
AXX = mybir.AxisListType.X

L = 2048
DM = 1024
DI = 2048
NS = 16
SEG = 512
NSEG = L // SEG
NCH = SEG // 128
NE = 8
H = 1024
FF = 2048
BIG = 1.0e30
EPS = 1.0e-5

TAPS = []
RUN_NSEG = NSEG
RUN_FFN = 5
RUN_WIN = True
RUN_CHUNKS = NCH
RUN_LN = 3
RUN_LNT = 16

ENG = ("tensor", "vector", "scalar", "gpsimd", "sync")
STREAM_MOD = {"const": 1, "wout": 1, "xin": 4, "outw": 1, "w1": 3,
              "w2": 3, "cfl": 1, "rfl": 1, "gfl": 1, "tap": 1}


class Buf:
    __slots__ = ("name", "writers", "readers")

    def __init__(self, name):
        self.name = name
        self.writers = {}
        self.readers = {}


class Plan:
    def __init__(self):
        self.ops = []
        self.per_eng = {e: [] for e in ENG}
        self.stream_counts = {}

    def op(self, eng, fn, reads=(), writes=(), dma_stream=None, tag="",
           small=False):
        gid = len(self.ops)
        if dma_stream is not None:
            k = self.stream_counts.get(dma_stream, 0)
            self.stream_counts[dma_stream] = k + 1
            chan = ("D", dma_stream, k % STREAM_MOD.get(dma_stream, 2))
        else:
            chan = eng
        deps = set()

        def dep_on(gid2):
            p = self.ops[gid2]
            if p["dma"] is None and p["eng"] == eng:
                return
            deps.add(gid2)

        for b in reads:
            for g2 in b.writers.values():
                dep_on(g2)
        for b in writes:
            for g2 in b.writers.values():
                dep_on(g2)
            for g2 in b.readers.values():
                dep_on(g2)
        o = dict(gid=gid, eng=eng, fn=fn, deps=deps, dma=dma_stream,
                 chan=chan, tag=tag, need_inc=False, sem=None, val=None,
                 small=small)
        self.ops.append(o)
        self.per_eng[eng].append(o)
        for b in reads:
            b.readers[chan] = gid
        for b in writes:
            b.writers[chan] = gid
        return gid

    def barrier(self, B):
        bb = B(f"__bar{len(self.ops)}")
        for e in ENG:
            self.op(e, lambda eng: eng.nop(), reads=[], writes=[bb],
                    tag="bar_w")
        for e in ENG:
            self.op(e, None, reads=[bb], writes=[], tag="bar_r")

    def finalize(self, nc, ctx):
        for o in self.ops:
            if o["eng"] == "tensor" and o["dma"] is None and o["fn"] is not None:
                o["need_inc"] = True
            for d in o["deps"]:
                p = self.ops[d]
                assert p["fn"] is not None or p["dma"] is not None, (
                    "dep on no-op " + p["tag"])
                p["need_inc"] = True
        comp_sems = {e: ctx.enter_context(nc.semaphore(f"sem_{e}"))
                     for e in ENG}
        stream_sems = {}
        counters = {e: 0 for e in ENG}
        for o in self.ops:
            if o["dma"] is not None:
                key = o["chan"]
                if key not in stream_sems:
                    stream_sems[key] = [ctx.enter_context(nc.semaphore(
                        f"dma_{key[1]}_{key[2]}")), 0]
                stream_sems[key][1] += 16
                o["need_inc"] = True
                o["sem"] = stream_sems[key][0]
                o["val"] = stream_sems[key][1]
                self.stream_final = getattr(self, "stream_final", {})
                self.stream_final[key] = stream_sems[key][1]
            elif o["need_inc"]:
                counters[o["eng"]] += 1
                o["sem"] = comp_sems[o["eng"]]
                o["val"] = counters[o["eng"]]

    def emit_engine(self, ename, eng):
        waited = {}
        for o in self.per_eng[ename]:
            pass
        for o in self.per_eng[ename]:
            for d in sorted(o["deps"]):
                p = self.ops[d]
                sem, val = p["sem"], p["val"]
                assert sem is not None, f"{o['tag']} dep {p['tag']} no sem"
                if waited.get(sem.name, -1) < val:
                    eng.wait_ge(sem, val)
                    waited[sem.name] = val
            if o["fn"] is None:
                continue
            inst = o["fn"](eng)
            if o.get("small"):
                inst = eng.drain()
            if o["need_inc"]:
                inst.then_inc(o["sem"], 16 if o["dma"] is not None else 1)
        for o in self.per_eng[ename]:
            if o["dma"] is not None:
                sem, val = o["sem"], self.stream_final[o["chan"]]
                if waited.get(sem.name, -1) < val:
                    eng.wait_ge(sem, val)
                    waited[sem.name] = val


def host_prep(inp):
    f = np.float32
    ac = np.ascontiguousarray

    def g(n):
        return np.asarray(inp[n], f)
    W_inT = ac(g("L0_attn_W_in").T)
    winS = ac(W_inT.reshape(DM, 16, 256).transpose(1, 0, 2))
    woutT = ac(g("L0_attn_W_out").T)
    wxdtT = ac(np.concatenate([g("L0_attn_W_x"), g("L0_attn_W_dt")], 0).T)
    convw = ac(g("L0_attn_conv_w").reshape(DI, 4)
               .reshape(16, 128, 4).transpose(1, 0, 2).reshape(128, 64))
    convb = ac(g("L0_attn_conv_b").reshape(16, 128).T)
    dvec = ac(g("L0_attn_D").reshape(16, 128).T)
    alog = ac(g("L0_attn_A_log").reshape(NS, 1))
    bdt = ac(g("L0_attn_b_dt").reshape(NS, 1))
    w1S = ac(g("L0_ffn_W1").reshape(NE, H, 16, 128).transpose(0, 2, 1, 3))
    w2S = ac(g("L0_ffn_W2").reshape(NE, FF, 8, 128).transpose(0, 2, 1, 3))
    b1h = ac(g("L0_ffn_b1").reshape(NE, 16, 128)
             .transpose(2, 0, 1).reshape(128, NE * 16))
    b2 = ac(g("L0_ffn_b2"))
    wrS = ac(g("L0_ffn_Wr").reshape(8, 128, 8).transpose(1, 0, 2))
    br = ac(g("L0_ffn_br").reshape(NE, 1))
    i128 = np.eye(128, dtype=f)
    i16 = np.eye(16, dtype=f)
    tau = np.arange(128)
    m1 = (tau[:, None] <= tau[None, :]).astype(f)
    m2 = (tau[:, None] > tau[None, :]).astype(f)
    ub = (-BIG * (tau[None, :] > tau[:, None])).astype(f)
    ones128 = np.ones((1, 128), dtype=f)
    maskrep = np.tile((tau[:, None] <= tau[None, :]).astype(f), (1, 16))
    return dict(maskrep=maskrep, winS=winS, woutT=woutT, wxdtT=wxdtT, convw=convw,
                convb=convb, dvec=dvec, alog=alog, bdt=bdt, w1S=w1S,
                w2S=w2S, b1h=b1h, b2=b2, wrS=wrS, br=br, i128=i128,
                i16=i16, m1=m1, m2=m2, ub=ub, ones128=ones128)


def build_program(taps=()):
    nc = bass.Bass("TRN2", target_bir_lowering=False,
                   detect_race_conditions=False)
    P = Plan()
    bufs = {}

    def B(name):
        if name not in bufs:
            bufs[name] = Buf(name)
        return bufs[name]

    dx = nc.dram_tensor("x", [L, DM], F32, kind="ExternalInput").ap()
    dwinS = nc.dram_tensor("winS", [16, DM, 256], F32, kind="ExternalInput").ap()
    dwoutT = nc.dram_tensor("woutT", [DI, DM], F32, kind="ExternalInput").ap()
    dwxdtT = nc.dram_tensor("wxdtT", [DI, 48], F32, kind="ExternalInput").ap()
    dconvw = nc.dram_tensor("convw", [128, 64], F32, kind="ExternalInput").ap()
    dconvb = nc.dram_tensor("convb", [128, 16], F32, kind="ExternalInput").ap()
    ddvec = nc.dram_tensor("dvec", [128, 16], F32, kind="ExternalInput").ap()
    dalog = nc.dram_tensor("alog", [NS, 1], F32, kind="ExternalInput").ap()
    dbdt = nc.dram_tensor("bdt", [NS, 1], F32, kind="ExternalInput").ap()
    dw1S = nc.dram_tensor("w1S", [NE, 16, H, 128], F32, kind="ExternalInput").ap()
    dw2S = nc.dram_tensor("w2S", [NE, 8, FF, 128], F32, kind="ExternalInput").ap()
    db1h = nc.dram_tensor("b1h", [128, NE * 16], F32, kind="ExternalInput").ap()
    db2 = nc.dram_tensor("b2", [NE, H], F32, kind="ExternalInput").ap()
    dwrS = nc.dram_tensor("wrS", [128, 8, 8], F32, kind="ExternalInput").ap()
    dbr = nc.dram_tensor("br", [NE, 1], F32, kind="ExternalInput").ap()
    di128 = nc.dram_tensor("i128", [128, 128], F32, kind="ExternalInput").ap()
    di16 = nc.dram_tensor("i16", [16, 16], F32, kind="ExternalInput").ap()
    dm1 = nc.dram_tensor("m1", [128, 128], F32, kind="ExternalInput").ap()
    dm2 = nc.dram_tensor("m2", [128, 128], F32, kind="ExternalInput").ap()
    dub = nc.dram_tensor("ub", [128, 128], F32, kind="ExternalInput").ap()
    dones = nc.dram_tensor("ones128", [1, 128], F32, kind="ExternalInput").ap()
    dmaskrep = nc.dram_tensor("maskrep", [128, 2048], F32, kind="ExternalInput").ap()
    dout = nc.dram_tensor("out", [L, DM], F32, kind="ExternalOutput").ap()
    dx1scr = nc.dram_tensor("x1scr", [L, DM], F32, kind="Internal").ap()
    dtaps = {}
    for (nm, shape, dt) in taps:
        dtaps[nm] = nc.dram_tensor("tap_" + nm, shape, dt,
                                   kind="ExternalOutput").ap()

    def tap(nm, tile, rbuf):
        if nm in dtaps:
            ap_ = tile if isinstance(tile, bass.AP) else tile[:]
            dma("sync", "tap", dtaps[nm], ap_, [rbuf], [B("tapout")],
                tag=f"tap_{nm}")

    gctx = ExitStack()

    def SB(c, n, s, d):
        return c.enter_context(nc.sbuf_tensor(n, s, d))

    def mm(out_ap, lhsT_ap, rhs_ap, pbufs, rbufs, start, stop, tag=""):
        P.op("tensor",
             lambda t, o=out_ap, l=lhsT_ap, r=rhs_ap, s0=start, s1=stop:
                 t.matmul(o, l, r, start=s0, stop=s1),
             reads=rbufs, writes=pbufs, tag=tag)

    def act(out_ap, in_ap, func, rbufs, wbufs, bias=0.0, scale=1.0, tag="",
            small=False):
        P.op("scalar",
             lambda s, o=out_ap, i=in_ap, f=func, b=bias, sc=scale:
                 s.activation(o, i, f, bias=b, scale=sc),
             reads=rbufs, writes=wbufs, tag=tag, small=small)

    def vop(fn, rbufs, wbufs, tag="", small=False):
        P.op("vector", fn, reads=rbufs, writes=wbufs, tag=tag, small=small)

    def dma(eng, stream, out_ap, in_ap, rbufs, wbufs, tag=""):
        P.op(eng, lambda e, o=out_ap, i=in_ap: e.dma_start(o, i),
             reads=rbufs, writes=wbufs, dma_stream=stream, tag=tag)

    def pb_A(lo, hi):
        return [B(f"psA_b{i}") for i in range(lo // 512, (hi - 1) // 512 + 1)]

    def pb_B(lo, hi):
        return [B(f"psB_b{i}") for i in range(lo // 512, (hi - 1) // 512 + 1)]

    c_i128b = SB(gctx, "c_i128b", [128, 128], BF16)
    c_i128f = SB(gctx, "c_i128f", [128, 128], F32)
    c_i16f = SB(gctx, "c_i16f", [16, 16], F32)
    c_m1 = SB(gctx, "c_m1", [128, 128], F32)
    c_m2 = SB(gctx, "c_m2", [128, 128], F32)
    c_ub = SB(gctx, "c_ub", [128, 128], F32)
    c_onesf = SB(gctx, "c_onesf", [1, 128], F32)
    c_onesb = SB(gctx, "c_onesb", [1, 128], BF16)
    c_mrep = SB(gctx, "c_mrep", [128, 2048], BF16)
    c_convw = SB(gctx, "c_convw", [128, 64], F32)
    c_convb = SB(gctx, "c_convb", [128, 16], F32)
    c_dvec = SB(gctx, "c_dvec", [128, 16], F32)
    c_alog = SB(gctx, "c_alog", [NS, 1], F32)
    c_acoef = SB(gctx, "c_acoef", [NS, 8], F32)
    c_bdt = SB(gctx, "c_bdt", [NS, 1], F32)
    c_wxdt = SB(gctx, "c_wxdt", [128, 16, 48], BF16)
    c_wrS = SB(gctx, "c_wrS", [128, 8, 8], BF16)
    c_br = SB(gctx, "c_br", [8, 1], F32)
    c_b1h = SB(gctx, "c_b1h", [128, NE * 16], F32)
    c_b2 = SB(gctx, "c_b2", [NE, H], BF16)
    lnst = SB(gctx, "lnst", [128, 12], F32)
    lnmv = SB(gctx, "lnmv", [128, 2], F32)
    lnrs = SB(gctx, "lnrs", [128, 8], F32)
    psA = gctx.enter_context(nc.psum_tensor("psA", [128, 2048], F32))
    psB = gctx.enter_context(nc.psum_tensor("psB", [128, 2048], F32))

    CB = B("consts")
    for dst, src in [
        (c_i128b[:], di128), (c_i128f[:], di128), (c_i16f[:], di16),
        (c_m1[:], dm1), (c_m2[:], dm2), (c_ub[:], dub),
        (c_onesf[:], dones), (c_onesb[:], dones), (c_mrep[:], dmaskrep),
        (c_convw[:], dconvw), (c_convb[:], dconvb), (c_dvec[:], ddvec),
        (c_alog[:], dalog), (c_bdt[:], dbdt),
        (c_wxdt[:], dwxdtT.rearrange("(a p) k -> p a k", p=128)),
        (c_wrS[:], dwrS), (c_br[:], dbr), (c_b1h[:], db1h), (c_b2[:], db2),
    ]:
        dma("gpsimd", "const", dst, src, [], [CB], tag="const")
    ACF = B("acoef")
    act(lnrs[0:1, 0:8], lnrs[0:1, 0:8], AF.Exp, [], [B("lnrs")], tag="warm_e", small=True)
    act(lnrs[0:1, 0:8], lnrs[0:1, 0:8], AF.Ln, [B("lnrs")], [B("lnrs")],
        tag="warm_l", small=True)
    act(lnrs[0:1, 0:8], lnrs[0:1, 0:8], AF.Silu, [B("lnrs")], [B("lnrs")],
        tag="warm_s", small=True)
    act(lnrs[0:1, 0:8], lnrs[0:1, 0:8], AF.Ln, [B("lnrs")], [B("lnrs")],
        tag="warm_l2", small=True)
    act(c_acoef[:], c_alog[:].broadcast_to([NS, 8]), AF.Exp, [CB], [ACF],
        tag="acoef_exp", small=True)
    vop(lambda v: v.tensor_scalar_mul(c_acoef[:], c_acoef[:], -1.0),
        [ACF], [ACF], tag="acoef_neg", small=True)

    mctx = ExitStack()
    x1t = SB(mctx, "x1t", [128, 2, DM], F32)
    woutT = SB(mctx, "sb_woutT", [128, 16, DM], BF16)
    winSl = SB(mctx, "winSl", [128, 2, 8, 256], BF16)
    x_seg = SB(mctx, "x_seg", [128, NCH, DM], F32)
    ln1tm = SB(mctx, "ln1tm", [128, 2, DM], BF16)
    ln1fm = SB(mctx, "ln1fm", [128, 8, SEG], BF16)
    xi = SB(mctx, "xi", [128, 16, SEG + 3], BF16)
    xi_sv = SB(mctx, "xi_sv", [128, 16, 3], BF16)
    zfm = SB(mctx, "zfm", [128, 16, SEG], BF16)
    ufm = SB(mctx, "ufm", [128, 16, SEG], BF16)
    t1c = SB(mctx, "t1c", [128, SEG], F32)
    utm = SB(mctx, "utm", [128, 2, DI], BF16)
    crow = SB(mctx, "crow", [128, DI], BF16)
    ebuf = SB(mctx, "ebuf", [128, DI], BF16)
    t1e = SB(mctx, "t1e", [128, DI], BF16)
    ybar = SB(mctx, "ybar", [128, 16, 128], BF16)
    bmt = SB(mctx, "bmt", [NS, SEG], F32)
    cmt = SB(mctx, "cmt", [NS, SEG], F32)
    cmtb = SB(mctx, "cmtb", [NS, SEG], BF16)
    delt = SB(mctx, "delt", [NS, SEG], F32)
    lat = SB(mctx, "lat", [NS, SEG], F32)
    dbt = SB(mctx, "dbt", [NS, SEG], F32)
    latm = SB(mctx, "latm", [128, 2, NS], F32)
    dbtm = SB(mctx, "dbtm", [128, 2, NS], F32)
    cumtm = SB(mctx, "cumtm", [128, 2, NS], F32)
    negtm = SB(mctx, "negtm", [128, 2, NS], F32)
    revtm = SB(mctx, "revtm", [128, 2, NS], F32)
    erev = SB(mctx, "erev", [128, 2, NS], F32)
    dbbtm = SB(mctx, "dbbtm", [128, 2, NS], BF16)
    cumt = SB(mctx, "cumt", [NS, 2, 128], F32)
    ect = SB(mctx, "ect", [NS, 2, 128], F32)
    cbart = SB(mctx, "cbart", [NS, 2, 128], BF16)
    daq = SB(mctx, "daq", [NS, 2, 8], F32)
    cumfl = SB(mctx, "cumfl", [1, DI], F32)
    rowfl = SB(mctx, "rowfl", [1, DI], BF16)
    hfp = SB(mctx, "hfp", [NS, DI], F32)
    hbf = SB(mctx, "hbf", [NS, DI], BF16)

    WOUT = B("woutT")
    for ec in range(16):
        dma("gpsimd", "wout", woutT[:, ec, :],
            dwoutT[ec * 128:(ec + 1) * 128, :], [], [WOUT], tag="woutd")
    HB, HBB = B("hfp"), B("hbf")
    vop(lambda v: v.memset(hfp[:], 0.0), [], [HB], tag="h0")
    vop(lambda v: v.memset(hbf[:], 0.0), [], [HBB], tag="hbf0")
    XIB = B("xi")
    vop(lambda v: v.memset(xi[:], 0.0), [], [XIB], tag="xipad0")

    XSEG, LT, LF = B("x_seg"), B("ln1tm"), B("ln1fm")
    ZB, UB_, UD = B("zfm"), B("ufm"), B("xi")
    LS, MV, RS = B("lnst"), B("lnmv"), B("lnrs")

    for s in range(RUN_NSEG):
        sbase = s * SEG
        for ch in range(NCH):
            dma("sync", "xin", x_seg[:, ch, :],
                dx[sbase + ch * 128: sbase + (ch + 1) * 128, :],
                [], [XSEG], tag=f"xin{s}_{ch}")
        for ch in range(NCH if RUN_LN >= 1 else 0):
            vop(lambda v, c=ch: v.bn_stats(lnst[:, 0:6], x_seg[:, c, 0:512]),
                [XSEG], [LS], tag="bn1", small=True)
            vop(lambda v, c=ch: v.bn_stats(lnst[:, 6:12],
                                           x_seg[:, c, 512:1024]),
                [XSEG], [LS], tag="bn2", small=True)
            vop(lambda v: v.bn_aggr(lnmv[:], lnst[:]), [LS], [MV], tag="agg", small=True)
            vop(lambda v: v.tensor_scalar_add(
                lnrs[:], lnmv[:, 1:2].broadcast_to([128, 8]), EPS),
                [MV], [RS], tag="veps", small=True)
            act(lnrs[:], lnrs[:], AF.Ln, [RS], [RS], tag="lnv", small=True)
            vop(lambda v: v.tensor_scalar_mul(lnrs[:], lnrs[:], -0.5),
                [RS], [RS], tag="lnvm", small=True)
            act(lnrs[:], lnrs[:], AF.Exp, [RS], [RS], tag="rstd", small=True)
            if RUN_LN < 2:
                continue
            vop(lambda v, c=ch: v.tensor_scalar(
                ln1tm[:, c % 2, :], x_seg[:, c, :], lnmv[:, 0:1], None,
                ALU.subtract), [XSEG, MV], [LT], tag="ln1sub")
            vop(lambda v, c=ch: v.tensor_scalar_mul(
                ln1tm[:, c % 2, :], ln1tm[:, c % 2, :], lnrs[:, 0:1]),
                [LT, RS], [LT], tag="ln1mul")
            for w in range(2 if RUN_LN >= 3 else 0):
                c0 = w * 512
                pb = pb_A(c0, c0 + 512)
                for j in range(4):
                    dc = w * 4 + j
                    mm(psA[:, c0 + j * 128:c0 + (j + 1) * 128],
                       ln1tm[:, ch % 2, dc * 128:(dc + 1) * 128], c_i128b[:],
                       pb, [LT, CB], start=(j == 0), stop=(j == 3),
                       tag="ln1T")
                vop(lambda v, c=ch, ww=w, cc0=c0: v.tensor_copy(
                    ln1fm[:, ww * 4:(ww + 1) * 4, c * 128:(c + 1) * 128],
                    psA[:, cc0:cc0 + 512].rearrange("p (a n) -> p a n", a=4)),
                    pb, [LF], tag="ln1ev")
        for g in range(16 if RUN_WIN else 0):
            sl = g % 2
            WS = B(f"winSl{sl}")
            dma("gpsimd", "winS", winSl[:, sl, :, :],
                dwinS[g].rearrange("(a p) n -> p a n", p=128),
                [], [WS], tag=f"winS{s}_{g}")
            b0 = sl * 2
            pb0 = pb_A(b0 * 512, b0 * 512 + 512)
            pb1 = pb_A((b0 + 1) * 512, (b0 + 1) * 512 + 512)
            for dc in range(8):
                mm(psA[:, b0 * 512:(b0 + 1) * 512], winSl[:, sl, dc, 0:128],
                   ln1fm[:, dc, :], pb0, [WS, LF],
                   start=(dc == 0), stop=(dc == 7), tag="xzA")
                mm(psA[:, (b0 + 1) * 512:(b0 + 2) * 512],
                   winSl[:, sl, dc, 128:256], ln1fm[:, dc, :], pb1, [WS, LF],
                   start=(dc == 0), stop=(dc == 7), tag="xzB")
            if s > 0 and g == 0:
                vop(lambda v: v.tensor_copy(xi[:, :, 0:3], xi_sv[:]),
                    [B("xi_sv")], [XIB], tag="xipad", small=True)
            if g < 8:
                act(xi[:, 2 * g, 3:SEG + 3], psA[:, b0 * 512:(b0 + 1) * 512],
                    AF.Copy, pb0, [XIB], tag="evxi")
                act(xi[:, 2 * g + 1, 3:SEG + 3],
                    psA[:, (b0 + 1) * 512:(b0 + 2) * 512], AF.Copy, pb1,
                    [XIB], tag="evxi")
            else:
                act(zfm[:, 2 * (g - 8), :], psA[:, b0 * 512:(b0 + 1) * 512],
                    AF.Copy, pb0, [ZB], tag="evz")
                act(zfm[:, 2 * (g - 8) + 1, :],
                    psA[:, (b0 + 1) * 512:(b0 + 2) * 512], AF.Copy, pb1,
                    [ZB], tag="evz")
        T1 = B("t1c")
        for ic in range(16 if RUN_WIN else 0):
            vop(lambda v, i=ic: v.tensor_scalar_mul(
                t1c[:], xi[:, i, 0:SEG], c_convw[:, 4 * i:4 * i + 1]),
                [XIB, CB], [T1], tag="cv0")
            for k in range(1, 4):
                vop(lambda v, i=ic, kk=k: v.scalar_tensor_tensor(
                    t1c[:], xi[:, i, kk:SEG + kk],
                    c_convw[:, 4 * i + kk:4 * i + kk + 1], t1c[:],
                    ALU.mult, ALU.add), [XIB, CB, T1], [T1], tag="cvk")
            act(ufm[:, ic, :], t1c[:], AF.Silu, [T1, CB], [UB_],
                bias=c_convb[:, ic:ic + 1], tag="silu_u")
        if RUN_WIN:
            vop(lambda v: v.tensor_copy(xi_sv[:], xi[:, :, SEG:SEG + 3]),
                [XIB], [B("xi_sv")], tag="xisave", small=True)

        for ic in range(16 if RUN_WIN else 0):
            act(zfm[:, ic, :], zfm[:, ic, :], AF.Silu, [ZB], [ZB],
                tag="silu_z")
        for ic in range(16 if RUN_WIN else 0):
            vop(lambda v, i=ic: v.tensor_scalar_mul(
                xi[:, i, 3:SEG + 3], ufm[:, i, :], c_dvec[:, i:i + 1]),
                [UB_, CB], [UD], tag="uD")
        pWx = pb_A(0, 1536)
        for ec in range(16 if RUN_WIN else 0):
            for j in range(3):
                mm(psA[0:16, j * 512:(j + 1) * 512],
                   c_wxdt[:, ec, j * 16:(j + 1) * 16], ufm[:, ec, :], pWx,
                   [CB, UB_], start=(ec == 0), stop=(ec == 15), tag="wxdt")
        BM, CM, CMB = B("bmt"), B("cmt"), B("cmtb")
        DL, LA, DB = B("delt"), B("lat"), B("dbt")
        if not RUN_WIN:
            continue
        vop(lambda v: v.tensor_copy(bmt[:], psA[0:16, 0:512]), pWx, [BM],
            tag="evbm")
        vop(lambda v: v.tensor_copy(cmt[:], psA[0:16, 512:1024]), pWx, [CM],
            tag="evcm")
        vop(lambda v: v.tensor_copy(cmtb[:], cmt[:]), [CM], [CMB], tag="cmtb")
        act(delt[:], psA[0:16, 1024:1536], AF.Exp, pWx + [CB], [DL],
            bias=c_bdt[:], tag="d_exp")
        vop(lambda v: v.tensor_scalar_add(delt[:], delt[:], 1.0), [DL], [DL],
            tag="d_p1")
        act(delt[:], delt[:], AF.Ln, [DL], [DL], tag="d_ln")
        vop(lambda v: v.tensor_scalar_mul(lat[:], delt[:], c_acoef[:, 0:1]),
            [DL, ACF], [LA], tag="la")
        vop(lambda v: v.tensor_mul(dbt[:], delt[:], bmt[:]), [DL, BM], [DB],
            tag="db")

        for cc in range(min(NCH, RUN_CHUNKS)):
            ci = cc % 2
            ut = cc % 2
            gch = s * NCH + cc
            pb3 = pb_B(1536, 2048)
            mm(psB[:, 1536:1552], lat[:, cc * 128:(cc + 1) * 128], c_i16f[:],
               pb3, [LA, CB], start=True, stop=True, tag="latmM")
            mm(psB[:, 1552:1568], dbt[:, cc * 128:(cc + 1) * 128], c_i16f[:],
               pb3, [DB, CB], start=True, stop=True, tag="dbtmM")
            LTM, DTM = B("latm"), B("dbtm")
            vop(lambda v, i=ci: v.tensor_copy(latm[:, i, :], psB[:, 1536:1552]),
                pb3, [LTM], tag="latmev", small=True)
            vop(lambda v, i=ci: v.tensor_copy(dbtm[:, i, :], psB[:, 1552:1568]),
                pb3, [DTM], tag="dbtmev", small=True)
            mm(psB[:, 1568:1584], c_m1[:], latm[:, ci, :], pb3, [CB, LTM],
               start=True, stop=True, tag="cumM")
            mm(psB[:, 1584:1600], c_m2[:], latm[:, ci, :], pb3, [CB, LTM],
               start=True, stop=True, tag="revM")
            mm(psB[0:16, 1600:1728], latm[:, ci, :], c_m1[:], pb3, [LTM, CB],
               start=True, stop=True, tag="cumTM")
            CUM, REV, CT = B("cumtm"), B("revtm"), B("cumt")
            vop(lambda v, i=ci: v.tensor_copy(cumtm[:, i, :], psB[:, 1568:1584]),
                pb3, [CUM], tag="cumev", small=True)
            NEG = B("negtm")
            vop(lambda v, i=ci: v.tensor_scalar_mul(
                negtm[:, i, :], cumtm[:, i, :], -1.0), [CUM], [NEG],
                tag="negtm", small=True)
            vop(lambda v, i=ci: v.tensor_copy(revtm[:, i, :], psB[:, 1584:1600]),
                pb3, [REV], tag="revev", small=True)
            vop(lambda v, i=ci: v.tensor_copy(cumt[:, i, :],
                                              psB[0:16, 1600:1728]),
                pb3, [CT], tag="cumtev")
            ER, ECT, DAQ = B("erev"), B("ect"), B("daq")
            act(erev[:, ci, :], revtm[:, ci, :], AF.Exp, [REV], [ER], tag="er")
            act(ect[:, ci, :], cumt[:, ci, :], AF.Exp, [CT], [ECT], tag="ec")
            act(daq[:, ci, :], cumt[:, ci, 127:128].broadcast_to([NS, 8]),
                AF.Exp, [CT], [DAQ], tag="daq", small=True)
            DBB, CBT = B("dbbtm"), B("cbart")
            vop(lambda v, i=ci: v.tensor_mul(
                dbbtm[:, i, :], dbtm[:, i, :], erev[:, i, :]), [DTM, ER],
                [DBB], tag="dbb", small=True)
            vop(lambda v, i=ci, c=cc: v.tensor_mul(
                cbart[:, i, :], cmt[:, c * 128:(c + 1) * 128], ect[:, i, :]),
                [CM, ECT], [CBT], tag="cbart")
            CFL, RFL = B("cumfl"), B("rowfl")
            dma("gpsimd", "cfl", cumfl[:], cumt[:, ci, :], [CT], [CFL],
                tag="cfl")
            dma("gpsimd", "rfl", rowfl[:], cmtb[:, cc * 128:(cc + 1) * 128],
                [CMB], [RFL], tag="rfl")
            UT = B(f"utm{ut}")
            for w in range(4):
                c0 = 1024 + (w % 2) * 512
                pbu = pb_B(c0, c0 + 512)
                for j in range(4):
                    ic = w * 4 + j
                    mm(psB[:, c0 + j * 128:c0 + (j + 1) * 128],
                       ufm[:, ic, cc * 128:(cc + 1) * 128], c_i128b[:], pbu,
                       [UB_, CB], start=(j == 0), stop=(j == 3), tag="uT")
                vop(lambda v, u=ut, ww=w, cc0=c0: v.tensor_copy(
                    utm[:, u, ww * 512:(ww + 1) * 512], psB[:, cc0:cc0 + 512]),
                    pbu, [UT], tag="uTev")
            pD = pb_A(0, 2048)
            for j in range(4):
                mm(psA[:, j * 512:(j + 1) * 512], c_onesf[:],
                   cumfl[:, j * 512:(j + 1) * 512], pD, [CB, CFL],
                   start=True, stop=True, tag="diff")
            EB = B("ebuf")
            for n in range(NS):
                act(ebuf[:, n * 128:(n + 1) * 128],
                    psA[:, n * 128:(n + 1) * 128], AF.Relu, pD + [CUM], [EB],
                    bias=cumtm[:, ci, n:n + 1], scale=-1.0, tag="erelu")
            for n in range(NS):
                act(ebuf[:, n * 128:(n + 1) * 128],
                    ebuf[:, n * 128:(n + 1) * 128], AF.Exp, [EB], [EB],
                    scale=-1.0, tag="eexp")
            vop(lambda v: v.tensor_mul(ebuf[:], ebuf[:], c_mrep[:]),
                [EB, CB], [EB], tag="emask")
            for n in range(NS):
                vop(lambda v, nn=n, i=ci: v.tensor_scalar_mul(
                    ebuf[:, nn * 128:(nn + 1) * 128],
                    ebuf[:, nn * 128:(nn + 1) * 128],
                    dbtm[:, i, nn:nn + 1]), [EB, DTM], [EB], tag="edb")
            pC = pD
            for j in range(4):
                mm(psA[:, j * 512:(j + 1) * 512], c_onesb[:],
                   rowfl[:, j * 512:(j + 1) * 512], pC, [CB, RFL],
                   start=True, stop=True, tag="crowM")
            CR = B("crow")
            act(crow[:], psA[:, 0:2048], AF.Copy, pC, [CR], tag="crowev")
            vop(lambda v: v.tensor_mul(ebuf[:], ebuf[:], crow[:]), [EB, CR],
                [EB], tag="wmul")
            for lo, hi in ((1024, 2048), (512, 1024), (256, 512), (128, 256)):
                vop(lambda v, l=lo, h=hi: v.tensor_add(
                    ebuf[:, 0:l], ebuf[:, 0:l], ebuf[:, l:h]), [EB], [EB],
                    tag="fold")
            pY = pb_B(0, 2048)
            for ec in range(16):
                mm(psB[:, ec * 128:(ec + 1) * 128],
                   utm[:, ut, ec * 128:(ec + 1) * 128], ebuf[:, 0:128], pY,
                   [UT, EB], start=(ec % 4 == 0), stop=False, tag="intra")
            for ec in range(16):
                mm(psB[:, ec * 128:(ec + 1) * 128],
                   hbf[:, ec * 128:(ec + 1) * 128], cbart[:, ci, :], pY,
                   [HBB, CBT], start=False, stop=(ec % 4 == 3), tag="inter")
            T1E, YB = B("t1e"), B("ybar")
            vop(lambda v, c=cc: v.tensor_add(
                t1e[:].rearrange("p (a n) -> p a n", a=16),
                psB[:, 0:2048].rearrange("p (a n) -> p a n", a=16),
                xi[:, :, 3 + c * 128:3 + (c + 1) * 128]), pY + [UD], [T1E],
                tag="epi1")
            vop(lambda v, c=cc: v.tensor_mul(
                ybar[:], t1e[:].rearrange("p (a n) -> p a n", a=16),
                zfm[:, :, c * 128:(c + 1) * 128]), [T1E, ZB], [YB], tag="epi2")
            pS = pD
            for j in range(4):
                mm(psA[0:16, j * 512:(j + 1) * 512], dbbtm[:, ci, :],
                   utm[:, ut, j * 512:(j + 1) * 512], pS, [DBB, UT],
                   start=True, stop=True, tag="state")

            vop(lambda v, i=ci: v.scalar_tensor_tensor(
                hfp[:], hfp[:], daq[:, i, 0:1], psA[0:16, 0:2048],
                ALU.mult, ALU.add), [HB, DAQ] + pS, [HB], tag="hupd")

            vop(lambda v: v.tensor_copy(hbf[:], hfp[:]), [HB], [HBB],
                tag="hcast")
            pW = pb_B(0, 1024)
            for ec in range(16):
                for dh in range(2):
                    mm(psB[:, dh * 512:(dh + 1) * 512], ybar[:, ec, :],
                       woutT[:, ec, dh * 512:(dh + 1) * 512], pW, [YB, WOUT],
                       start=(ec == 0), stop=(ec == 15), tag="wout")
            X1C = B("x1t")
            sl = cc % 2
            vop(lambda v, c=cc, s_=sl: v.tensor_add(
                x1t[:, s_, :], psB[:, 0:1024], x_seg[:, c, :]),
                pW + [XSEG], [X1C], tag="resid")
            dma("sync", "x1w", dx1scr[gch * 128:(gch + 1) * 128, :],
                x1t[:, sl, :], [X1C], [B("x1scr")], tag="x1w")

    tap("ufm", ufm, UB_)
    tap("zfm", zfm, ZB)
    tap("lat", lat, B("lat"))
    tap("dbt", dbt, B("dbt"))
    tap("cmt", cmt, B("cmt"))
    tap("cumt", cumt, B("cumt"))
    tap("latm", latm, B("latm"))
    tap("dbbtm", dbbtm, B("dbbtm"))
    tap("cbart", cbart, B("cbart"))
    tap("daq", daq, B("daq"))
    tap("utm", utm, B("utm0"))
    tap("acoef", c_acoef, B("acoef"))
    tap("crow", crow, B("crow"))
    tap("delt", delt, B("delt"))
    tap("hfp", hfp, HB)
    tap("ebuf", ebuf, B("ebuf"))
    tap("ybar", ybar, B("ybar"))
    tap("x1scr", dx1scr, B("x1scr"))
    P.barrier(B)
    mctx.close()

    fctx = ExitStack()
    x1t2 = SB(fctx, "x1t2", [128, 2, DM], F32)
    outf = SB(fctx, "outf", [128, 2, DM], F32)
    ln2fm = SB(fctx, "ln2fm", [128, 8, L], BF16) if RUN_FFN >= 1 else None
    ln2tm = SB(fctx, "ln2tm", [128, 2, DM], BF16)
    logt = SB(fctx, "logt", [8, L], F32)
    lgc = SB(fctx, "lgc", [128, 8], F32)
    g_m1 = SB(fctx, "g_m1", [128, 1], F32)
    g_m2 = SB(fctx, "g_m2", [128, 1], F32)
    g_e1 = SB(fctx, "g_e1", [128, 8], F32)
    g_e2 = SB(fctx, "g_e2", [128, 8], F32)
    g_msk = SB(fctx, "g_msk", [128, 8], F32)
    g_w = SB(fctx, "g_w", [128, 32], F32)
    g_t = SB(fctx, "g_t", [128, 8], F32)
    gatec = SB(fctx, "gatec", [128, 8], BF16)
    gatet = SB(fctx, "gatet", [8, L], BF16)
    gfl = SB(fctx, "gfl", [1, L], BF16)
    if RUN_FFN >= 4:
        grep = SB(fctx, "grep", [128, L], BF16)
        w1fc = SB(fctx, "w1fc", [128, 3, 8, 128], BF16)
        w2dc = SB(fctx, "w2dc", [128, 3, 16, 128], BF16)
        hscl = SB(fctx, "hscl", [128, 16, L], BF16)
        htmp = SB(fctx, "htmp", [128, L], BF16)
        outac = SB(fctx, "outac", [128, 8, L], BF16)
    else:
        grep = w1fc = w2dc = hscl = htmp = outac = None

    if RUN_FFN >= 4:
        P.op("gpsimd", None, reads=[B("x1scr")], writes=[], tag="gfence")
    X1R, LT2, LN2 = B("x1r"), B("ln2tm"), B("ln2fm")
    for tc in range(16 if RUN_FFN >= 1 else 0):
        sl = tc % 2
        dma("sync", "x1r", x1t2[:, sl, :], dx1scr[tc * 128:(tc + 1) * 128, :],
            [B("x1scr")], [X1R], tag="x1r")
        if RUN_LN < 1:
            continue
        vop(lambda v, s_=sl: v.bn_stats(lnst[:, 0:6], x1t2[:, s_, 0:512]),
            [X1R], [LS], tag="bn1f", small=True)
        vop(lambda v, s_=sl: v.bn_stats(lnst[:, 6:12], x1t2[:, s_, 512:1024]),
            [X1R], [LS], tag="bn2f", small=True)
        vop(lambda v: v.bn_aggr(lnmv[:], lnst[:]), [LS], [MV], tag="aggf", small=True)
        vop(lambda v: v.tensor_scalar_add(
            lnrs[:], lnmv[:, 1:2].broadcast_to([128, 8]), EPS),
            [MV], [RS], tag="vepsf", small=True)
        act(lnrs[:], lnrs[:], AF.Ln, [RS], [RS], tag="lnvf", small=True)
        vop(lambda v: v.tensor_scalar_mul(lnrs[:], lnrs[:], -0.5),
            [RS], [RS], tag="lnvmf", small=True)
        act(lnrs[:], lnrs[:], AF.Exp, [RS], [RS], tag="rstdf", small=True)
        if RUN_LN < 2:
            continue
        vop(lambda v, s_=sl: v.tensor_scalar(
            ln2tm[:, s_, :], x1t2[:, s_, :], lnmv[:, 0:1], None,
            ALU.subtract), [X1R, MV], [LT2], tag="ln2sub")
        vop(lambda v, s_=sl: v.tensor_scalar_mul(
            ln2tm[:, s_, :], ln2tm[:, s_, :], lnrs[:, 0:1]),
            [LT2, RS], [LT2], tag="ln2mul")
        for w in range(2 if (RUN_LN >= 3 and tc < RUN_LNT) else 0):
            c0 = w * 512
            pb = pb_A(c0, c0 + 512)
            for j in range(4):
                dc = w * 4 + j
                mm(psA[:, c0 + j * 128:c0 + (j + 1) * 128],
                   ln2tm[:, sl, dc * 128:(dc + 1) * 128], c_i128b[:], pb,
                   [LT2, CB], start=(j == 0), stop=(j == 3), tag="ln2T")
            vop(lambda v, t_=tc, ww=w, cc0=c0: v.tensor_copy(
                ln2fm[:, ww * 4:(ww + 1) * 4, t_ * 128:(t_ + 1) * 128],
                psA[:, cc0:cc0 + 512].rearrange("p (a n) -> p a n", a=4)),
                pb, [LN2], tag="ln2ev")
    pR = pb_A(0, 2048)
    for dc in range(8 if RUN_FFN >= 2 else 0):
        for ts_ in range(4):
            mm(psA[0:8, ts_ * 512:(ts_ + 1) * 512], c_wrS[:, dc, :],
               ln2fm[:, dc, ts_ * 512:(ts_ + 1) * 512], pR, [CB, LN2],
               start=(dc == 0), stop=(dc == 7), tag="rout")
    LOG = B("logt")
    if RUN_FFN >= 2:
        act(logt[:], psA[0:8, 0:2048], AF.Copy, pR, [LOG], tag="routev")
    if RUN_FFN >= 2:
        vop(lambda v: v.tensor_scalar_add(logt[:], logt[:], c_br[:]),
            [LOG, CB], [LOG], tag="routbr")
    GT = B("gatet")
    for tc in range(16 if RUN_FFN >= 3 else 0):
        pb3 = pb_B(1536, 2048)
        mm(psB[0:128, 1536:1544], logt[:, tc * 128:(tc + 1) * 128],
           c_i16f[0:8, 0:8], pb3, [LOG, CB], start=True, stop=True,
           tag="gT")
        LGC = B("lgc")
        vop(lambda v: v.tensor_copy(lgc[:], psB[:, 1536:1544]), pb3, [LGC],
            tag="lgcev", small=True)
        M1_, M2_ = B("g_m1"), B("g_m2")
        E1, E2, MSK = B("g_e1"), B("g_e2"), B("g_msk")
        W_, T_, GC = B("g_w"), B("g_t"), B("gatec")
        vop(lambda v: v.reduce_max(g_m1[:], lgc[:], axis=AXX), [LGC], [M1_],
            tag="gm1", small=True)
        vop(lambda v: v.tensor_scalar(g_e1[:], lgc[:], g_m1[:], None,
                                      ALU.is_equal), [LGC, M1_], [E1],
            tag="ge1", small=True)
        vop(lambda v: v.scalar_tensor_tensor(g_msk[:], g_e1[:], -BIG, lgc[:],
                                             ALU.mult, ALU.add), [E1, LGC],
            [MSK], tag="gmsk", small=True)
        vop(lambda v: v.reduce_max(g_m2[:], g_msk[:], axis=AXX), [MSK],
            [M2_], tag="gm2", small=True)
        vop(lambda v: v.tensor_scalar(g_e2[:], lgc[:], g_m2[:], None,
                                      ALU.is_equal), [LGC, M2_], [E2],
            tag="ge2", small=True)
        vop(lambda v: v.tensor_tensor(
            g_w[:, 0:8], g_m2[:].broadcast_to([128, 8]),
            g_m1[:].broadcast_to([128, 8]), ALU.subtract), [M1_, M2_], [W_],
            tag="gd", small=True)
        act(g_w[:, 8:16], g_w[:, 0:8], AF.Exp, [W_], [W_], scale=-1.0,
            tag="gexp", small=True)
        vop(lambda v: v.tensor_scalar_add(g_w[:, 8:16], g_w[:, 8:16], 1.0),
            [W_], [W_], tag="gp1", small=True)
        vop(lambda v: v.reciprocal(g_w[:, 16:24], g_w[:, 8:16]), [W_], [W_],
            tag="grec", small=True)
        vop(lambda v: v.tensor_scalar(g_w[:, 24:32], g_w[:, 16:24], -1.0,
                                      1.0, ALU.mult, ALU.add), [W_], [W_],
            tag="gw1", small=True)
        vop(lambda v: v.tensor_scalar_mul(g_t[:], g_e2[:], g_w[:, 16:17]),
            [E2, W_], [T_], tag="gt2", small=True)
        vop(lambda v: v.scalar_tensor_tensor(gatec[:], g_e1[:], g_w[:, 24:25],
                                             g_t[:], ALU.mult, ALU.add),
            [E1, W_, T_], [GC], tag="gate", small=True)
        mm(psB[0:8, 1544:1672], gatec[:], c_i128b[:], pb3, [GC, CB],
           start=True, stop=True, tag="gTT")
        vop(lambda v, t_=tc: v.tensor_copy(gatet[:, t_ * 128:(t_ + 1) * 128],
                                           psB[0:8, 1544:1672]), pb3, [GT],
            tag="gatetev")
    tap("logt", logt, LOG)
    tap("gatet", gatet, GT)
    OUT = B("outac")
    GFL, GR, HS, HT = B("gfl"), B("grep"), B("hscl"), B("htmp")
    for e in range(NE if RUN_FFN >= 4 else 0):
        dma("gpsimd", "gfl", gfl[:], gatet[e:e + 1, :], [GT], [GFL],
            tag="gfl")
        pG = pb_A(0, 2048)
        for j in range(4):
            mm(psA[:, j * 512:(j + 1) * 512], c_onesb[:],
               gfl[:, j * 512:(j + 1) * 512], pG, [CB, GFL],
               start=True, stop=True, tag="grepM")
        act(grep[:], psA[:, 0:2048], AF.Copy, pG, [GR], tag="grepev")
        for fc in range(16):
            sl = fc % 3
            WF = B(f"w1fc{sl}")
            dma("gpsimd", "w1", w1fc[:, sl, :, :],
                dw1S[e, fc].rearrange("(a p) n -> p a n", p=128), [], [WF],
                tag="w1d")
            ps, pbank = (psA, pb_A) if fc % 2 == 0 else (psB, pb_B)
            pH = pbank(0, 2048)
            for dc in range(8):
                for ts_ in range(4):
                    mm(ps[:, ts_ * 512:(ts_ + 1) * 512], w1fc[:, sl, dc, :],
                       ln2fm[:, dc, ts_ * 512:(ts_ + 1) * 512], pH, [WF, LN2],
                       start=(dc == 0), stop=(dc == 7), tag="w1mm")
            act(htmp[:], ps[:, 0:2048], AF.Relu, pH + [CB], [HT],
                bias=c_b1h[:, e * 16 + fc:e * 16 + fc + 1], tag="hrelu")
            vop(lambda v, f_=fc: v.tensor_mul(hscl[:, f_, :], htmp[:],
                                              grep[:]), [HT, GR], [HS],
                tag="hscale")
        for dc in range(8):
            sl = dc % 3
            WD = B(f"w2dc{sl}")
            dma("gpsimd", "w2", w2dc[:, sl, :, :],
                dw2S[e, dc].rearrange("(a p) n -> p a n", p=128), [], [WD],
                tag="w2d")
            ps, pbank = (psA, pb_A) if dc % 2 == 0 else (psB, pb_B)
            pO = pbank(0, 2048)
            if e == 0:
                for ts_ in range(4):
                    mm(ps[:, ts_ * 512:(ts_ + 1) * 512],
                       c_b2[:, dc * 128:(dc + 1) * 128],
                       gatet[:, ts_ * 512:(ts_ + 1) * 512], pO, [CB, GT],
                       start=True, stop=False, tag="b2mm")
            for fc in range(16):
                for ts_ in range(4):
                    mm(ps[:, ts_ * 512:(ts_ + 1) * 512], w2dc[:, sl, fc, :],
                       hscl[:, fc, ts_ * 512:(ts_ + 1) * 512], pO, [WD, HS],
                       start=(e != 0 and fc == 0), stop=(fc == 15),
                       tag="w2mm")
            if e == 0:
                act(outac[:, dc, :], ps[:, 0:2048], AF.Copy, pO, [OUT],
                    tag="oev")
            else:
                vop(lambda v, d=dc, p=ps: v.tensor_add(
                    outac[:, d, :], outac[:, d, :], p[:, 0:2048]),
                    pO + [OUT], [OUT], tag="oadd")
    tap("hscl", hscl, HS)
    tap("outac", outac, OUT)
    OUTS, FO = B("outstore"), B("outf")
    for tc in range(16 if RUN_FFN >= 5 else 0):
        sl = tc % 2
        dma("sync", "x1r2", x1t2[:, sl, :], dx1scr[tc * 128:(tc + 1) * 128, :],
            [X1R, FO], [X1R], tag="x1rr")
        ps, pbank = (psA, pb_A) if tc % 2 == 0 else (psB, pb_B)
        pF = pbank(0, 1024)
        for dc in range(8):
            mm(ps[:, dc * 128:(dc + 1) * 128],
               outac[:, dc, tc * 128:(tc + 1) * 128], c_i128b[:], pF,
               [OUT, CB], start=(dc % 4 == 0), stop=(dc % 4 == 3),
               tag="foutT")
        vop(lambda v, s_=sl, p=ps: v.tensor_add(
            outf[:, s_, :], p[:, 0:1024], x1t2[:, s_, :]), pF + [X1R], [FO],
            tag="fadd")
        dma("sync", "outw", dout[tc * 128:(tc + 1) * 128, :], outf[:, sl, :],
            [FO], [OUTS], tag="outw")
    P.op("sync", None, reads=[OUTS], writes=[], tag="fence")
    if dtaps:
        P.op("sync", None, reads=[B("tapout")], writes=[], tag="tapfence")

    P.finalize(nc, gctx)
    with nc.Block() as block:
        block.tensor(lambda t: P.emit_engine("tensor", t))
        block.vector(lambda v: P.emit_engine("vector", v))
        block.scalar(lambda s: P.emit_engine("scalar", s))
        block.gpsimd(lambda g: P.emit_engine("gpsimd", g))
        block.sync(lambda q: P.emit_engine("sync", q))
    fctx.close()
    gctx.close()
    return nc


_tap_shapes = {
    "ufm": ([128, 16, SEG], BF16),
    "lat": ([NS, SEG], F32),
    "dbt": ([NS, SEG], F32),
    "cmt": ([NS, SEG], F32),
    "cumt": ([NS, 2, 128], F32),
    "latm": ([128, 2, NS], F32),
    "dbbtm": ([128, 2, NS], BF16),
    "cbart": ([NS, 2, 128], BF16),
    "daq": ([NS, 2, 8], F32),
    "utm": ([128, 2, DI], BF16),
    "acoef": ([NS, 8], F32),
    "crow": ([128, DI], BF16),
    "zfm": ([128, 16, SEG], BF16),
    "delt": ([NS, SEG], F32),
    "hfp": ([NS, DI], F32),
    "ebuf": ([128, DI], BF16),
    "ybar": ([128, 16, 128], BF16),
    "x1scr": ([L, DM], F32),
    "logt": ([8, L], F32),
    "gatet": ([8, L], BF16),
    "hscl": ([128, 16, L], BF16),
    "outac": ([128, 8, L], BF16),
}


def kernel(**inputs):
    x_full = np.asarray(inputs["L0_attn_x"], np.float32)
    W = host_prep(inputs)
    taps = [(nm, *(_tap_shapes[nm])) for nm in TAPS]
    nc = build_program(taps=taps)
    in_maps = []
    for c in range(8):
        m = dict(x=np.ascontiguousarray(x_full[c]))
        m.update(W)
        in_maps.append(m)
    res = run_bass_kernel_spmd(nc, in_maps, core_ids=list(range(8)))
    out = np.stack([np.asarray(r["out"], np.float32) for r in res.results])
    if TAPS:
        kernel.tap_results = [{nm: np.asarray(r["tap_" + nm]) for nm in TAPS}
                              for r in res.results]
    return out


# version 04
# speedup vs baseline: 1.0794x; 1.0420x over previous
import numpy as np
import ml_dtypes
from contextlib import ExitStack

import concourse.bass as bass
import concourse.mybir as mybir
from concourse.bass_utils import run_bass_kernel_spmd

F32 = mybir.dt.float32
BF16 = mybir.dt.bfloat16
AF = mybir.ActivationFunctionType
ALU = mybir.AluOpType
AXX = mybir.AxisListType.X

L = 2048
DM = 1024
DI = 2048
NS = 16
SEG = 512
NSEG = L // SEG
NCH = SEG // 128
NE = 8
H = 1024
FF = 2048
BIG = 1.0e30
EPS = 1.0e-5

TAPS = []
RUN_NSEG = NSEG
RUN_FFN = 5
RUN_WIN = True
RUN_CHUNKS = NCH
RUN_LN = 3
RUN_LNT = 16

ENG = ("tensor", "vector", "scalar", "gpsimd", "sync")
STREAM_MOD = {"const": 1, "wout": 1, "xin": 4, "outw": 1, "w1": 3,
              "w2": 3, "cfl": 1, "rfl": 1, "gfl": 1, "tap": 1}


class Buf:
    __slots__ = ("name", "writers", "readers")

    def __init__(self, name):
        self.name = name
        self.writers = {}
        self.readers = {}


class Plan:
    def __init__(self):
        self.ops = []
        self.per_eng = {e: [] for e in ENG}
        self.stream_counts = {}

    def op(self, eng, fn, reads=(), writes=(), dma_stream=None, tag="",
           small=False):
        gid = len(self.ops)
        if dma_stream is not None:
            k = self.stream_counts.get(dma_stream, 0)
            self.stream_counts[dma_stream] = k + 1
            chan = ("D", dma_stream, k % STREAM_MOD.get(dma_stream, 2))
        else:
            chan = eng
        deps = set()

        def dep_on(gid2):
            p = self.ops[gid2]
            if p["dma"] is None and p["eng"] == eng:
                return
            deps.add(gid2)

        for b in reads:
            for g2 in b.writers.values():
                dep_on(g2)
        for b in writes:
            for g2 in b.writers.values():
                dep_on(g2)
            for g2 in b.readers.values():
                dep_on(g2)
        o = dict(gid=gid, eng=eng, fn=fn, deps=deps, dma=dma_stream,
                 chan=chan, tag=tag, need_inc=False, sem=None, val=None,
                 small=small)
        self.ops.append(o)
        self.per_eng[eng].append(o)
        for b in reads:
            b.readers[chan] = gid
        for b in writes:
            b.writers[chan] = gid
        return gid

    def barrier(self, B):
        bb = B(f"__bar{len(self.ops)}")
        for e in ENG:
            self.op(e, lambda eng: eng.nop(), reads=[], writes=[bb],
                    tag="bar_w")
        for e in ENG:
            self.op(e, None, reads=[bb], writes=[], tag="bar_r")

    def finalize(self, nc, ctx):
        for o in self.ops:
            if o["eng"] == "tensor" and o["dma"] is None and o["fn"] is not None:
                o["need_inc"] = True
            for d in o["deps"]:
                p = self.ops[d]
                assert p["fn"] is not None or p["dma"] is not None, (
                    "dep on no-op " + p["tag"])
                p["need_inc"] = True
        comp_sems = {e: ctx.enter_context(nc.semaphore(f"sem_{e}"))
                     for e in ENG}
        stream_sems = {}
        counters = {e: 0 for e in ENG}
        for o in self.ops:
            if o["dma"] is not None:
                key = o["chan"]
                if key not in stream_sems:
                    stream_sems[key] = [ctx.enter_context(nc.semaphore(
                        f"dma_{key[1]}_{key[2]}")), 0]
                stream_sems[key][1] += 16
                o["need_inc"] = True
                o["sem"] = stream_sems[key][0]
                o["val"] = stream_sems[key][1]
                self.stream_final = getattr(self, "stream_final", {})
                self.stream_final[key] = stream_sems[key][1]
            elif o["need_inc"]:
                counters[o["eng"]] += 1
                o["sem"] = comp_sems[o["eng"]]
                o["val"] = counters[o["eng"]]

    def emit_engine(self, ename, eng):
        waited = {}
        for o in self.per_eng[ename]:
            pass
        for o in self.per_eng[ename]:
            for d in sorted(o["deps"]):
                p = self.ops[d]
                sem, val = p["sem"], p["val"]
                assert sem is not None, f"{o['tag']} dep {p['tag']} no sem"
                if waited.get(sem.name, -1) < val:
                    eng.wait_ge(sem, val)
                    waited[sem.name] = val
            if o["fn"] is None:
                continue
            inst = o["fn"](eng)
            if o.get("small"):
                inst = eng.drain()
            if o["need_inc"]:
                inst.then_inc(o["sem"], 16 if o["dma"] is not None else 1)
        for o in self.per_eng[ename]:
            if o["dma"] is not None:
                sem, val = o["sem"], self.stream_final[o["chan"]]
                if waited.get(sem.name, -1) < val:
                    eng.wait_ge(sem, val)
                    waited[sem.name] = val


def host_prep(inp):
    f = np.float32
    ac = np.ascontiguousarray

    def g(n):
        return np.asarray(inp[n], f)
    W_inT = ac(g("L0_attn_W_in").T)
    winS = ac(W_inT.reshape(DM, 16, 256).transpose(1, 0, 2))
    woutT = ac(g("L0_attn_W_out").T)
    wxdtT = ac(np.concatenate([g("L0_attn_W_x"), g("L0_attn_W_dt")], 0).T)
    convw = ac(g("L0_attn_conv_w").reshape(DI, 4)
               .reshape(16, 128, 4).transpose(1, 0, 2).reshape(128, 64))
    convb = ac(g("L0_attn_conv_b").reshape(16, 128).T)
    dvec = ac(g("L0_attn_D").reshape(16, 128).T)
    alog = ac(g("L0_attn_A_log").reshape(NS, 1))
    bdt = ac(g("L0_attn_b_dt").reshape(NS, 1))
    w1S = ac(g("L0_ffn_W1").reshape(NE, H, 16, 128).transpose(0, 2, 1, 3))
    w2S = ac(g("L0_ffn_W2").reshape(NE, FF, 8, 128).transpose(0, 2, 1, 3))
    b1h = ac(g("L0_ffn_b1").reshape(NE, 16, 128)
             .transpose(2, 0, 1).reshape(128, NE * 16))
    b2 = ac(g("L0_ffn_b2"))
    wrS = ac(g("L0_ffn_Wr").reshape(8, 128, 8).transpose(1, 0, 2))
    br = ac(g("L0_ffn_br").reshape(NE, 1))
    i128 = np.eye(128, dtype=f)
    i16 = np.eye(16, dtype=f)
    tau = np.arange(128)
    m1 = (tau[:, None] <= tau[None, :]).astype(f)
    m2 = (tau[:, None] > tau[None, :]).astype(f)
    ub = (-BIG * (tau[None, :] > tau[:, None])).astype(f)
    ones128 = np.ones((1, 128), dtype=f)
    maskrep = np.tile((tau[:, None] <= tau[None, :]).astype(f), (1, 16))
    return dict(maskrep=maskrep, winS=winS, woutT=woutT, wxdtT=wxdtT, convw=convw,
                convb=convb, dvec=dvec, alog=alog, bdt=bdt, w1S=w1S,
                w2S=w2S, b1h=b1h, b2=b2, wrS=wrS, br=br, i128=i128,
                i16=i16, m1=m1, m2=m2, ub=ub, ones128=ones128)


def build_program(taps=()):
    nc = bass.Bass("TRN2", target_bir_lowering=False,
                   detect_race_conditions=False)
    P = Plan()
    bufs = {}

    def B(name):
        if name not in bufs:
            bufs[name] = Buf(name)
        return bufs[name]

    dx = nc.dram_tensor("x", [L, DM], F32, kind="ExternalInput").ap()
    dwinS = nc.dram_tensor("winS", [16, DM, 256], F32, kind="ExternalInput").ap()
    dwoutT = nc.dram_tensor("woutT", [DI, DM], F32, kind="ExternalInput").ap()
    dwxdtT = nc.dram_tensor("wxdtT", [DI, 48], F32, kind="ExternalInput").ap()
    dconvw = nc.dram_tensor("convw", [128, 64], F32, kind="ExternalInput").ap()
    dconvb = nc.dram_tensor("convb", [128, 16], F32, kind="ExternalInput").ap()
    ddvec = nc.dram_tensor("dvec", [128, 16], F32, kind="ExternalInput").ap()
    dalog = nc.dram_tensor("alog", [NS, 1], F32, kind="ExternalInput").ap()
    dbdt = nc.dram_tensor("bdt", [NS, 1], F32, kind="ExternalInput").ap()
    dw1S = nc.dram_tensor("w1S", [NE, 16, H, 128], F32, kind="ExternalInput").ap()
    dw2S = nc.dram_tensor("w2S", [NE, 8, FF, 128], F32, kind="ExternalInput").ap()
    db1h = nc.dram_tensor("b1h", [128, NE * 16], F32, kind="ExternalInput").ap()
    db2 = nc.dram_tensor("b2", [NE, H], F32, kind="ExternalInput").ap()
    dwrS = nc.dram_tensor("wrS", [128, 8, 8], F32, kind="ExternalInput").ap()
    dbr = nc.dram_tensor("br", [NE, 1], F32, kind="ExternalInput").ap()
    di128 = nc.dram_tensor("i128", [128, 128], F32, kind="ExternalInput").ap()
    di16 = nc.dram_tensor("i16", [16, 16], F32, kind="ExternalInput").ap()
    dm1 = nc.dram_tensor("m1", [128, 128], F32, kind="ExternalInput").ap()
    dm2 = nc.dram_tensor("m2", [128, 128], F32, kind="ExternalInput").ap()
    dub = nc.dram_tensor("ub", [128, 128], F32, kind="ExternalInput").ap()
    dones = nc.dram_tensor("ones128", [1, 128], F32, kind="ExternalInput").ap()
    dmaskrep = nc.dram_tensor("maskrep", [128, 2048], F32, kind="ExternalInput").ap()
    dout = nc.dram_tensor("out", [L, DM], F32, kind="ExternalOutput").ap()
    dx1scr = nc.dram_tensor("x1scr", [L, DM], F32, kind="Internal").ap()
    dtaps = {}
    for (nm, shape, dt) in taps:
        dtaps[nm] = nc.dram_tensor("tap_" + nm, shape, dt,
                                   kind="ExternalOutput").ap()

    def tap(nm, tile, rbuf):
        if nm in dtaps:
            ap_ = tile if isinstance(tile, bass.AP) else tile[:]
            dma("sync", "tap", dtaps[nm], ap_, [rbuf], [B("tapout")],
                tag=f"tap_{nm}")

    gctx = ExitStack()

    def SB(c, n, s, d):
        return c.enter_context(nc.sbuf_tensor(n, s, d))

    def mm(out_ap, lhsT_ap, rhs_ap, pbufs, rbufs, start, stop, tag=""):
        P.op("tensor",
             lambda t, o=out_ap, l=lhsT_ap, r=rhs_ap, s0=start, s1=stop:
                 t.matmul(o, l, r, start=s0, stop=s1),
             reads=rbufs, writes=pbufs, tag=tag)

    def act(out_ap, in_ap, func, rbufs, wbufs, bias=0.0, scale=1.0, tag="",
            small=False):
        P.op("scalar",
             lambda s, o=out_ap, i=in_ap, f=func, b=bias, sc=scale:
                 s.activation(o, i, f, bias=b, scale=sc),
             reads=rbufs, writes=wbufs, tag=tag, small=small)

    def vop(fn, rbufs, wbufs, tag="", small=False):
        P.op("vector", fn, reads=rbufs, writes=wbufs, tag=tag, small=small)

    def dma(eng, stream, out_ap, in_ap, rbufs, wbufs, tag=""):
        P.op(eng, lambda e, o=out_ap, i=in_ap: e.dma_start(o, i),
             reads=rbufs, writes=wbufs, dma_stream=stream, tag=tag)

    def pb_A(lo, hi):
        return [B(f"psA_b{i}") for i in range(lo // 512, (hi - 1) // 512 + 1)]

    def pb_B(lo, hi):
        return [B(f"psB_b{i}") for i in range(lo // 512, (hi - 1) // 512 + 1)]

    c_i128b = SB(gctx, "c_i128b", [128, 128], BF16)
    c_i128f = SB(gctx, "c_i128f", [128, 128], F32)
    c_i16f = SB(gctx, "c_i16f", [16, 16], F32)
    c_m1 = SB(gctx, "c_m1", [128, 128], F32)
    c_m2 = SB(gctx, "c_m2", [128, 128], F32)
    c_ub = SB(gctx, "c_ub", [128, 128], F32)
    c_onesf = SB(gctx, "c_onesf", [1, 128], F32)
    c_onesb = SB(gctx, "c_onesb", [1, 128], BF16)
    c_mrep = SB(gctx, "c_mrep", [128, 2048], BF16)
    c_convw = SB(gctx, "c_convw", [128, 64], F32)
    c_convb = SB(gctx, "c_convb", [128, 16], F32)
    c_dvec = SB(gctx, "c_dvec", [128, 16], F32)
    c_alog = SB(gctx, "c_alog", [NS, 1], F32)
    c_acoef = SB(gctx, "c_acoef", [NS, 8], F32)
    c_bdt = SB(gctx, "c_bdt", [NS, 1], F32)
    c_wxdt = SB(gctx, "c_wxdt", [128, 16, 48], BF16)
    c_wrS = SB(gctx, "c_wrS", [128, 8, 8], BF16)
    c_br = SB(gctx, "c_br", [8, 1], F32)
    c_b1h = SB(gctx, "c_b1h", [128, NE * 16], F32)
    c_b2 = SB(gctx, "c_b2", [NE, H], BF16)
    lnst = SB(gctx, "lnst", [128, 12], F32)
    lnmv = SB(gctx, "lnmv", [128, 2], F32)
    lnrs = SB(gctx, "lnrs", [128, 8], F32)
    psA = gctx.enter_context(nc.psum_tensor("psA", [128, 2048], F32))
    psB = gctx.enter_context(nc.psum_tensor("psB", [128, 2048], F32))

    CB = B("consts")
    for dst, src in [
        (c_i128b[:], di128), (c_i128f[:], di128), (c_i16f[:], di16),
        (c_m1[:], dm1), (c_m2[:], dm2), (c_ub[:], dub),
        (c_onesf[:], dones), (c_onesb[:], dones), (c_mrep[:], dmaskrep),
        (c_convw[:], dconvw), (c_convb[:], dconvb), (c_dvec[:], ddvec),
        (c_alog[:], dalog), (c_bdt[:], dbdt),
        (c_wxdt[:], dwxdtT.rearrange("(a p) k -> p a k", p=128)),
        (c_wrS[:], dwrS), (c_br[:], dbr), (c_b1h[:], db1h), (c_b2[:], db2),
    ]:
        dma("gpsimd", "const", dst, src, [], [CB], tag="const")
    ACF = B("acoef")
    act(lnrs[0:1, 0:8], lnrs[0:1, 0:8], AF.Exp, [], [B("lnrs")], tag="warm_e", small=True)
    act(lnrs[0:1, 0:8], lnrs[0:1, 0:8], AF.Ln, [B("lnrs")], [B("lnrs")],
        tag="warm_l", small=True)
    act(lnrs[0:1, 0:8], lnrs[0:1, 0:8], AF.Silu, [B("lnrs")], [B("lnrs")],
        tag="warm_s", small=True)
    act(lnrs[0:1, 0:8], lnrs[0:1, 0:8], AF.Ln, [B("lnrs")], [B("lnrs")],
        tag="warm_l2", small=True)
    act(c_acoef[:], c_alog[:].broadcast_to([NS, 8]), AF.Exp, [CB], [ACF],
        tag="acoef_exp", small=True)
    vop(lambda v: v.tensor_scalar_mul(c_acoef[:], c_acoef[:], -1.0),
        [ACF], [ACF], tag="acoef_neg", small=True)

    mctx = ExitStack()
    x1t = SB(mctx, "x1t", [128, 2, DM], F32)
    woutT = SB(mctx, "sb_woutT", [128, 16, DM], BF16)
    winSl = SB(mctx, "winSl", [128, 2, 8, 256], BF16)
    x_seg = SB(mctx, "x_seg", [128, NCH, DM], F32)
    ln1tm = SB(mctx, "ln1tm", [128, 2, DM], BF16)
    ln1fm = SB(mctx, "ln1fm", [128, 8, SEG], BF16)
    xi = SB(mctx, "xi", [128, 16, SEG + 3], BF16)
    xi_sv = SB(mctx, "xi_sv", [128, 16, 3], BF16)
    zfm = SB(mctx, "zfm", [128, 16, SEG], BF16)
    ufm = SB(mctx, "ufm", [128, 16, SEG], BF16)
    t1c = SB(mctx, "t1c", [128, SEG], F32)
    utm = SB(mctx, "utm", [128, 2, DI], BF16)
    crow = SB(mctx, "crow", [128, DI], BF16)
    ebuf = SB(mctx, "ebuf", [128, DI], BF16)
    t1e = SB(mctx, "t1e", [128, DI], BF16)
    ybar = SB(mctx, "ybar", [128, 16, 128], BF16)
    bmt = SB(mctx, "bmt", [NS, SEG], F32)
    cmt = SB(mctx, "cmt", [NS, SEG], F32)
    cmtb = SB(mctx, "cmtb", [NS, SEG], BF16)
    delt = SB(mctx, "delt", [NS, SEG], F32)
    lat = SB(mctx, "lat", [NS, SEG], F32)
    dbt = SB(mctx, "dbt", [NS, SEG], F32)
    latm = SB(mctx, "latm", [128, 2, NS], F32)
    dbtm = SB(mctx, "dbtm", [128, 2, NS], F32)
    cumtm = SB(mctx, "cumtm", [128, 2, NS], F32)
    negtm = SB(mctx, "negtm", [128, 2, NS], F32)
    revtm = SB(mctx, "revtm", [128, 2, NS], F32)
    erev = SB(mctx, "erev", [128, 2, NS], F32)
    dbbtm = SB(mctx, "dbbtm", [128, 2, NS], BF16)
    cumt = SB(mctx, "cumt", [NS, 2, 128], F32)
    ect = SB(mctx, "ect", [NS, 2, 128], F32)
    cbart = SB(mctx, "cbart", [NS, 2, 128], BF16)
    daq = SB(mctx, "daq", [NS, 2, 8], F32)
    cumfl = SB(mctx, "cumfl", [1, DI], F32)
    rowfl = SB(mctx, "rowfl", [1, DI], BF16)
    hfp = SB(mctx, "hfp", [NS, DI], F32)
    hbf = SB(mctx, "hbf", [NS, DI], BF16)

    WOUT = B("woutT")
    for ec in range(16):
        dma("gpsimd", "wout", woutT[:, ec, :],
            dwoutT[ec * 128:(ec + 1) * 128, :], [], [WOUT], tag="woutd")
    HB, HBB = B("hfp"), B("hbf")
    vop(lambda v: v.memset(hfp[:], 0.0), [], [HB], tag="h0")
    vop(lambda v: v.memset(hbf[:], 0.0), [], [HBB], tag="hbf0")
    XIB = B("xi")
    vop(lambda v: v.memset(xi[:], 0.0), [], [XIB], tag="xipad0")

    T1 = B("t1c")

    def emit_conv(ic):
        vop(lambda v, i=ic: v.tensor_scalar_mul(
            t1c[:], xi[:, i, 0:SEG], c_convw[:, 4 * i:4 * i + 1]),
            [XIB, CB], [T1], tag="cv0")
        for k in range(1, 4):
            vop(lambda v, i=ic, kk=k: v.scalar_tensor_tensor(
                t1c[:], xi[:, i, kk:SEG + kk],
                c_convw[:, 4 * i + kk:4 * i + kk + 1], t1c[:],
                ALU.mult, ALU.add), [XIB, CB, T1], [T1], tag="cvk")
        act(ufm[:, ic, :], t1c[:], AF.Silu, [T1, CB], [UB_],
            bias=c_convb[:, ic:ic + 1], tag="silu_u")

    def emit_zsilu(ic):
        act(zfm[:, ic, :], zfm[:, ic, :], AF.Silu, [ZB], [ZB], tag="silu_z")

    XSEG, LT, LF = B("x_seg"), B("ln1tm"), B("ln1fm")
    ZB, UB_, UD = B("zfm"), B("ufm"), B("xi")
    LS, MV, RS = B("lnst"), B("lnmv"), B("lnrs")

    for s in range(RUN_NSEG):
        sbase = s * SEG
        for ch in range(NCH):
            dma("sync", "xin", x_seg[:, ch, :],
                dx[sbase + ch * 128: sbase + (ch + 1) * 128, :],
                [], [XSEG], tag=f"xin{s}_{ch}")
        for ch in range(NCH if RUN_LN >= 1 else 0):
            vop(lambda v, c=ch: v.bn_stats(lnst[:, 0:6], x_seg[:, c, 0:512]),
                [XSEG], [LS], tag="bn1", small=True)
            vop(lambda v, c=ch: v.bn_stats(lnst[:, 6:12],
                                           x_seg[:, c, 512:1024]),
                [XSEG], [LS], tag="bn2", small=True)
            vop(lambda v: v.bn_aggr(lnmv[:], lnst[:]), [LS], [MV], tag="agg", small=True)
            vop(lambda v: v.tensor_scalar_add(
                lnrs[:], lnmv[:, 1:2].broadcast_to([128, 8]), EPS),
                [MV], [RS], tag="veps", small=True)
            act(lnrs[:], lnrs[:], AF.Ln, [RS], [RS], tag="lnv", small=True)
            vop(lambda v: v.tensor_scalar_mul(lnrs[:], lnrs[:], -0.5),
                [RS], [RS], tag="lnvm", small=True)
            act(lnrs[:], lnrs[:], AF.Exp, [RS], [RS], tag="rstd", small=True)
            if RUN_LN < 2:
                continue
            vop(lambda v, c=ch: v.tensor_scalar(
                ln1tm[:, c % 2, :], x_seg[:, c, :], lnmv[:, 0:1], None,
                ALU.subtract), [XSEG, MV], [LT], tag="ln1sub")
            vop(lambda v, c=ch: v.tensor_scalar_mul(
                ln1tm[:, c % 2, :], ln1tm[:, c % 2, :], lnrs[:, 0:1]),
                [LT, RS], [LT], tag="ln1mul")
            for w in range(2 if RUN_LN >= 3 else 0):
                c0 = w * 512
                pb = pb_A(c0, c0 + 512)
                for j in range(4):
                    dc = w * 4 + j
                    mm(psA[:, c0 + j * 128:c0 + (j + 1) * 128],
                       ln1tm[:, ch % 2, dc * 128:(dc + 1) * 128], c_i128b[:],
                       pb, [LT, CB], start=(j == 0), stop=(j == 3),
                       tag="ln1T")
                vop(lambda v, c=ch, ww=w, cc0=c0: v.tensor_copy(
                    ln1fm[:, ww * 4:(ww + 1) * 4, c * 128:(c + 1) * 128],
                    psA[:, cc0:cc0 + 512].rearrange("p (a n) -> p a n", a=4)),
                    pb, [LF], tag="ln1ev")
        for g in range(16 if RUN_WIN else 0):
            sl = g % 2
            WS = B(f"winSl{sl}")
            dma("gpsimd", "winS", winSl[:, sl, :, :],
                dwinS[g].rearrange("(a p) n -> p a n", p=128),
                [], [WS], tag=f"winS{s}_{g}")
            b0 = sl * 2
            pb0 = pb_A(b0 * 512, b0 * 512 + 512)
            pb1 = pb_A((b0 + 1) * 512, (b0 + 1) * 512 + 512)
            for dc in range(8):
                mm(psA[:, b0 * 512:(b0 + 1) * 512], winSl[:, sl, dc, 0:128],
                   ln1fm[:, dc, :], pb0, [WS, LF],
                   start=(dc == 0), stop=(dc == 7), tag="xzA")
                mm(psA[:, (b0 + 1) * 512:(b0 + 2) * 512],
                   winSl[:, sl, dc, 128:256], ln1fm[:, dc, :], pb1, [WS, LF],
                   start=(dc == 0), stop=(dc == 7), tag="xzB")
            if s > 0 and g == 0:
                vop(lambda v: v.tensor_copy(xi[:, :, 0:3], xi_sv[:]),
                    [B("xi_sv")], [XIB], tag="xipad", small=True)
            if g < 8:
                act(xi[:, 2 * g, 3:SEG + 3], psA[:, b0 * 512:(b0 + 1) * 512],
                    AF.Copy, pb0, [XIB], tag="evxi")
                act(xi[:, 2 * g + 1, 3:SEG + 3],
                    psA[:, (b0 + 1) * 512:(b0 + 2) * 512], AF.Copy, pb1,
                    [XIB], tag="evxi")
            else:
                act(zfm[:, 2 * (g - 8), :], psA[:, b0 * 512:(b0 + 1) * 512],
                    AF.Copy, pb0, [ZB], tag="evz")
                act(zfm[:, 2 * (g - 8) + 1, :],
                    psA[:, (b0 + 1) * 512:(b0 + 2) * 512], AF.Copy, pb1,
                    [ZB], tag="evz")
            if RUN_WIN and 1 <= g <= 8:
                emit_conv(2 * (g - 1))
                emit_conv(2 * (g - 1) + 1)
            if RUN_WIN and g >= 9:
                emit_zsilu(2 * (g - 9))
                emit_zsilu(2 * (g - 9) + 1)
        if RUN_WIN:
            emit_zsilu(14)
            emit_zsilu(15)
            vop(lambda v: v.tensor_copy(xi_sv[:], xi[:, :, SEG:SEG + 3]),
                [XIB], [B("xi_sv")], tag="xisave", small=True)

        for ic in range(16 if RUN_WIN else 0):
            vop(lambda v, i=ic: v.tensor_scalar_mul(
                xi[:, i, 3:SEG + 3], ufm[:, i, :], c_dvec[:, i:i + 1]),
                [UB_, CB], [UD], tag="uD")
        pWx = pb_A(0, 1536)
        for ec in range(16 if RUN_WIN else 0):
            for j in range(3):
                mm(psA[0:16, j * 512:(j + 1) * 512],
                   c_wxdt[:, ec, j * 16:(j + 1) * 16], ufm[:, ec, :], pWx,
                   [CB, UB_], start=(ec == 0), stop=(ec == 15), tag="wxdt")
        BM, CM, CMB = B("bmt"), B("cmt"), B("cmtb")
        DL, LA, DB = B("delt"), B("lat"), B("dbt")
        if not RUN_WIN:
            continue
        vop(lambda v: v.tensor_copy(bmt[:], psA[0:16, 0:512]), pWx, [BM],
            tag="evbm")
        vop(lambda v: v.tensor_copy(cmt[:], psA[0:16, 512:1024]), pWx, [CM],
            tag="evcm")
        vop(lambda v: v.tensor_copy(cmtb[:], cmt[:]), [CM], [CMB], tag="cmtb")
        act(delt[:], psA[0:16, 1024:1536], AF.Exp, pWx + [CB], [DL],
            bias=c_bdt[:], tag="d_exp")
        vop(lambda v: v.tensor_scalar_add(delt[:], delt[:], 1.0), [DL], [DL],
            tag="d_p1")
        act(delt[:], delt[:], AF.Ln, [DL], [DL], tag="d_ln")
        vop(lambda v: v.tensor_scalar_mul(lat[:], delt[:], c_acoef[:, 0:1]),
            [DL, ACF], [LA], tag="la")
        vop(lambda v: v.tensor_mul(dbt[:], delt[:], bmt[:]), [DL, BM], [DB],
            tag="db")

        for cc in range(min(NCH, RUN_CHUNKS)):
            ci = cc % 2
            ut = cc % 2
            gch = s * NCH + cc
            pb3 = pb_B(1536, 2048)
            mm(psB[:, 1536:1552], lat[:, cc * 128:(cc + 1) * 128], c_i16f[:],
               pb3, [LA, CB], start=True, stop=True, tag="latmM")
            mm(psB[:, 1552:1568], dbt[:, cc * 128:(cc + 1) * 128], c_i16f[:],
               pb3, [DB, CB], start=True, stop=True, tag="dbtmM")
            LTM, DTM = B("latm"), B("dbtm")
            vop(lambda v, i=ci: v.tensor_copy(latm[:, i, :], psB[:, 1536:1552]),
                pb3, [LTM], tag="latmev", small=True)
            vop(lambda v, i=ci: v.tensor_copy(dbtm[:, i, :], psB[:, 1552:1568]),
                pb3, [DTM], tag="dbtmev", small=True)
            mm(psB[:, 1568:1584], c_m1[:], latm[:, ci, :], pb3, [CB, LTM],
               start=True, stop=True, tag="cumM")
            mm(psB[:, 1584:1600], c_m2[:], latm[:, ci, :], pb3, [CB, LTM],
               start=True, stop=True, tag="revM")
            mm(psB[0:16, 1600:1728], latm[:, ci, :], c_m1[:], pb3, [LTM, CB],
               start=True, stop=True, tag="cumTM")
            CUM, REV, CT = B("cumtm"), B("revtm"), B("cumt")
            vop(lambda v, i=ci: v.tensor_copy(cumtm[:, i, :], psB[:, 1568:1584]),
                pb3, [CUM], tag="cumev", small=True)
            NEG = B("negtm")
            vop(lambda v, i=ci: v.tensor_scalar_mul(
                negtm[:, i, :], cumtm[:, i, :], -1.0), [CUM], [NEG],
                tag="negtm", small=True)
            vop(lambda v, i=ci: v.tensor_copy(revtm[:, i, :], psB[:, 1584:1600]),
                pb3, [REV], tag="revev", small=True)
            vop(lambda v, i=ci: v.tensor_copy(cumt[:, i, :],
                                              psB[0:16, 1600:1728]),
                pb3, [CT], tag="cumtev")
            ER, ECT, DAQ = B("erev"), B("ect"), B("daq")
            act(erev[:, ci, :], revtm[:, ci, :], AF.Exp, [REV], [ER], tag="er")
            act(ect[:, ci, :], cumt[:, ci, :], AF.Exp, [CT], [ECT], tag="ec")
            act(daq[:, ci, :], cumt[:, ci, 127:128].broadcast_to([NS, 8]),
                AF.Exp, [CT], [DAQ], tag="daq", small=True)
            DBB, CBT = B("dbbtm"), B("cbart")
            vop(lambda v, i=ci: v.tensor_mul(
                dbbtm[:, i, :], dbtm[:, i, :], erev[:, i, :]), [DTM, ER],
                [DBB], tag="dbb", small=True)
            vop(lambda v, i=ci, c=cc: v.tensor_mul(
                cbart[:, i, :], cmt[:, c * 128:(c + 1) * 128], ect[:, i, :]),
                [CM, ECT], [CBT], tag="cbart")
            CFL, RFL = B("cumfl"), B("rowfl")
            dma("gpsimd", "cfl", cumfl[:], cumt[:, ci, :], [CT], [CFL],
                tag="cfl")
            dma("gpsimd", "rfl", rowfl[:], cmtb[:, cc * 128:(cc + 1) * 128],
                [CMB], [RFL], tag="rfl")
            UT = B(f"utm{ut}")
            for w in range(4):
                c0 = 1024 + (w % 2) * 512
                pbu = pb_B(c0, c0 + 512)
                for j in range(4):
                    ic = w * 4 + j
                    mm(psB[:, c0 + j * 128:c0 + (j + 1) * 128],
                       ufm[:, ic, cc * 128:(cc + 1) * 128], c_i128b[:], pbu,
                       [UB_, CB], start=(j == 0), stop=(j == 3), tag="uT")
                vop(lambda v, u=ut, ww=w, cc0=c0: v.tensor_copy(
                    utm[:, u, ww * 512:(ww + 1) * 512], psB[:, cc0:cc0 + 512]),
                    pbu, [UT], tag="uTev")
            pD = pb_A(0, 2048)
            for j in range(4):
                mm(psA[:, j * 512:(j + 1) * 512], c_onesf[:],
                   cumfl[:, j * 512:(j + 1) * 512], pD, [CB, CFL],
                   start=True, stop=True, tag="diff")
            EB = B("ebuf")
            for n in range(NS):
                act(ebuf[:, n * 128:(n + 1) * 128],
                    psA[:, n * 128:(n + 1) * 128], AF.Relu, pD + [CUM], [EB],
                    bias=cumtm[:, ci, n:n + 1], scale=-1.0, tag="erelu")
            for n in range(NS):
                act(ebuf[:, n * 128:(n + 1) * 128],
                    ebuf[:, n * 128:(n + 1) * 128], AF.Exp, [EB], [EB],
                    scale=-1.0, tag="eexp")
            vop(lambda v: v.tensor_mul(ebuf[:], ebuf[:], c_mrep[:]),
                [EB, CB], [EB], tag="emask")
            for n in range(NS):
                vop(lambda v, nn=n, i=ci: v.tensor_scalar_mul(
                    ebuf[:, nn * 128:(nn + 1) * 128],
                    ebuf[:, nn * 128:(nn + 1) * 128],
                    dbtm[:, i, nn:nn + 1]), [EB, DTM], [EB], tag="edb")
            pC = pD
            for j in range(4):
                mm(psA[:, j * 512:(j + 1) * 512], c_onesb[:],
                   rowfl[:, j * 512:(j + 1) * 512], pC, [CB, RFL],
                   start=True, stop=True, tag="crowM")
            CR = B("crow")
            act(crow[:], psA[:, 0:2048], AF.Copy, pC, [CR], tag="crowev")
            vop(lambda v: v.tensor_mul(ebuf[:], ebuf[:], crow[:]), [EB, CR],
                [EB], tag="wmul")
            for lo, hi in ((1024, 2048), (512, 1024), (256, 512), (128, 256)):
                vop(lambda v, l=lo, h=hi: v.tensor_add(
                    ebuf[:, 0:l], ebuf[:, 0:l], ebuf[:, l:h]), [EB], [EB],
                    tag="fold")
            pY = pb_B(0, 2048)
            for ec in range(16):
                mm(psB[:, ec * 128:(ec + 1) * 128],
                   utm[:, ut, ec * 128:(ec + 1) * 128], ebuf[:, 0:128], pY,
                   [UT, EB], start=(ec % 4 == 0), stop=False, tag="intra")
            for ec in range(16):
                mm(psB[:, ec * 128:(ec + 1) * 128],
                   hbf[:, ec * 128:(ec + 1) * 128], cbart[:, ci, :], pY,
                   [HBB, CBT], start=False, stop=(ec % 4 == 3), tag="inter")
            T1E, YB = B("t1e"), B("ybar")
            vop(lambda v, c=cc: v.tensor_add(
                t1e[:].rearrange("p (a n) -> p a n", a=16),
                psB[:, 0:2048].rearrange("p (a n) -> p a n", a=16),
                xi[:, :, 3 + c * 128:3 + (c + 1) * 128]), pY + [UD], [T1E],
                tag="epi1")
            vop(lambda v, c=cc: v.tensor_mul(
                ybar[:], t1e[:].rearrange("p (a n) -> p a n", a=16),
                zfm[:, :, c * 128:(c + 1) * 128]), [T1E, ZB], [YB], tag="epi2")
            pS = pD
            for j in range(4):
                mm(psA[0:16, j * 512:(j + 1) * 512], dbbtm[:, ci, :],
                   utm[:, ut, j * 512:(j + 1) * 512], pS, [DBB, UT],
                   start=True, stop=True, tag="state")

            vop(lambda v, i=ci: v.scalar_tensor_tensor(
                hfp[:], hfp[:], daq[:, i, 0:1], psA[0:16, 0:2048],
                ALU.mult, ALU.add), [HB, DAQ] + pS, [HB], tag="hupd")

            vop(lambda v: v.tensor_copy(hbf[:], hfp[:]), [HB], [HBB],
                tag="hcast")
            pW = pb_B(0, 1024)
            for ec in range(16):
                for dh in range(2):
                    mm(psB[:, dh * 512:(dh + 1) * 512], ybar[:, ec, :],
                       woutT[:, ec, dh * 512:(dh + 1) * 512], pW, [YB, WOUT],
                       start=(ec == 0), stop=(ec == 15), tag="wout")
            X1C = B("x1t")
            sl = cc % 2
            vop(lambda v, c=cc, s_=sl: v.tensor_add(
                x1t[:, s_, :], psB[:, 0:1024], x_seg[:, c, :]),
                pW + [XSEG], [X1C], tag="resid")
            dma("sync", "x1w", dx1scr[gch * 128:(gch + 1) * 128, :],
                x1t[:, sl, :], [X1C], [B("x1scr")], tag="x1w")

    tap("ufm", ufm, UB_)
    tap("zfm", zfm, ZB)
    tap("lat", lat, B("lat"))
    tap("dbt", dbt, B("dbt"))
    tap("cmt", cmt, B("cmt"))
    tap("cumt", cumt, B("cumt"))
    tap("latm", latm, B("latm"))
    tap("dbbtm", dbbtm, B("dbbtm"))
    tap("cbart", cbart, B("cbart"))
    tap("daq", daq, B("daq"))
    tap("utm", utm, B("utm0"))
    tap("acoef", c_acoef, B("acoef"))
    tap("crow", crow, B("crow"))
    tap("delt", delt, B("delt"))
    tap("hfp", hfp, HB)
    tap("ebuf", ebuf, B("ebuf"))
    tap("ybar", ybar, B("ybar"))
    tap("x1scr", dx1scr, B("x1scr"))
    P.barrier(B)
    mctx.close()

    fctx = ExitStack()
    x1t2 = SB(fctx, "x1t2", [128, 2, DM], F32)
    outf = SB(fctx, "outf", [128, 2, DM], F32)
    ln2fm = SB(fctx, "ln2fm", [128, 8, L], BF16) if RUN_FFN >= 1 else None
    ln2tm = SB(fctx, "ln2tm", [128, 2, DM], BF16)
    logt = SB(fctx, "logt", [8, L], F32)
    lgc = SB(fctx, "lgc", [128, 8], F32)
    g_m1 = SB(fctx, "g_m1", [128, 1], F32)
    g_m2 = SB(fctx, "g_m2", [128, 1], F32)
    g_e1 = SB(fctx, "g_e1", [128, 8], F32)
    g_e2 = SB(fctx, "g_e2", [128, 8], F32)
    g_msk = SB(fctx, "g_msk", [128, 8], F32)
    g_w = SB(fctx, "g_w", [128, 32], F32)
    g_t = SB(fctx, "g_t", [128, 8], F32)
    gatec = SB(fctx, "gatec", [128, 8], BF16)
    gatet = SB(fctx, "gatet", [8, L], BF16)
    gfl = SB(fctx, "gfl", [1, L], BF16)
    if RUN_FFN >= 4:
        grep = SB(fctx, "grep", [128, L], BF16)
        w1fc = SB(fctx, "w1fc", [128, 3, 8, 128], BF16)
        w2dc = SB(fctx, "w2dc", [128, 3, 16, 128], BF16)
        hscl = SB(fctx, "hscl", [128, 16, L], BF16)
        htmp = SB(fctx, "htmp", [128, L], BF16)
        outac = SB(fctx, "outac", [128, 8, L], BF16)
    else:
        grep = w1fc = w2dc = hscl = htmp = outac = None

    if RUN_FFN >= 4:
        P.op("gpsimd", None, reads=[B("x1scr")], writes=[], tag="gfence")
    X1R, LT2, LN2 = B("x1r"), B("ln2tm"), B("ln2fm")
    for tc in range(16 if RUN_FFN >= 1 else 0):
        sl = tc % 2
        dma("sync", "x1r", x1t2[:, sl, :], dx1scr[tc * 128:(tc + 1) * 128, :],
            [B("x1scr")], [X1R], tag="x1r")
        if RUN_LN < 1:
            continue
        vop(lambda v, s_=sl: v.bn_stats(lnst[:, 0:6], x1t2[:, s_, 0:512]),
            [X1R], [LS], tag="bn1f", small=True)
        vop(lambda v, s_=sl: v.bn_stats(lnst[:, 6:12], x1t2[:, s_, 512:1024]),
            [X1R], [LS], tag="bn2f", small=True)
        vop(lambda v: v.bn_aggr(lnmv[:], lnst[:]), [LS], [MV], tag="aggf", small=True)
        vop(lambda v: v.tensor_scalar_add(
            lnrs[:], lnmv[:, 1:2].broadcast_to([128, 8]), EPS),
            [MV], [RS], tag="vepsf", small=True)
        act(lnrs[:], lnrs[:], AF.Ln, [RS], [RS], tag="lnvf", small=True)
        vop(lambda v: v.tensor_scalar_mul(lnrs[:], lnrs[:], -0.5),
            [RS], [RS], tag="lnvmf", small=True)
        act(lnrs[:], lnrs[:], AF.Exp, [RS], [RS], tag="rstdf", small=True)
        if RUN_LN < 2:
            continue
        vop(lambda v, s_=sl: v.tensor_scalar(
            ln2tm[:, s_, :], x1t2[:, s_, :], lnmv[:, 0:1], None,
            ALU.subtract), [X1R, MV], [LT2], tag="ln2sub")
        vop(lambda v, s_=sl: v.tensor_scalar_mul(
            ln2tm[:, s_, :], ln2tm[:, s_, :], lnrs[:, 0:1]),
            [LT2, RS], [LT2], tag="ln2mul")
        for w in range(2 if (RUN_LN >= 3 and tc < RUN_LNT) else 0):
            c0 = w * 512
            pb = pb_A(c0, c0 + 512)
            for j in range(4):
                dc = w * 4 + j
                mm(psA[:, c0 + j * 128:c0 + (j + 1) * 128],
                   ln2tm[:, sl, dc * 128:(dc + 1) * 128], c_i128b[:], pb,
                   [LT2, CB], start=(j == 0), stop=(j == 3), tag="ln2T")
            vop(lambda v, t_=tc, ww=w, cc0=c0: v.tensor_copy(
                ln2fm[:, ww * 4:(ww + 1) * 4, t_ * 128:(t_ + 1) * 128],
                psA[:, cc0:cc0 + 512].rearrange("p (a n) -> p a n", a=4)),
                pb, [LN2], tag="ln2ev")
    pR = pb_A(0, 2048)
    for dc in range(8 if RUN_FFN >= 2 else 0):
        for ts_ in range(4):
            mm(psA[0:8, ts_ * 512:(ts_ + 1) * 512], c_wrS[:, dc, :],
               ln2fm[:, dc, ts_ * 512:(ts_ + 1) * 512], pR, [CB, LN2],
               start=(dc == 0), stop=(dc == 7), tag="rout")
    LOG = B("logt")
    if RUN_FFN >= 2:
        act(logt[:], psA[0:8, 0:2048], AF.Copy, pR, [LOG], tag="routev")
    if RUN_FFN >= 2:
        vop(lambda v: v.tensor_scalar_add(logt[:], logt[:], c_br[:]),
            [LOG, CB], [LOG], tag="routbr")
    GT = B("gatet")
    for tc in range(16 if RUN_FFN >= 3 else 0):
        pb3 = pb_B(1536, 2048)
        mm(psB[0:128, 1536:1544], logt[:, tc * 128:(tc + 1) * 128],
           c_i16f[0:8, 0:8], pb3, [LOG, CB], start=True, stop=True,
           tag="gT")
        LGC = B("lgc")
        vop(lambda v: v.tensor_copy(lgc[:], psB[:, 1536:1544]), pb3, [LGC],
            tag="lgcev", small=True)
        M1_, M2_ = B("g_m1"), B("g_m2")
        E1, E2, MSK = B("g_e1"), B("g_e2"), B("g_msk")
        W_, T_, GC = B("g_w"), B("g_t"), B("gatec")
        vop(lambda v: v.reduce_max(g_m1[:], lgc[:], axis=AXX), [LGC], [M1_],
            tag="gm1", small=True)
        vop(lambda v: v.tensor_scalar(g_e1[:], lgc[:], g_m1[:], None,
                                      ALU.is_equal), [LGC, M1_], [E1],
            tag="ge1", small=True)
        vop(lambda v: v.scalar_tensor_tensor(g_msk[:], g_e1[:], -BIG, lgc[:],
                                             ALU.mult, ALU.add), [E1, LGC],
            [MSK], tag="gmsk", small=True)
        vop(lambda v: v.reduce_max(g_m2[:], g_msk[:], axis=AXX), [MSK],
            [M2_], tag="gm2", small=True)
        vop(lambda v: v.tensor_scalar(g_e2[:], lgc[:], g_m2[:], None,
                                      ALU.is_equal), [LGC, M2_], [E2],
            tag="ge2", small=True)
        vop(lambda v: v.tensor_tensor(
            g_w[:, 0:8], g_m2[:].broadcast_to([128, 8]),
            g_m1[:].broadcast_to([128, 8]), ALU.subtract), [M1_, M2_], [W_],
            tag="gd", small=True)
        act(g_w[:, 8:16], g_w[:, 0:8], AF.Exp, [W_], [W_], scale=-1.0,
            tag="gexp", small=True)
        vop(lambda v: v.tensor_scalar_add(g_w[:, 8:16], g_w[:, 8:16], 1.0),
            [W_], [W_], tag="gp1", small=True)
        vop(lambda v: v.reciprocal(g_w[:, 16:24], g_w[:, 8:16]), [W_], [W_],
            tag="grec", small=True)
        vop(lambda v: v.tensor_scalar(g_w[:, 24:32], g_w[:, 16:24], -1.0,
                                      1.0, ALU.mult, ALU.add), [W_], [W_],
            tag="gw1", small=True)
        vop(lambda v: v.tensor_scalar_mul(g_t[:], g_e2[:], g_w[:, 16:17]),
            [E2, W_], [T_], tag="gt2", small=True)
        vop(lambda v: v.scalar_tensor_tensor(gatec[:], g_e1[:], g_w[:, 24:25],
                                             g_t[:], ALU.mult, ALU.add),
            [E1, W_, T_], [GC], tag="gate", small=True)
        mm(psB[0:8, 1544:1672], gatec[:], c_i128b[:], pb3, [GC, CB],
           start=True, stop=True, tag="gTT")
        vop(lambda v, t_=tc: v.tensor_copy(gatet[:, t_ * 128:(t_ + 1) * 128],
                                           psB[0:8, 1544:1672]), pb3, [GT],
            tag="gatetev")
    tap("logt", logt, LOG)
    tap("gatet", gatet, GT)
    OUT = B("outac")
    GFL, GR, HS, HT = B("gfl"), B("grep"), B("hscl"), B("htmp")
    for e in range(NE if RUN_FFN >= 4 else 0):
        dma("gpsimd", "gfl", gfl[:], gatet[e:e + 1, :], [GT], [GFL],
            tag="gfl")
        pG = pb_A(0, 2048)
        for j in range(4):
            mm(psA[:, j * 512:(j + 1) * 512], c_onesb[:],
               gfl[:, j * 512:(j + 1) * 512], pG, [CB, GFL],
               start=True, stop=True, tag="grepM")
        act(grep[:], psA[:, 0:2048], AF.Copy, pG, [GR], tag="grepev")
        for fc in range(16):
            sl = fc % 3
            WF = B(f"w1fc{sl}")
            dma("gpsimd", "w1", w1fc[:, sl, :, :],
                dw1S[e, fc].rearrange("(a p) n -> p a n", p=128), [], [WF],
                tag="w1d")
            ps, pbank = (psA, pb_A) if fc % 2 == 0 else (psB, pb_B)
            pH = pbank(0, 2048)
            for dc in range(8):
                for ts_ in range(4):
                    mm(ps[:, ts_ * 512:(ts_ + 1) * 512], w1fc[:, sl, dc, :],
                       ln2fm[:, dc, ts_ * 512:(ts_ + 1) * 512], pH, [WF, LN2],
                       start=(dc == 0), stop=(dc == 7), tag="w1mm")
            act(htmp[:], ps[:, 0:2048], AF.Relu, pH + [CB], [HT],
                bias=c_b1h[:, e * 16 + fc:e * 16 + fc + 1], tag="hrelu")
            vop(lambda v, f_=fc: v.tensor_mul(hscl[:, f_, :], htmp[:],
                                              grep[:]), [HT, GR], [HS],
                tag="hscale")
        for dc in range(8):
            sl = dc % 3
            WD = B(f"w2dc{sl}")
            dma("gpsimd", "w2", w2dc[:, sl, :, :],
                dw2S[e, dc].rearrange("(a p) n -> p a n", p=128), [], [WD],
                tag="w2d")
            ps, pbank = (psA, pb_A) if dc % 2 == 0 else (psB, pb_B)
            pO = pbank(0, 2048)
            if e == 0:
                for ts_ in range(4):
                    mm(ps[:, ts_ * 512:(ts_ + 1) * 512],
                       c_b2[:, dc * 128:(dc + 1) * 128],
                       gatet[:, ts_ * 512:(ts_ + 1) * 512], pO, [CB, GT],
                       start=True, stop=False, tag="b2mm")
            for fc in range(16):
                for ts_ in range(4):
                    mm(ps[:, ts_ * 512:(ts_ + 1) * 512], w2dc[:, sl, fc, :],
                       hscl[:, fc, ts_ * 512:(ts_ + 1) * 512], pO, [WD, HS],
                       start=(e != 0 and fc == 0), stop=(fc == 15),
                       tag="w2mm")
            if e == 0:
                act(outac[:, dc, :], ps[:, 0:2048], AF.Copy, pO, [OUT],
                    tag="oev")
            else:
                vop(lambda v, d=dc, p=ps: v.tensor_add(
                    outac[:, d, :], outac[:, d, :], p[:, 0:2048]),
                    pO + [OUT], [OUT], tag="oadd")
    tap("hscl", hscl, HS)
    tap("outac", outac, OUT)
    OUTS, FO = B("outstore"), B("outf")
    for tc in range(16 if RUN_FFN >= 5 else 0):
        sl = tc % 2
        dma("sync", "x1r2", x1t2[:, sl, :], dx1scr[tc * 128:(tc + 1) * 128, :],
            [X1R, FO], [X1R], tag="x1rr")
        ps, pbank = (psA, pb_A) if tc % 2 == 0 else (psB, pb_B)
        pF = pbank(0, 1024)
        for dc in range(8):
            mm(ps[:, dc * 128:(dc + 1) * 128],
               outac[:, dc, tc * 128:(tc + 1) * 128], c_i128b[:], pF,
               [OUT, CB], start=(dc % 4 == 0), stop=(dc % 4 == 3),
               tag="foutT")
        vop(lambda v, s_=sl, p=ps: v.tensor_add(
            outf[:, s_, :], p[:, 0:1024], x1t2[:, s_, :]), pF + [X1R], [FO],
            tag="fadd")
        dma("sync", "outw", dout[tc * 128:(tc + 1) * 128, :], outf[:, sl, :],
            [FO], [OUTS], tag="outw")
    P.op("sync", None, reads=[OUTS], writes=[], tag="fence")
    if dtaps:
        P.op("sync", None, reads=[B("tapout")], writes=[], tag="tapfence")

    P.finalize(nc, gctx)
    with nc.Block() as block:
        block.tensor(lambda t: P.emit_engine("tensor", t))
        block.vector(lambda v: P.emit_engine("vector", v))
        block.scalar(lambda s: P.emit_engine("scalar", s))
        block.gpsimd(lambda g: P.emit_engine("gpsimd", g))
        block.sync(lambda q: P.emit_engine("sync", q))
    fctx.close()
    gctx.close()
    return nc


_tap_shapes = {
    "ufm": ([128, 16, SEG], BF16),
    "lat": ([NS, SEG], F32),
    "dbt": ([NS, SEG], F32),
    "cmt": ([NS, SEG], F32),
    "cumt": ([NS, 2, 128], F32),
    "latm": ([128, 2, NS], F32),
    "dbbtm": ([128, 2, NS], BF16),
    "cbart": ([NS, 2, 128], BF16),
    "daq": ([NS, 2, 8], F32),
    "utm": ([128, 2, DI], BF16),
    "acoef": ([NS, 8], F32),
    "crow": ([128, DI], BF16),
    "zfm": ([128, 16, SEG], BF16),
    "delt": ([NS, SEG], F32),
    "hfp": ([NS, DI], F32),
    "ebuf": ([128, DI], BF16),
    "ybar": ([128, 16, 128], BF16),
    "x1scr": ([L, DM], F32),
    "logt": ([8, L], F32),
    "gatet": ([8, L], BF16),
    "hscl": ([128, 16, L], BF16),
    "outac": ([128, 8, L], BF16),
}


def kernel(**inputs):
    x_full = np.asarray(inputs["L0_attn_x"], np.float32)
    W = host_prep(inputs)
    taps = [(nm, *(_tap_shapes[nm])) for nm in TAPS]
    nc = build_program(taps=taps)
    in_maps = []
    for c in range(8):
        m = dict(x=np.ascontiguousarray(x_full[c]))
        m.update(W)
        in_maps.append(m)
    res = run_bass_kernel_spmd(nc, in_maps, core_ids=list(range(8)))
    out = np.stack([np.asarray(r["out"], np.float32) for r in res.results])
    if TAPS:
        kernel.tap_results = [{nm: np.asarray(r["tap_" + nm]) for nm in TAPS}
                              for r in res.results]
    return out


# version 05
# speedup vs baseline: 1.0937x; 1.0132x over previous
import numpy as np
import ml_dtypes
from contextlib import ExitStack

import concourse.bass as bass
import concourse.mybir as mybir
from concourse.bass_utils import run_bass_kernel_spmd

F32 = mybir.dt.float32
BF16 = mybir.dt.bfloat16
AF = mybir.ActivationFunctionType
ALU = mybir.AluOpType
AXX = mybir.AxisListType.X

L = 2048
DM = 1024
DI = 2048
NS = 16
SEG = 512
NSEG = L // SEG
NCH = SEG // 128
NE = 8
H = 1024
FF = 2048
BIG = 1.0e30
EPS = 1.0e-5

TAPS = []
RUN_NSEG = NSEG
RUN_FFN = 5
RUN_WIN = True
RUN_CHUNKS = NCH
RUN_LN = 3
RUN_LNT = 16

ENG = ("tensor", "vector", "scalar", "gpsimd", "sync")
STREAM_MOD = {"const": 1, "wout": 1, "xin": 4, "outw": 1, "w1": 3,
              "w2": 3, "cfl": 1, "rfl": 1, "gfl": 1, "tap": 1}


class Buf:
    __slots__ = ("name", "writers", "readers")

    def __init__(self, name):
        self.name = name
        self.writers = {}
        self.readers = {}


class Plan:
    def __init__(self):
        self.ops = []
        self.per_eng = {e: [] for e in ENG}
        self.stream_counts = {}

    def op(self, eng, fn, reads=(), writes=(), dma_stream=None, tag="",
           small=False):
        gid = len(self.ops)
        if dma_stream is not None:
            k = self.stream_counts.get(dma_stream, 0)
            self.stream_counts[dma_stream] = k + 1
            chan = ("D", dma_stream, k % STREAM_MOD.get(dma_stream, 2))
        else:
            chan = eng
        deps = set()

        def dep_on(gid2):
            p = self.ops[gid2]
            if p["dma"] is None and p["eng"] == eng:
                return
            deps.add(gid2)

        for b in reads:
            for g2 in b.writers.values():
                dep_on(g2)
        for b in writes:
            for g2 in b.writers.values():
                dep_on(g2)
            for g2 in b.readers.values():
                dep_on(g2)
        o = dict(gid=gid, eng=eng, fn=fn, deps=deps, dma=dma_stream,
                 chan=chan, tag=tag, need_inc=False, sem=None, val=None,
                 small=small)
        self.ops.append(o)
        self.per_eng[eng].append(o)
        for b in reads:
            b.readers[chan] = gid
        for b in writes:
            b.writers[chan] = gid
        return gid

    def barrier(self, B):
        bb = B(f"__bar{len(self.ops)}")
        for e in ENG:
            self.op(e, lambda eng: eng.nop(), reads=[], writes=[bb],
                    tag="bar_w")
        for e in ENG:
            self.op(e, None, reads=[bb], writes=[], tag="bar_r")

    def finalize(self, nc, ctx):
        for o in self.ops:
            if o["eng"] == "tensor" and o["dma"] is None and o["fn"] is not None:
                o["need_inc"] = True
            for d in o["deps"]:
                p = self.ops[d]
                assert p["fn"] is not None or p["dma"] is not None, (
                    "dep on no-op " + p["tag"])
                p["need_inc"] = True
        comp_sems = {e: ctx.enter_context(nc.semaphore(f"sem_{e}"))
                     for e in ENG}
        stream_sems = {}
        counters = {e: 0 for e in ENG}
        for o in self.ops:
            if o["dma"] is not None:
                key = o["chan"]
                if key not in stream_sems:
                    stream_sems[key] = [ctx.enter_context(nc.semaphore(
                        f"dma_{key[1]}_{key[2]}")), 0]
                stream_sems[key][1] += 16
                o["need_inc"] = True
                o["sem"] = stream_sems[key][0]
                o["val"] = stream_sems[key][1]
                self.stream_final = getattr(self, "stream_final", {})
                self.stream_final[key] = stream_sems[key][1]
            elif o["need_inc"]:
                counters[o["eng"]] += 1
                o["sem"] = comp_sems[o["eng"]]
                o["val"] = counters[o["eng"]]

    def emit_engine(self, ename, eng):
        waited = {}
        for o in self.per_eng[ename]:
            pass
        for o in self.per_eng[ename]:
            for d in sorted(o["deps"]):
                p = self.ops[d]
                sem, val = p["sem"], p["val"]
                assert sem is not None, f"{o['tag']} dep {p['tag']} no sem"
                if waited.get(sem.name, -1) < val:
                    eng.wait_ge(sem, val)
                    waited[sem.name] = val
            if o["fn"] is None:
                continue
            inst = o["fn"](eng)
            if o.get("small"):
                inst = eng.drain()
            if o["need_inc"]:
                inst.then_inc(o["sem"], 16 if o["dma"] is not None else 1)
        for o in self.per_eng[ename]:
            if o["dma"] is not None:
                sem, val = o["sem"], self.stream_final[o["chan"]]
                if waited.get(sem.name, -1) < val:
                    eng.wait_ge(sem, val)
                    waited[sem.name] = val


def host_prep(inp):
    f = np.float32
    ac = np.ascontiguousarray

    def g(n):
        return np.asarray(inp[n], f)
    W_inT = ac(g("L0_attn_W_in").T)
    winS = ac(W_inT.reshape(DM, 16, 256).transpose(1, 0, 2))
    woutT = ac(g("L0_attn_W_out").T)
    wxdtT = ac(np.concatenate([g("L0_attn_W_x"), g("L0_attn_W_dt")], 0).T)
    convw = ac(g("L0_attn_conv_w").reshape(DI, 4)
               .reshape(16, 128, 4).transpose(1, 0, 2).reshape(128, 64))
    convb = ac(g("L0_attn_conv_b").reshape(16, 128).T)
    dvec = ac(g("L0_attn_D").reshape(16, 128).T)
    alog = ac(g("L0_attn_A_log").reshape(NS, 1))
    bdt = ac(g("L0_attn_b_dt").reshape(NS, 1))
    w1S = ac(g("L0_ffn_W1").reshape(NE, H, 16, 128).transpose(0, 2, 1, 3))
    w2S = ac(g("L0_ffn_W2").reshape(NE, FF, 8, 128).transpose(0, 2, 1, 3))
    b1h = ac(g("L0_ffn_b1").reshape(NE, 16, 128)
             .transpose(2, 0, 1).reshape(128, NE * 16))
    b2 = ac(g("L0_ffn_b2"))
    wrS = ac(g("L0_ffn_Wr").reshape(8, 128, 8).transpose(1, 0, 2))
    br = ac(g("L0_ffn_br").reshape(NE, 1))
    i128 = np.eye(128, dtype=f)
    i16 = np.eye(16, dtype=f)
    tau = np.arange(128)
    m1 = (tau[:, None] <= tau[None, :]).astype(f)
    m2 = (tau[:, None] > tau[None, :]).astype(f)
    ub = (-BIG * (tau[None, :] > tau[:, None])).astype(f)
    ones128 = np.ones((1, 128), dtype=f)
    maskrep = np.tile((tau[:, None] <= tau[None, :]).astype(f), (1, 16))
    return dict(maskrep=maskrep, winS=winS, woutT=woutT, wxdtT=wxdtT, convw=convw,
                convb=convb, dvec=dvec, alog=alog, bdt=bdt, w1S=w1S,
                w2S=w2S, b1h=b1h, b2=b2, wrS=wrS, br=br, i128=i128,
                i16=i16, m1=m1, m2=m2, ub=ub, ones128=ones128)


def build_program(taps=()):
    nc = bass.Bass("TRN2", target_bir_lowering=False,
                   detect_race_conditions=False)
    P = Plan()
    bufs = {}

    def B(name):
        if name not in bufs:
            bufs[name] = Buf(name)
        return bufs[name]

    dx = nc.dram_tensor("x", [L, DM], F32, kind="ExternalInput").ap()
    dwinS = nc.dram_tensor("winS", [16, DM, 256], F32, kind="ExternalInput").ap()
    dwoutT = nc.dram_tensor("woutT", [DI, DM], F32, kind="ExternalInput").ap()
    dwxdtT = nc.dram_tensor("wxdtT", [DI, 48], F32, kind="ExternalInput").ap()
    dconvw = nc.dram_tensor("convw", [128, 64], F32, kind="ExternalInput").ap()
    dconvb = nc.dram_tensor("convb", [128, 16], F32, kind="ExternalInput").ap()
    ddvec = nc.dram_tensor("dvec", [128, 16], F32, kind="ExternalInput").ap()
    dalog = nc.dram_tensor("alog", [NS, 1], F32, kind="ExternalInput").ap()
    dbdt = nc.dram_tensor("bdt", [NS, 1], F32, kind="ExternalInput").ap()
    dw1S = nc.dram_tensor("w1S", [NE, 16, H, 128], F32, kind="ExternalInput").ap()
    dw2S = nc.dram_tensor("w2S", [NE, 8, FF, 128], F32, kind="ExternalInput").ap()
    db1h = nc.dram_tensor("b1h", [128, NE * 16], F32, kind="ExternalInput").ap()
    db2 = nc.dram_tensor("b2", [NE, H], F32, kind="ExternalInput").ap()
    dwrS = nc.dram_tensor("wrS", [128, 8, 8], F32, kind="ExternalInput").ap()
    dbr = nc.dram_tensor("br", [NE, 1], F32, kind="ExternalInput").ap()
    di128 = nc.dram_tensor("i128", [128, 128], F32, kind="ExternalInput").ap()
    di16 = nc.dram_tensor("i16", [16, 16], F32, kind="ExternalInput").ap()
    dm1 = nc.dram_tensor("m1", [128, 128], F32, kind="ExternalInput").ap()
    dm2 = nc.dram_tensor("m2", [128, 128], F32, kind="ExternalInput").ap()
    dub = nc.dram_tensor("ub", [128, 128], F32, kind="ExternalInput").ap()
    dones = nc.dram_tensor("ones128", [1, 128], F32, kind="ExternalInput").ap()
    dmaskrep = nc.dram_tensor("maskrep", [128, 2048], F32, kind="ExternalInput").ap()
    dout = nc.dram_tensor("out", [L, DM], F32, kind="ExternalOutput").ap()
    dx1scr = nc.dram_tensor("x1scr", [L, DM], F32, kind="Internal").ap()
    dtaps = {}
    for (nm, shape, dt) in taps:
        dtaps[nm] = nc.dram_tensor("tap_" + nm, shape, dt,
                                   kind="ExternalOutput").ap()

    def tap(nm, tile, rbuf):
        if nm in dtaps:
            ap_ = tile if isinstance(tile, bass.AP) else tile[:]
            dma("sync", "tap", dtaps[nm], ap_, [rbuf], [B("tapout")],
                tag=f"tap_{nm}")

    gctx = ExitStack()

    def SB(c, n, s, d):
        return c.enter_context(nc.sbuf_tensor(n, s, d))

    def mm(out_ap, lhsT_ap, rhs_ap, pbufs, rbufs, start, stop, tag=""):
        P.op("tensor",
             lambda t, o=out_ap, l=lhsT_ap, r=rhs_ap, s0=start, s1=stop:
                 t.matmul(o, l, r, start=s0, stop=s1),
             reads=rbufs, writes=pbufs, tag=tag)

    def act(out_ap, in_ap, func, rbufs, wbufs, bias=0.0, scale=1.0, tag="",
            small=False):
        P.op("scalar",
             lambda s, o=out_ap, i=in_ap, f=func, b=bias, sc=scale:
                 s.activation(o, i, f, bias=b, scale=sc),
             reads=rbufs, writes=wbufs, tag=tag, small=small)

    def vop(fn, rbufs, wbufs, tag="", small=False):
        P.op("vector", fn, reads=rbufs, writes=wbufs, tag=tag, small=small)

    def dma(eng, stream, out_ap, in_ap, rbufs, wbufs, tag=""):
        P.op(eng, lambda e, o=out_ap, i=in_ap: e.dma_start(o, i),
             reads=rbufs, writes=wbufs, dma_stream=stream, tag=tag)

    def pb_A(lo, hi):
        return [B(f"psA_b{i}") for i in range(lo // 512, (hi - 1) // 512 + 1)]

    def pb_B(lo, hi):
        return [B(f"psB_b{i}") for i in range(lo // 512, (hi - 1) // 512 + 1)]

    c_i128b = SB(gctx, "c_i128b", [128, 128], BF16)
    c_i128f = SB(gctx, "c_i128f", [128, 128], F32)
    c_i16f = SB(gctx, "c_i16f", [16, 16], F32)
    c_m1 = SB(gctx, "c_m1", [128, 128], F32)
    c_m2 = SB(gctx, "c_m2", [128, 128], F32)
    c_ub = SB(gctx, "c_ub", [128, 128], F32)
    c_onesf = SB(gctx, "c_onesf", [1, 128], F32)
    c_onesb = SB(gctx, "c_onesb", [1, 128], BF16)
    c_mrep = SB(gctx, "c_mrep", [128, 2048], BF16)
    c_convw = SB(gctx, "c_convw", [128, 64], F32)
    c_convb = SB(gctx, "c_convb", [128, 16], F32)
    c_dvec = SB(gctx, "c_dvec", [128, 16], F32)
    c_alog = SB(gctx, "c_alog", [NS, 1], F32)
    c_acoef = SB(gctx, "c_acoef", [NS, 8], F32)
    c_bdt = SB(gctx, "c_bdt", [NS, 1], F32)
    c_wxdt = SB(gctx, "c_wxdt", [128, 16, 48], BF16)
    c_wrS = SB(gctx, "c_wrS", [128, 8, 8], BF16)
    c_br = SB(gctx, "c_br", [8, 1], F32)
    c_b1h = SB(gctx, "c_b1h", [128, NE * 16], F32)
    c_b2 = SB(gctx, "c_b2", [NE, H], BF16)
    lnst = SB(gctx, "lnst", [128, 12], F32)
    lnmv = SB(gctx, "lnmv", [128, 2], F32)
    lnrs = SB(gctx, "lnrs", [128, 8], F32)
    psA = gctx.enter_context(nc.psum_tensor("psA", [128, 2048], F32))
    psB = gctx.enter_context(nc.psum_tensor("psB", [128, 2048], F32))

    CB = B("consts")
    for dst, src in [
        (c_i128b[:], di128), (c_i128f[:], di128), (c_i16f[:], di16),
        (c_m1[:], dm1), (c_m2[:], dm2), (c_ub[:], dub),
        (c_onesf[:], dones), (c_onesb[:], dones), (c_mrep[:], dmaskrep),
        (c_convw[:], dconvw), (c_convb[:], dconvb), (c_dvec[:], ddvec),
        (c_alog[:], dalog), (c_bdt[:], dbdt),
        (c_wxdt[:], dwxdtT.rearrange("(a p) k -> p a k", p=128)),
        (c_wrS[:], dwrS), (c_br[:], dbr), (c_b1h[:], db1h), (c_b2[:], db2),
    ]:
        dma("gpsimd", "const", dst, src, [], [CB], tag="const")
    ACF = B("acoef")
    act(lnrs[0:1, 0:8], lnrs[0:1, 0:8], AF.Exp, [], [B("lnrs")], tag="warm_e", small=True)
    act(lnrs[0:1, 0:8], lnrs[0:1, 0:8], AF.Ln, [B("lnrs")], [B("lnrs")],
        tag="warm_l", small=True)
    act(lnrs[0:1, 0:8], lnrs[0:1, 0:8], AF.Silu, [B("lnrs")], [B("lnrs")],
        tag="warm_s", small=True)
    act(lnrs[0:1, 0:8], lnrs[0:1, 0:8], AF.Ln, [B("lnrs")], [B("lnrs")],
        tag="warm_l2", small=True)
    act(c_acoef[:], c_alog[:].broadcast_to([NS, 8]), AF.Exp, [CB], [ACF],
        tag="acoef_exp", small=True)
    vop(lambda v: v.tensor_scalar_mul(c_acoef[:], c_acoef[:], -1.0),
        [ACF], [ACF], tag="acoef_neg", small=True)

    mctx = ExitStack()
    x1t = SB(mctx, "x1t", [128, 2, DM], F32)
    woutT = SB(mctx, "sb_woutT", [128, 16, DM], BF16)
    winSl = SB(mctx, "winSl", [128, 2, 8, 256], BF16)
    x_seg = SB(mctx, "x_seg", [128, NCH, DM], F32)
    ln1tm = SB(mctx, "ln1tm", [128, NCH, DM], BF16)
    ln1fm = SB(mctx, "ln1fm", [128, 8, SEG], BF16)
    xi = SB(mctx, "xi", [128, 16, SEG + 3], BF16)
    xi_sv = SB(mctx, "xi_sv", [128, 16, 3], BF16)
    zfm = SB(mctx, "zfm", [128, 16, SEG], BF16)
    ufm = SB(mctx, "ufm", [128, 16, SEG], BF16)
    t1c = SB(mctx, "t1c", [128, SEG], F32)
    utm = SB(mctx, "utm", [128, 2, DI], BF16)
    crow = SB(mctx, "crow", [128, DI], BF16)
    ebuf = SB(mctx, "ebuf", [128, DI], BF16)
    t1e = SB(mctx, "t1e", [128, DI], BF16)
    ybar = SB(mctx, "ybar", [128, 16, 128], BF16)
    bmt = SB(mctx, "bmt", [NS, SEG], F32)
    cmt = SB(mctx, "cmt", [NS, SEG], F32)
    cmtb = SB(mctx, "cmtb", [NS, SEG], BF16)
    delt = SB(mctx, "delt", [NS, SEG], F32)
    lat = SB(mctx, "lat", [NS, SEG], F32)
    dbt = SB(mctx, "dbt", [NS, SEG], F32)
    latm = SB(mctx, "latm", [128, 2, NS], F32)
    dbtm = SB(mctx, "dbtm", [128, 2, NS], F32)
    cumtm = SB(mctx, "cumtm", [128, 2, NS], F32)
    negtm = SB(mctx, "negtm", [128, 2, NS], F32)
    revtm = SB(mctx, "revtm", [128, 2, NS], F32)
    erev = SB(mctx, "erev", [128, 2, NS], F32)
    dbbtm = SB(mctx, "dbbtm", [128, 2, NS], BF16)
    cumt = SB(mctx, "cumt", [NS, 2, 128], F32)
    ect = SB(mctx, "ect", [NS, 2, 128], F32)
    cbart = SB(mctx, "cbart", [NS, 2, 128], BF16)
    daq = SB(mctx, "daq", [NS, 2, 8], F32)
    cumfl = SB(mctx, "cumfl", [1, DI], F32)
    rowfl = SB(mctx, "rowfl", [1, DI], BF16)
    hfp = SB(mctx, "hfp", [NS, DI], F32)
    hbf = SB(mctx, "hbf", [NS, DI], BF16)

    WOUT = B("woutT")
    for ec in range(16):
        dma("gpsimd", "wout", woutT[:, ec, :],
            dwoutT[ec * 128:(ec + 1) * 128, :], [], [WOUT], tag="woutd")
    HB, HBB = B("hfp"), B("hbf")
    vop(lambda v: v.memset(hfp[:], 0.0), [], [HB], tag="h0")
    vop(lambda v: v.memset(hbf[:], 0.0), [], [HBB], tag="hbf0")
    XIB = B("xi")
    vop(lambda v: v.memset(xi[:], 0.0), [], [XIB], tag="xipad0")

    T1 = B("t1c")

    def emit_conv(ic):
        vop(lambda v, i=ic: v.tensor_scalar_mul(
            t1c[:], xi[:, i, 0:SEG], c_convw[:, 4 * i:4 * i + 1]),
            [XIB, CB], [T1], tag="cv0")
        for k in range(1, 4):
            vop(lambda v, i=ic, kk=k: v.scalar_tensor_tensor(
                t1c[:], xi[:, i, kk:SEG + kk],
                c_convw[:, 4 * i + kk:4 * i + kk + 1], t1c[:],
                ALU.mult, ALU.add), [XIB, CB, T1], [T1], tag="cvk")
        act(ufm[:, ic, :], t1c[:], AF.Silu, [T1, CB], [UB_],
            bias=c_convb[:, ic:ic + 1], tag="silu_u")

    def emit_zsilu(ic):
        act(zfm[:, ic, :], zfm[:, ic, :], AF.Silu, [ZB], [ZB], tag="silu_z")

    XSEG, LT, LF = B("x_seg"), B("ln1tm"), B("ln1fm")
    ZB, UB_, UD = B("zfm"), B("ufm"), B("xi")
    LS, MV, RS = B("lnst"), B("lnmv"), B("lnrs")

    for s in range(RUN_NSEG):
        sbase = s * SEG
        for ch in range(NCH):
            dma("sync", "xin", x_seg[:, ch, :],
                dx[sbase + ch * 128: sbase + (ch + 1) * 128, :],
                [], [XSEG], tag=f"xin{s}_{ch}")
        for ch in range(NCH):
            vop(lambda v, c=ch: v.bn_stats(lnst[:, 0:6], x_seg[:, c, 0:512]),
                [XSEG], [LS], tag="bn1", small=True)
            vop(lambda v, c=ch: v.bn_stats(lnst[:, 6:12],
                                           x_seg[:, c, 512:1024]),
                [XSEG], [LS], tag="bn2", small=True)
            vop(lambda v: v.bn_aggr(lnmv[:], lnst[:]), [LS], [MV], tag="agg",
                small=True)
            vop(lambda v, c=ch: v.tensor_scalar_add(
                lnrs[:, c:c + 1], lnmv[:, 1:2], EPS),
                [MV], [RS], tag="veps", small=True)
            vop(lambda v, c=ch: v.tensor_scalar(
                ln1tm[:, c, :], x_seg[:, c, :], lnmv[:, 0:1], None,
                ALU.subtract), [XSEG, MV], [LT], tag="ln1sub")
        act(lnrs[:, 0:4], lnrs[:, 0:4], AF.Ln, [RS], [RS], tag="lnv",
            small=True)
        vop(lambda v: v.tensor_scalar_mul(lnrs[:, 0:4], lnrs[:, 0:4], -0.5),
            [RS], [RS], tag="lnvm", small=True)
        act(lnrs[:, 0:4], lnrs[:, 0:4], AF.Exp, [RS], [RS], tag="rstd",
            small=True)
        for ch in range(NCH):
            vop(lambda v, c=ch: v.tensor_scalar_mul(
                ln1tm[:, c, :], ln1tm[:, c, :], lnrs[:, c:c + 1]),
                [LT, RS], [LT], tag="ln1mul")
            for w in range(2):
                c0 = w * 512
                pb = pb_A(c0, c0 + 512)
                for j in range(4):
                    dc = w * 4 + j
                    mm(psA[:, c0 + j * 128:c0 + (j + 1) * 128],
                       ln1tm[:, ch, dc * 128:(dc + 1) * 128], c_i128b[:],
                       pb, [LT, CB], start=(j == 0), stop=(j == 3),
                       tag="ln1T")
                vop(lambda v, c=ch, ww=w, cc0=c0: v.tensor_copy(
                    ln1fm[:, ww * 4:(ww + 1) * 4, c * 128:(c + 1) * 128],
                    psA[:, cc0:cc0 + 512].rearrange("p (a n) -> p a n", a=4)),
                    pb, [LF], tag="ln1ev")
        for g in range(16 if RUN_WIN else 0):
            sl = g % 2
            WS = B(f"winSl{sl}")
            dma("gpsimd", "winS", winSl[:, sl, :, :],
                dwinS[g].rearrange("(a p) n -> p a n", p=128),
                [], [WS], tag=f"winS{s}_{g}")
            b0 = sl * 2
            pb0 = pb_A(b0 * 512, b0 * 512 + 512)
            pb1 = pb_A((b0 + 1) * 512, (b0 + 1) * 512 + 512)
            for dc in range(8):
                mm(psA[:, b0 * 512:(b0 + 1) * 512], winSl[:, sl, dc, 0:128],
                   ln1fm[:, dc, :], pb0, [WS, LF],
                   start=(dc == 0), stop=(dc == 7), tag="xzA")
                mm(psA[:, (b0 + 1) * 512:(b0 + 2) * 512],
                   winSl[:, sl, dc, 128:256], ln1fm[:, dc, :], pb1, [WS, LF],
                   start=(dc == 0), stop=(dc == 7), tag="xzB")
            if s > 0 and g == 0:
                vop(lambda v: v.tensor_copy(xi[:, :, 0:3], xi_sv[:]),
                    [B("xi_sv")], [XIB], tag="xipad", small=True)
            if g < 8:
                act(xi[:, 2 * g, 3:SEG + 3], psA[:, b0 * 512:(b0 + 1) * 512],
                    AF.Copy, pb0, [XIB], tag="evxi")
                act(xi[:, 2 * g + 1, 3:SEG + 3],
                    psA[:, (b0 + 1) * 512:(b0 + 2) * 512], AF.Copy, pb1,
                    [XIB], tag="evxi")
            else:
                act(zfm[:, 2 * (g - 8), :], psA[:, b0 * 512:(b0 + 1) * 512],
                    AF.Copy, pb0, [ZB], tag="evz")
                act(zfm[:, 2 * (g - 8) + 1, :],
                    psA[:, (b0 + 1) * 512:(b0 + 2) * 512], AF.Copy, pb1,
                    [ZB], tag="evz")
            if RUN_WIN and 1 <= g <= 8:
                emit_conv(2 * (g - 1))
                emit_conv(2 * (g - 1) + 1)
            if RUN_WIN and g >= 9:
                emit_zsilu(2 * (g - 9))
                emit_zsilu(2 * (g - 9) + 1)
        if RUN_WIN:
            emit_zsilu(14)
            emit_zsilu(15)
            vop(lambda v: v.tensor_copy(xi_sv[:], xi[:, :, SEG:SEG + 3]),
                [XIB], [B("xi_sv")], tag="xisave", small=True)

        for ic in range(16 if RUN_WIN else 0):
            vop(lambda v, i=ic: v.tensor_scalar_mul(
                xi[:, i, 3:SEG + 3], ufm[:, i, :], c_dvec[:, i:i + 1]),
                [UB_, CB], [UD], tag="uD")
        pWx = pb_A(0, 1536)
        for ec in range(16 if RUN_WIN else 0):
            for j in range(3):
                mm(psA[0:16, j * 512:(j + 1) * 512],
                   c_wxdt[:, ec, j * 16:(j + 1) * 16], ufm[:, ec, :], pWx,
                   [CB, UB_], start=(ec == 0), stop=(ec == 15), tag="wxdt")
        BM, CM, CMB = B("bmt"), B("cmt"), B("cmtb")
        DL, LA, DB = B("delt"), B("lat"), B("dbt")
        if not RUN_WIN:
            continue
        vop(lambda v: v.tensor_copy(bmt[:], psA[0:16, 0:512]), pWx, [BM],
            tag="evbm")
        vop(lambda v: v.tensor_copy(cmt[:], psA[0:16, 512:1024]), pWx, [CM],
            tag="evcm")
        vop(lambda v: v.tensor_copy(cmtb[:], cmt[:]), [CM], [CMB], tag="cmtb")
        act(delt[:], psA[0:16, 1024:1536], AF.Exp, pWx + [CB], [DL],
            bias=c_bdt[:], tag="d_exp")
        vop(lambda v: v.tensor_scalar_add(delt[:], delt[:], 1.0), [DL], [DL],
            tag="d_p1")
        act(delt[:], delt[:], AF.Ln, [DL], [DL], tag="d_ln")
        vop(lambda v: v.tensor_scalar_mul(lat[:], delt[:], c_acoef[:, 0:1]),
            [DL, ACF], [LA], tag="la")
        vop(lambda v: v.tensor_mul(dbt[:], delt[:], bmt[:]), [DL, BM], [DB],
            tag="db")

        for cc in range(min(NCH, RUN_CHUNKS)):
            ci = cc % 2
            ut = cc % 2
            gch = s * NCH + cc
            pb3 = pb_B(1536, 2048)
            mm(psB[:, 1536:1552], lat[:, cc * 128:(cc + 1) * 128], c_i16f[:],
               pb3, [LA, CB], start=True, stop=True, tag="latmM")
            mm(psB[:, 1552:1568], dbt[:, cc * 128:(cc + 1) * 128], c_i16f[:],
               pb3, [DB, CB], start=True, stop=True, tag="dbtmM")
            LTM, DTM = B("latm"), B("dbtm")
            vop(lambda v, i=ci: v.tensor_copy(latm[:, i, :], psB[:, 1536:1552]),
                pb3, [LTM], tag="latmev", small=True)
            vop(lambda v, i=ci: v.tensor_copy(dbtm[:, i, :], psB[:, 1552:1568]),
                pb3, [DTM], tag="dbtmev", small=True)
            mm(psB[:, 1568:1584], c_m1[:], latm[:, ci, :], pb3, [CB, LTM],
               start=True, stop=True, tag="cumM")
            mm(psB[:, 1584:1600], c_m2[:], latm[:, ci, :], pb3, [CB, LTM],
               start=True, stop=True, tag="revM")
            mm(psB[0:16, 1600:1728], latm[:, ci, :], c_m1[:], pb3, [LTM, CB],
               start=True, stop=True, tag="cumTM")
            CUM, REV, CT = B("cumtm"), B("revtm"), B("cumt")
            vop(lambda v, i=ci: v.tensor_copy(cumtm[:, i, :], psB[:, 1568:1584]),
                pb3, [CUM], tag="cumev", small=True)
            NEG = B("negtm")
            vop(lambda v, i=ci: v.tensor_scalar_mul(
                negtm[:, i, :], cumtm[:, i, :], -1.0), [CUM], [NEG],
                tag="negtm", small=True)
            vop(lambda v, i=ci: v.tensor_copy(revtm[:, i, :], psB[:, 1584:1600]),
                pb3, [REV], tag="revev", small=True)
            vop(lambda v, i=ci: v.tensor_copy(cumt[:, i, :],
                                              psB[0:16, 1600:1728]),
                pb3, [CT], tag="cumtev")
            ER, ECT, DAQ = B("erev"), B("ect"), B("daq")
            act(erev[:, ci, :], revtm[:, ci, :], AF.Exp, [REV], [ER], tag="er")
            act(ect[:, ci, :], cumt[:, ci, :], AF.Exp, [CT], [ECT], tag="ec")
            act(daq[:, ci, :], cumt[:, ci, 127:128].broadcast_to([NS, 8]),
                AF.Exp, [CT], [DAQ], tag="daq", small=True)
            DBB, CBT = B("dbbtm"), B("cbart")
            vop(lambda v, i=ci: v.tensor_mul(
                dbbtm[:, i, :], dbtm[:, i, :], erev[:, i, :]), [DTM, ER],
                [DBB], tag="dbb", small=True)
            vop(lambda v, i=ci, c=cc: v.tensor_mul(
                cbart[:, i, :], cmt[:, c * 128:(c + 1) * 128], ect[:, i, :]),
                [CM, ECT], [CBT], tag="cbart")
            CFL, RFL = B("cumfl"), B("rowfl")
            dma("gpsimd", "cfl", cumfl[:], cumt[:, ci, :], [CT], [CFL],
                tag="cfl")
            dma("gpsimd", "rfl", rowfl[:], cmtb[:, cc * 128:(cc + 1) * 128],
                [CMB], [RFL], tag="rfl")
            UT = B(f"utm{ut}")
            for w in range(4):
                c0 = 1024 + (w % 2) * 512
                pbu = pb_B(c0, c0 + 512)
                for j in range(4):
                    ic = w * 4 + j
                    mm(psB[:, c0 + j * 128:c0 + (j + 1) * 128],
                       ufm[:, ic, cc * 128:(cc + 1) * 128], c_i128b[:], pbu,
                       [UB_, CB], start=(j == 0), stop=(j == 3), tag="uT")
                vop(lambda v, u=ut, ww=w, cc0=c0: v.tensor_copy(
                    utm[:, u, ww * 512:(ww + 1) * 512], psB[:, cc0:cc0 + 512]),
                    pbu, [UT], tag="uTev")
            pD = pb_A(0, 2048)
            for j in range(4):
                mm(psA[:, j * 512:(j + 1) * 512], c_onesf[:],
                   cumfl[:, j * 512:(j + 1) * 512], pD, [CB, CFL],
                   start=True, stop=True, tag="diff")
            EB = B("ebuf")
            for n in range(NS):
                act(ebuf[:, n * 128:(n + 1) * 128],
                    psA[:, n * 128:(n + 1) * 128], AF.Relu, pD + [CUM], [EB],
                    bias=cumtm[:, ci, n:n + 1], scale=-1.0, tag="erelu")
            for n in range(NS):
                act(ebuf[:, n * 128:(n + 1) * 128],
                    ebuf[:, n * 128:(n + 1) * 128], AF.Exp, [EB], [EB],
                    scale=-1.0, tag="eexp")
            vop(lambda v: v.tensor_mul(ebuf[:], ebuf[:], c_mrep[:]),
                [EB, CB], [EB], tag="emask")
            for n in range(NS):
                vop(lambda v, nn=n, i=ci: v.tensor_scalar_mul(
                    ebuf[:, nn * 128:(nn + 1) * 128],
                    ebuf[:, nn * 128:(nn + 1) * 128],
                    dbtm[:, i, nn:nn + 1]), [EB, DTM], [EB], tag="edb")
            pC = pD
            for j in range(4):
                mm(psA[:, j * 512:(j + 1) * 512], c_onesb[:],
                   rowfl[:, j * 512:(j + 1) * 512], pC, [CB, RFL],
                   start=True, stop=True, tag="crowM")
            CR = B("crow")
            act(crow[:], psA[:, 0:2048], AF.Copy, pC, [CR], tag="crowev")
            vop(lambda v: v.tensor_mul(ebuf[:], ebuf[:], crow[:]), [EB, CR],
                [EB], tag="wmul")
            for lo, hi in ((1024, 2048), (512, 1024), (256, 512), (128, 256)):
                vop(lambda v, l=lo, h=hi: v.tensor_add(
                    ebuf[:, 0:l], ebuf[:, 0:l], ebuf[:, l:h]), [EB], [EB],
                    tag="fold")
            pY = pb_B(0, 2048)
            for ec in range(16):
                mm(psB[:, ec * 128:(ec + 1) * 128],
                   utm[:, ut, ec * 128:(ec + 1) * 128], ebuf[:, 0:128], pY,
                   [UT, EB], start=(ec % 4 == 0), stop=False, tag="intra")
            for ec in range(16):
                mm(psB[:, ec * 128:(ec + 1) * 128],
                   hbf[:, ec * 128:(ec + 1) * 128], cbart[:, ci, :], pY,
                   [HBB, CBT], start=False, stop=(ec % 4 == 3), tag="inter")
            T1E, YB = B("t1e"), B("ybar")
            vop(lambda v, c=cc: v.tensor_add(
                t1e[:].rearrange("p (a n) -> p a n", a=16),
                psB[:, 0:2048].rearrange("p (a n) -> p a n", a=16),
                xi[:, :, 3 + c * 128:3 + (c + 1) * 128]), pY + [UD], [T1E],
                tag="epi1")
            vop(lambda v, c=cc: v.tensor_mul(
                ybar[:], t1e[:].rearrange("p (a n) -> p a n", a=16),
                zfm[:, :, c * 128:(c + 1) * 128]), [T1E, ZB], [YB], tag="epi2")
            pS = pD
            for j in range(4):
                mm(psA[0:16, j * 512:(j + 1) * 512], dbbtm[:, ci, :],
                   utm[:, ut, j * 512:(j + 1) * 512], pS, [DBB, UT],
                   start=True, stop=True, tag="state")

            vop(lambda v, i=ci: v.scalar_tensor_tensor(
                hfp[:], hfp[:], daq[:, i, 0:1], psA[0:16, 0:2048],
                ALU.mult, ALU.add), [HB, DAQ] + pS, [HB], tag="hupd")

            vop(lambda v: v.tensor_copy(hbf[:], hfp[:]), [HB], [HBB],
                tag="hcast")
            pW = pb_B(0, 1024)
            for ec in range(16):
                for dh in range(2):
                    mm(psB[:, dh * 512:(dh + 1) * 512], ybar[:, ec, :],
                       woutT[:, ec, dh * 512:(dh + 1) * 512], pW, [YB, WOUT],
                       start=(ec == 0), stop=(ec == 15), tag="wout")
            X1C = B("x1t")
            sl = cc % 2
            vop(lambda v, c=cc, s_=sl: v.tensor_add(
                x1t[:, s_, :], psB[:, 0:1024], x_seg[:, c, :]),
                pW + [XSEG], [X1C], tag="resid")
            dma("sync", "x1w", dx1scr[gch * 128:(gch + 1) * 128, :],
                x1t[:, sl, :], [X1C], [B("x1scr")], tag="x1w")

    tap("ufm", ufm, UB_)
    tap("zfm", zfm, ZB)
    tap("lat", lat, B("lat"))
    tap("dbt", dbt, B("dbt"))
    tap("cmt", cmt, B("cmt"))
    tap("cumt", cumt, B("cumt"))
    tap("latm", latm, B("latm"))
    tap("dbbtm", dbbtm, B("dbbtm"))
    tap("cbart", cbart, B("cbart"))
    tap("daq", daq, B("daq"))
    tap("utm", utm, B("utm0"))
    tap("acoef", c_acoef, B("acoef"))
    tap("crow", crow, B("crow"))
    tap("delt", delt, B("delt"))
    tap("hfp", hfp, HB)
    tap("ebuf", ebuf, B("ebuf"))
    tap("ybar", ybar, B("ybar"))
    tap("x1scr", dx1scr, B("x1scr"))
    P.barrier(B)
    mctx.close()

    fctx = ExitStack()
    x1t2 = SB(fctx, "x1t2", [128, 2, DM], F32)
    outf = SB(fctx, "outf", [128, 2, DM], F32)
    ln2fm = SB(fctx, "ln2fm", [128, 8, L], BF16) if RUN_FFN >= 1 else None
    ln2tm = SB(fctx, "ln2tm", [128, 2, DM], BF16)
    logt = SB(fctx, "logt", [8, L], F32)
    lgc = SB(fctx, "lgc", [128, 8], F32)
    g_m1 = SB(fctx, "g_m1", [128, 1], F32)
    g_m2 = SB(fctx, "g_m2", [128, 1], F32)
    g_e1 = SB(fctx, "g_e1", [128, 8], F32)
    g_e2 = SB(fctx, "g_e2", [128, 8], F32)
    g_msk = SB(fctx, "g_msk", [128, 8], F32)
    g_w = SB(fctx, "g_w", [128, 32], F32)
    g_t = SB(fctx, "g_t", [128, 8], F32)
    gatec = SB(fctx, "gatec", [128, 8], BF16)
    gatet = SB(fctx, "gatet", [8, L], BF16)
    gfl = SB(fctx, "gfl", [1, L], BF16)
    if RUN_FFN >= 4:
        grep = SB(fctx, "grep", [128, L], BF16)
        w1fc = SB(fctx, "w1fc", [128, 3, 8, 128], BF16)
        w2dc = SB(fctx, "w2dc", [128, 3, 16, 128], BF16)
        hscl = SB(fctx, "hscl", [128, 16, L], BF16)
        htmp = SB(fctx, "htmp", [128, L], BF16)
        outac = SB(fctx, "outac", [128, 8, L], BF16)
    else:
        grep = w1fc = w2dc = hscl = htmp = outac = None

    if RUN_FFN >= 4:
        P.op("gpsimd", None, reads=[B("x1scr")], writes=[], tag="gfence")
    X1R, LT2, LN2 = B("x1r"), B("ln2tm"), B("ln2fm")
    for tc in range(16 if RUN_FFN >= 1 else 0):
        sl = tc % 2
        dma("sync", "x1r", x1t2[:, sl, :], dx1scr[tc * 128:(tc + 1) * 128, :],
            [B("x1scr")], [X1R], tag="x1r")
        if RUN_LN < 1:
            continue
        vop(lambda v, s_=sl: v.bn_stats(lnst[:, 0:6], x1t2[:, s_, 0:512]),
            [X1R], [LS], tag="bn1f", small=True)
        vop(lambda v, s_=sl: v.bn_stats(lnst[:, 6:12], x1t2[:, s_, 512:1024]),
            [X1R], [LS], tag="bn2f", small=True)
        vop(lambda v: v.bn_aggr(lnmv[:], lnst[:]), [LS], [MV], tag="aggf", small=True)
        vop(lambda v: v.tensor_scalar_add(
            lnrs[:], lnmv[:, 1:2].broadcast_to([128, 8]), EPS),
            [MV], [RS], tag="vepsf", small=True)
        act(lnrs[:], lnrs[:], AF.Ln, [RS], [RS], tag="lnvf", small=True)
        vop(lambda v: v.tensor_scalar_mul(lnrs[:], lnrs[:], -0.5),
            [RS], [RS], tag="lnvmf", small=True)
        act(lnrs[:], lnrs[:], AF.Exp, [RS], [RS], tag="rstdf", small=True)
        if RUN_LN < 2:
            continue
        vop(lambda v, s_=sl: v.tensor_scalar(
            ln2tm[:, s_, :], x1t2[:, s_, :], lnmv[:, 0:1], None,
            ALU.subtract), [X1R, MV], [LT2], tag="ln2sub")
        vop(lambda v, s_=sl: v.tensor_scalar_mul(
            ln2tm[:, s_, :], ln2tm[:, s_, :], lnrs[:, 0:1]),
            [LT2, RS], [LT2], tag="ln2mul")
        for w in range(2 if (RUN_LN >= 3 and tc < RUN_LNT) else 0):
            c0 = w * 512
            pb = pb_A(c0, c0 + 512)
            for j in range(4):
                dc = w * 4 + j
                mm(psA[:, c0 + j * 128:c0 + (j + 1) * 128],
                   ln2tm[:, sl, dc * 128:(dc + 1) * 128], c_i128b[:], pb,
                   [LT2, CB], start=(j == 0), stop=(j == 3), tag="ln2T")
            vop(lambda v, t_=tc, ww=w, cc0=c0: v.tensor_copy(
                ln2fm[:, ww * 4:(ww + 1) * 4, t_ * 128:(t_ + 1) * 128],
                psA[:, cc0:cc0 + 512].rearrange("p (a n) -> p a n", a=4)),
                pb, [LN2], tag="ln2ev")
    pR = pb_A(0, 2048)
    for dc in range(8 if RUN_FFN >= 2 else 0):
        for ts_ in range(4):
            mm(psA[0:8, ts_ * 512:(ts_ + 1) * 512], c_wrS[:, dc, :],
               ln2fm[:, dc, ts_ * 512:(ts_ + 1) * 512], pR, [CB, LN2],
               start=(dc == 0), stop=(dc == 7), tag="rout")
    LOG = B("logt")
    if RUN_FFN >= 2:
        act(logt[:], psA[0:8, 0:2048], AF.Copy, pR, [LOG], tag="routev")
    if RUN_FFN >= 2:
        vop(lambda v: v.tensor_scalar_add(logt[:], logt[:], c_br[:]),
            [LOG, CB], [LOG], tag="routbr")
    GT = B("gatet")
    for tc in range(16 if RUN_FFN >= 3 else 0):
        pb3 = pb_B(1536, 2048)
        mm(psB[0:128, 1536:1544], logt[:, tc * 128:(tc + 1) * 128],
           c_i16f[0:8, 0:8], pb3, [LOG, CB], start=True, stop=True,
           tag="gT")
        LGC = B("lgc")
        vop(lambda v: v.tensor_copy(lgc[:], psB[:, 1536:1544]), pb3, [LGC],
            tag="lgcev", small=True)
        M1_, M2_ = B("g_m1"), B("g_m2")
        E1, E2, MSK = B("g_e1"), B("g_e2"), B("g_msk")
        W_, T_, GC = B("g_w"), B("g_t"), B("gatec")
        vop(lambda v: v.reduce_max(g_m1[:], lgc[:], axis=AXX), [LGC], [M1_],
            tag="gm1", small=True)
        vop(lambda v: v.tensor_scalar(g_e1[:], lgc[:], g_m1[:], None,
                                      ALU.is_equal), [LGC, M1_], [E1],
            tag="ge1", small=True)
        vop(lambda v: v.scalar_tensor_tensor(g_msk[:], g_e1[:], -BIG, lgc[:],
                                             ALU.mult, ALU.add), [E1, LGC],
            [MSK], tag="gmsk", small=True)
        vop(lambda v: v.reduce_max(g_m2[:], g_msk[:], axis=AXX), [MSK],
            [M2_], tag="gm2", small=True)
        vop(lambda v: v.tensor_scalar(g_e2[:], lgc[:], g_m2[:], None,
                                      ALU.is_equal), [LGC, M2_], [E2],
            tag="ge2", small=True)
        vop(lambda v: v.tensor_tensor(
            g_w[:, 0:8], g_m2[:].broadcast_to([128, 8]),
            g_m1[:].broadcast_to([128, 8]), ALU.subtract), [M1_, M2_], [W_],
            tag="gd", small=True)
        act(g_w[:, 8:16], g_w[:, 0:8], AF.Exp, [W_], [W_], scale=-1.0,
            tag="gexp", small=True)
        vop(lambda v: v.tensor_scalar_add(g_w[:, 8:16], g_w[:, 8:16], 1.0),
            [W_], [W_], tag="gp1", small=True)
        vop(lambda v: v.reciprocal(g_w[:, 16:24], g_w[:, 8:16]), [W_], [W_],
            tag="grec", small=True)
        vop(lambda v: v.tensor_scalar(g_w[:, 24:32], g_w[:, 16:24], -1.0,
                                      1.0, ALU.mult, ALU.add), [W_], [W_],
            tag="gw1", small=True)
        vop(lambda v: v.tensor_scalar_mul(g_t[:], g_e2[:], g_w[:, 16:17]),
            [E2, W_], [T_], tag="gt2", small=True)
        vop(lambda v: v.scalar_tensor_tensor(gatec[:], g_e1[:], g_w[:, 24:25],
                                             g_t[:], ALU.mult, ALU.add),
            [E1, W_, T_], [GC], tag="gate", small=True)
        mm(psB[0:8, 1544:1672], gatec[:], c_i128b[:], pb3, [GC, CB],
           start=True, stop=True, tag="gTT")
        vop(lambda v, t_=tc: v.tensor_copy(gatet[:, t_ * 128:(t_ + 1) * 128],
                                           psB[0:8, 1544:1672]), pb3, [GT],
            tag="gatetev")
    tap("logt", logt, LOG)
    tap("gatet", gatet, GT)
    OUT = B("outac")
    GFL, GR, HS, HT = B("gfl"), B("grep"), B("hscl"), B("htmp")
    for e in range(NE if RUN_FFN >= 4 else 0):
        dma("gpsimd", "gfl", gfl[:], gatet[e:e + 1, :], [GT], [GFL],
            tag="gfl")
        pG = pb_A(0, 2048)
        for j in range(4):
            mm(psA[:, j * 512:(j + 1) * 512], c_onesb[:],
               gfl[:, j * 512:(j + 1) * 512], pG, [CB, GFL],
               start=True, stop=True, tag="grepM")
        act(grep[:], psA[:, 0:2048], AF.Copy, pG, [GR], tag="grepev")
        for fc in range(16):
            sl = fc % 3
            WF = B(f"w1fc{sl}")
            dma("gpsimd", "w1", w1fc[:, sl, :, :],
                dw1S[e, fc].rearrange("(a p) n -> p a n", p=128), [], [WF],
                tag="w1d")
            ps, pbank = (psA, pb_A) if fc % 2 == 0 else (psB, pb_B)
            pH = pbank(0, 2048)
            for dc in range(8):
                for ts_ in range(4):
                    mm(ps[:, ts_ * 512:(ts_ + 1) * 512], w1fc[:, sl, dc, :],
                       ln2fm[:, dc, ts_ * 512:(ts_ + 1) * 512], pH, [WF, LN2],
                       start=(dc == 0), stop=(dc == 7), tag="w1mm")
            act(htmp[:], ps[:, 0:2048], AF.Relu, pH + [CB], [HT],
                bias=c_b1h[:, e * 16 + fc:e * 16 + fc + 1], tag="hrelu")
            vop(lambda v, f_=fc: v.tensor_mul(hscl[:, f_, :], htmp[:],
                                              grep[:]), [HT, GR], [HS],
                tag="hscale")
        for dc in range(8):
            sl = dc % 3
            WD = B(f"w2dc{sl}")
            dma("gpsimd", "w2", w2dc[:, sl, :, :],
                dw2S[e, dc].rearrange("(a p) n -> p a n", p=128), [], [WD],
                tag="w2d")
            ps, pbank = (psA, pb_A) if dc % 2 == 0 else (psB, pb_B)
            pO = pbank(0, 2048)
            if e == 0:
                for ts_ in range(4):
                    mm(ps[:, ts_ * 512:(ts_ + 1) * 512],
                       c_b2[:, dc * 128:(dc + 1) * 128],
                       gatet[:, ts_ * 512:(ts_ + 1) * 512], pO, [CB, GT],
                       start=True, stop=False, tag="b2mm")
            for fc in range(16):
                for ts_ in range(4):
                    mm(ps[:, ts_ * 512:(ts_ + 1) * 512], w2dc[:, sl, fc, :],
                       hscl[:, fc, ts_ * 512:(ts_ + 1) * 512], pO, [WD, HS],
                       start=(e != 0 and fc == 0), stop=(fc == 15),
                       tag="w2mm")
            if e == 0:
                act(outac[:, dc, :], ps[:, 0:2048], AF.Copy, pO, [OUT],
                    tag="oev")
            else:
                vop(lambda v, d=dc, p=ps: v.tensor_add(
                    outac[:, d, :], outac[:, d, :], p[:, 0:2048]),
                    pO + [OUT], [OUT], tag="oadd")
    tap("hscl", hscl, HS)
    tap("outac", outac, OUT)
    OUTS, FO = B("outstore"), B("outf")
    for tc in range(16 if RUN_FFN >= 5 else 0):
        sl = tc % 2
        dma("sync", "x1r2", x1t2[:, sl, :], dx1scr[tc * 128:(tc + 1) * 128, :],
            [X1R, FO], [X1R], tag="x1rr")
        ps, pbank = (psA, pb_A) if tc % 2 == 0 else (psB, pb_B)
        pF = pbank(0, 1024)
        for dc in range(8):
            mm(ps[:, dc * 128:(dc + 1) * 128],
               outac[:, dc, tc * 128:(tc + 1) * 128], c_i128b[:], pF,
               [OUT, CB], start=(dc % 4 == 0), stop=(dc % 4 == 3),
               tag="foutT")
        vop(lambda v, s_=sl, p=ps: v.tensor_add(
            outf[:, s_, :], p[:, 0:1024], x1t2[:, s_, :]), pF + [X1R], [FO],
            tag="fadd")
        dma("sync", "outw", dout[tc * 128:(tc + 1) * 128, :], outf[:, sl, :],
            [FO], [OUTS], tag="outw")
    P.op("sync", None, reads=[OUTS], writes=[], tag="fence")
    if dtaps:
        P.op("sync", None, reads=[B("tapout")], writes=[], tag="tapfence")

    P.finalize(nc, gctx)
    with nc.Block() as block:
        block.tensor(lambda t: P.emit_engine("tensor", t))
        block.vector(lambda v: P.emit_engine("vector", v))
        block.scalar(lambda s: P.emit_engine("scalar", s))
        block.gpsimd(lambda g: P.emit_engine("gpsimd", g))
        block.sync(lambda q: P.emit_engine("sync", q))
    fctx.close()
    gctx.close()
    return nc


_tap_shapes = {
    "ufm": ([128, 16, SEG], BF16),
    "lat": ([NS, SEG], F32),
    "dbt": ([NS, SEG], F32),
    "cmt": ([NS, SEG], F32),
    "cumt": ([NS, 2, 128], F32),
    "latm": ([128, 2, NS], F32),
    "dbbtm": ([128, 2, NS], BF16),
    "cbart": ([NS, 2, 128], BF16),
    "daq": ([NS, 2, 8], F32),
    "utm": ([128, 2, DI], BF16),
    "acoef": ([NS, 8], F32),
    "crow": ([128, DI], BF16),
    "zfm": ([128, 16, SEG], BF16),
    "delt": ([NS, SEG], F32),
    "hfp": ([NS, DI], F32),
    "ebuf": ([128, DI], BF16),
    "ybar": ([128, 16, 128], BF16),
    "x1scr": ([L, DM], F32),
    "logt": ([8, L], F32),
    "gatet": ([8, L], BF16),
    "hscl": ([128, 16, L], BF16),
    "outac": ([128, 8, L], BF16),
}


def kernel(**inputs):
    x_full = np.asarray(inputs["L0_attn_x"], np.float32)
    W = host_prep(inputs)
    taps = [(nm, *(_tap_shapes[nm])) for nm in TAPS]
    nc = build_program(taps=taps)
    in_maps = []
    for c in range(8):
        m = dict(x=np.ascontiguousarray(x_full[c]))
        m.update(W)
        in_maps.append(m)
    res = run_bass_kernel_spmd(nc, in_maps, core_ids=list(range(8)))
    out = np.stack([np.asarray(r["out"], np.float32) for r in res.results])
    if TAPS:
        kernel.tap_results = [{nm: np.asarray(r["tap_" + nm]) for nm in TAPS}
                              for r in res.results]
    return out


# version 06 (final)
# speedup vs baseline: 1.1021x; 1.0077x over previous
import numpy as np
import ml_dtypes
from contextlib import ExitStack

import concourse.bass as bass
import concourse.mybir as mybir
from concourse.bass_utils import run_bass_kernel_spmd

F32 = mybir.dt.float32
BF16 = mybir.dt.bfloat16
AF = mybir.ActivationFunctionType
ALU = mybir.AluOpType
AXX = mybir.AxisListType.X

L = 2048
DM = 1024
DI = 2048
NS = 16
SEG = 512
NSEG = L // SEG
NCH = SEG // 128
NE = 8
H = 1024
FF = 2048
BIG = 1.0e30
EPS = 1.0e-5

TAPS = []
RUN_NSEG = NSEG
RUN_FFN = 5
RUN_WIN = True
RUN_CHUNKS = NCH
RUN_LN = 3
RUN_LNT = 16

ENG = ("tensor", "vector", "scalar", "gpsimd", "sync")
STREAM_MOD = {"const": 1, "wout": 1, "xin": 4, "outw": 1,
              "winS": 3, "w1": 3,
              "w2": 3, "cfl": 1, "rfl": 1, "gfl": 1, "tap": 1}


class Buf:
    __slots__ = ("name", "writers", "readers")

    def __init__(self, name):
        self.name = name
        self.writers = {}
        self.readers = {}


class Plan:
    def __init__(self):
        self.ops = []
        self.per_eng = {e: [] for e in ENG}
        self.stream_counts = {}

    def op(self, eng, fn, reads=(), writes=(), dma_stream=None, tag="",
           small=False):
        gid = len(self.ops)
        if dma_stream is not None:
            k = self.stream_counts.get(dma_stream, 0)
            self.stream_counts[dma_stream] = k + 1
            chan = ("D", dma_stream, k % STREAM_MOD.get(dma_stream, 2))
        else:
            chan = eng
        deps = set()

        def dep_on(gid2):
            p = self.ops[gid2]
            if p["dma"] is None and p["eng"] == eng:
                return
            deps.add(gid2)

        for b in reads:
            for g2 in b.writers.values():
                dep_on(g2)
        for b in writes:
            for g2 in b.writers.values():
                dep_on(g2)
            for g2 in b.readers.values():
                dep_on(g2)
        o = dict(gid=gid, eng=eng, fn=fn, deps=deps, dma=dma_stream,
                 chan=chan, tag=tag, need_inc=False, sem=None, val=None,
                 small=small)
        self.ops.append(o)
        self.per_eng[eng].append(o)
        for b in reads:
            b.readers[chan] = gid
        for b in writes:
            b.writers[chan] = gid
        return gid

    def barrier(self, B):
        bb = B(f"__bar{len(self.ops)}")
        for e in ENG:
            self.op(e, lambda eng: eng.nop(), reads=[], writes=[bb],
                    tag="bar_w")
        for e in ENG:
            self.op(e, None, reads=[bb], writes=[], tag="bar_r")

    def finalize(self, nc, ctx):
        for o in self.ops:
            if o["eng"] == "tensor" and o["dma"] is None and o["fn"] is not None:
                o["need_inc"] = True
            for d in o["deps"]:
                p = self.ops[d]
                assert p["fn"] is not None or p["dma"] is not None, (
                    "dep on no-op " + p["tag"])
                p["need_inc"] = True
        comp_sems = {e: ctx.enter_context(nc.semaphore(f"sem_{e}"))
                     for e in ENG}
        stream_sems = {}
        counters = {e: 0 for e in ENG}
        for o in self.ops:
            if o["dma"] is not None:
                key = o["chan"]
                if key not in stream_sems:
                    stream_sems[key] = [ctx.enter_context(nc.semaphore(
                        f"dma_{key[1]}_{key[2]}")), 0]
                stream_sems[key][1] += 16
                o["need_inc"] = True
                o["sem"] = stream_sems[key][0]
                o["val"] = stream_sems[key][1]
                self.stream_final = getattr(self, "stream_final", {})
                self.stream_final[key] = stream_sems[key][1]
            elif o["need_inc"]:
                counters[o["eng"]] += 1
                o["sem"] = comp_sems[o["eng"]]
                o["val"] = counters[o["eng"]]

    def emit_engine(self, ename, eng):
        waited = {}
        for o in self.per_eng[ename]:
            pass
        for o in self.per_eng[ename]:
            for d in sorted(o["deps"]):
                p = self.ops[d]
                sem, val = p["sem"], p["val"]
                assert sem is not None, f"{o['tag']} dep {p['tag']} no sem"
                if waited.get(sem.name, -1) < val:
                    eng.wait_ge(sem, val)
                    waited[sem.name] = val
            if o["fn"] is None:
                continue
            inst = o["fn"](eng)
            if o.get("small"):
                inst = eng.drain()
            if o["need_inc"]:
                inst.then_inc(o["sem"], 16 if o["dma"] is not None else 1)
        for o in self.per_eng[ename]:
            if o["dma"] is not None:
                sem, val = o["sem"], self.stream_final[o["chan"]]
                if waited.get(sem.name, -1) < val:
                    eng.wait_ge(sem, val)
                    waited[sem.name] = val


def host_prep(inp):
    f = np.float32
    ac = np.ascontiguousarray

    def g(n):
        return np.asarray(inp[n], f)
    W_inT = ac(g("L0_attn_W_in").T)
    winS = ac(W_inT.reshape(DM, 16, 256).transpose(1, 0, 2))
    woutT = ac(g("L0_attn_W_out").T)
    wxdtT = ac(np.concatenate([g("L0_attn_W_x"), g("L0_attn_W_dt")], 0).T)
    convw = ac(g("L0_attn_conv_w").reshape(DI, 4)
               .reshape(16, 128, 4).transpose(1, 0, 2).reshape(128, 64))
    convb = ac(g("L0_attn_conv_b").reshape(16, 128).T)
    dvec = ac(g("L0_attn_D").reshape(16, 128).T)
    alog = ac(g("L0_attn_A_log").reshape(NS, 1))
    bdt = ac(g("L0_attn_b_dt").reshape(NS, 1))
    w1S = ac(g("L0_ffn_W1").reshape(NE, H, 16, 128).transpose(0, 2, 1, 3))
    w2S = ac(g("L0_ffn_W2").reshape(NE, FF, 8, 128).transpose(0, 2, 1, 3))
    b1h = ac(g("L0_ffn_b1").reshape(NE, 16, 128)
             .transpose(2, 0, 1).reshape(128, NE * 16))
    b2 = ac(g("L0_ffn_b2"))
    wrS = ac(g("L0_ffn_Wr").reshape(8, 128, 8).transpose(1, 0, 2))
    br = ac(g("L0_ffn_br").reshape(NE, 1))
    i128 = np.eye(128, dtype=f)
    i16 = np.eye(16, dtype=f)
    tau = np.arange(128)
    m1 = (tau[:, None] <= tau[None, :]).astype(f)
    m2 = (tau[:, None] > tau[None, :]).astype(f)
    ub = (-BIG * (tau[None, :] > tau[:, None])).astype(f)
    ones128 = np.ones((1, 128), dtype=f)
    maskrep = np.tile((tau[:, None] <= tau[None, :]).astype(f), (1, 16))
    return dict(maskrep=maskrep, winS=winS, woutT=woutT, wxdtT=wxdtT, convw=convw,
                convb=convb, dvec=dvec, alog=alog, bdt=bdt, w1S=w1S,
                w2S=w2S, b1h=b1h, b2=b2, wrS=wrS, br=br, i128=i128,
                i16=i16, m1=m1, m2=m2, ub=ub, ones128=ones128)


def build_program(taps=()):
    nc = bass.Bass("TRN2", target_bir_lowering=False,
                   detect_race_conditions=False)
    P = Plan()
    bufs = {}

    def B(name):
        if name not in bufs:
            bufs[name] = Buf(name)
        return bufs[name]

    dx = nc.dram_tensor("x", [L, DM], F32, kind="ExternalInput").ap()
    dwinS = nc.dram_tensor("winS", [16, DM, 256], F32, kind="ExternalInput").ap()
    dwoutT = nc.dram_tensor("woutT", [DI, DM], F32, kind="ExternalInput").ap()
    dwxdtT = nc.dram_tensor("wxdtT", [DI, 48], F32, kind="ExternalInput").ap()
    dconvw = nc.dram_tensor("convw", [128, 64], F32, kind="ExternalInput").ap()
    dconvb = nc.dram_tensor("convb", [128, 16], F32, kind="ExternalInput").ap()
    ddvec = nc.dram_tensor("dvec", [128, 16], F32, kind="ExternalInput").ap()
    dalog = nc.dram_tensor("alog", [NS, 1], F32, kind="ExternalInput").ap()
    dbdt = nc.dram_tensor("bdt", [NS, 1], F32, kind="ExternalInput").ap()
    dw1S = nc.dram_tensor("w1S", [NE, 16, H, 128], F32, kind="ExternalInput").ap()
    dw2S = nc.dram_tensor("w2S", [NE, 8, FF, 128], F32, kind="ExternalInput").ap()
    db1h = nc.dram_tensor("b1h", [128, NE * 16], F32, kind="ExternalInput").ap()
    db2 = nc.dram_tensor("b2", [NE, H], F32, kind="ExternalInput").ap()
    dwrS = nc.dram_tensor("wrS", [128, 8, 8], F32, kind="ExternalInput").ap()
    dbr = nc.dram_tensor("br", [NE, 1], F32, kind="ExternalInput").ap()
    di128 = nc.dram_tensor("i128", [128, 128], F32, kind="ExternalInput").ap()
    di16 = nc.dram_tensor("i16", [16, 16], F32, kind="ExternalInput").ap()
    dm1 = nc.dram_tensor("m1", [128, 128], F32, kind="ExternalInput").ap()
    dm2 = nc.dram_tensor("m2", [128, 128], F32, kind="ExternalInput").ap()
    dub = nc.dram_tensor("ub", [128, 128], F32, kind="ExternalInput").ap()
    dones = nc.dram_tensor("ones128", [1, 128], F32, kind="ExternalInput").ap()
    dmaskrep = nc.dram_tensor("maskrep", [128, 2048], F32, kind="ExternalInput").ap()
    dout = nc.dram_tensor("out", [L, DM], F32, kind="ExternalOutput").ap()
    dx1scr = nc.dram_tensor("x1scr", [L, DM], F32, kind="Internal").ap()
    dtaps = {}
    for (nm, shape, dt) in taps:
        dtaps[nm] = nc.dram_tensor("tap_" + nm, shape, dt,
                                   kind="ExternalOutput").ap()

    def tap(nm, tile, rbuf):
        if nm in dtaps:
            ap_ = tile if isinstance(tile, bass.AP) else tile[:]
            dma("sync", "tap", dtaps[nm], ap_, [rbuf], [B("tapout")],
                tag=f"tap_{nm}")

    gctx = ExitStack()

    def SB(c, n, s, d):
        return c.enter_context(nc.sbuf_tensor(n, s, d))

    def mm(out_ap, lhsT_ap, rhs_ap, pbufs, rbufs, start, stop, tag=""):
        P.op("tensor",
             lambda t, o=out_ap, l=lhsT_ap, r=rhs_ap, s0=start, s1=stop:
                 t.matmul(o, l, r, start=s0, stop=s1),
             reads=rbufs, writes=pbufs, tag=tag)

    def act(out_ap, in_ap, func, rbufs, wbufs, bias=0.0, scale=1.0, tag="",
            small=False):
        P.op("scalar",
             lambda s, o=out_ap, i=in_ap, f=func, b=bias, sc=scale:
                 s.activation(o, i, f, bias=b, scale=sc),
             reads=rbufs, writes=wbufs, tag=tag, small=small)

    def vop(fn, rbufs, wbufs, tag="", small=False):
        P.op("vector", fn, reads=rbufs, writes=wbufs, tag=tag, small=small)

    def dma(eng, stream, out_ap, in_ap, rbufs, wbufs, tag=""):
        P.op(eng, lambda e, o=out_ap, i=in_ap: e.dma_start(o, i),
             reads=rbufs, writes=wbufs, dma_stream=stream, tag=tag)

    def pb_A(lo, hi):
        return [B(f"psA_b{i}") for i in range(lo // 512, (hi - 1) // 512 + 1)]

    def pb_B(lo, hi):
        return [B(f"psB_b{i}") for i in range(lo // 512, (hi - 1) // 512 + 1)]

    c_i128b = SB(gctx, "c_i128b", [128, 128], BF16)
    c_i16f = SB(gctx, "c_i16f", [16, 16], F32)
    c_m1 = SB(gctx, "c_m1", [128, 128], F32)
    c_m2 = SB(gctx, "c_m2", [128, 128], F32)
    c_onesf = SB(gctx, "c_onesf", [1, 128], F32)
    c_onesb = SB(gctx, "c_onesb", [1, 128], BF16)
    c_mrep = SB(gctx, "c_mrep", [128, 2048], BF16)
    c_convw = SB(gctx, "c_convw", [128, 64], F32)
    c_convb = SB(gctx, "c_convb", [128, 16], F32)
    c_dvec = SB(gctx, "c_dvec", [128, 16], F32)
    c_alog = SB(gctx, "c_alog", [NS, 1], F32)
    c_acoef = SB(gctx, "c_acoef", [NS, 8], F32)
    c_bdt = SB(gctx, "c_bdt", [NS, 1], F32)
    c_wxdt = SB(gctx, "c_wxdt", [128, 16, 48], BF16)
    c_wrS = SB(gctx, "c_wrS", [128, 8, 8], BF16)
    c_br = SB(gctx, "c_br", [8, 1], F32)
    c_b1h = SB(gctx, "c_b1h", [128, NE * 16], F32)
    c_b2 = SB(gctx, "c_b2", [NE, H], BF16)
    lnst = SB(gctx, "lnst", [128, 12], F32)
    lnmv = SB(gctx, "lnmv", [128, 2], F32)
    lnrs = SB(gctx, "lnrs", [128, 8], F32)
    psA = gctx.enter_context(nc.psum_tensor("psA", [128, 2048], F32))
    psB = gctx.enter_context(nc.psum_tensor("psB", [128, 2048], F32))

    CB = B("consts")
    for dst, src in [
        (c_i128b[:], di128), (c_i16f[:], di16),
        (c_m1[:], dm1), (c_m2[:], dm2),
        (c_onesf[:], dones), (c_onesb[:], dones), (c_mrep[:], dmaskrep),
        (c_convw[:], dconvw), (c_convb[:], dconvb), (c_dvec[:], ddvec),
        (c_alog[:], dalog), (c_bdt[:], dbdt),
        (c_wxdt[:], dwxdtT.rearrange("(a p) k -> p a k", p=128)),
        (c_wrS[:], dwrS), (c_br[:], dbr), (c_b1h[:], db1h), (c_b2[:], db2),
    ]:
        dma("gpsimd", "const", dst, src, [], [CB], tag="const")
    ACF = B("acoef")
    act(lnrs[0:1, 0:8], lnrs[0:1, 0:8], AF.Exp, [], [B("lnrs")], tag="warm_e", small=True)
    act(lnrs[0:1, 0:8], lnrs[0:1, 0:8], AF.Ln, [B("lnrs")], [B("lnrs")],
        tag="warm_l", small=True)
    act(lnrs[0:1, 0:8], lnrs[0:1, 0:8], AF.Silu, [B("lnrs")], [B("lnrs")],
        tag="warm_s", small=True)
    act(lnrs[0:1, 0:8], lnrs[0:1, 0:8], AF.Ln, [B("lnrs")], [B("lnrs")],
        tag="warm_l2", small=True)
    act(c_acoef[:], c_alog[:].broadcast_to([NS, 8]), AF.Exp, [CB], [ACF],
        tag="acoef_exp", small=True)
    vop(lambda v: v.tensor_scalar_mul(c_acoef[:], c_acoef[:], -1.0),
        [ACF], [ACF], tag="acoef_neg", small=True)

    mctx = ExitStack()
    x1t = SB(mctx, "x1t", [128, 2, DM], F32)
    woutT = SB(mctx, "sb_woutT", [128, 16, DM], BF16)
    winSl = SB(mctx, "winSl", [128, 3, 8, 256], BF16)
    x_seg = SB(mctx, "x_seg", [128, NCH, DM], F32)
    ln1tm = SB(mctx, "ln1tm", [128, 2, DM], BF16)
    ln1fm = SB(mctx, "ln1fm", [128, 8, SEG], BF16)
    xi = SB(mctx, "xi", [128, 16, SEG + 3], BF16)
    xi_sv = SB(mctx, "xi_sv", [128, 16, 3], BF16)
    zfm = SB(mctx, "zfm", [128, 16, SEG], BF16)
    ufm = SB(mctx, "ufm", [128, 16, SEG], BF16)
    t1c = SB(mctx, "t1c", [128, SEG], F32)
    utm = SB(mctx, "utm", [128, 2, DI], BF16)
    crow = SB(mctx, "crow", [128, DI], BF16)
    ebuf = SB(mctx, "ebuf", [128, DI], BF16)
    t1e = SB(mctx, "t1e", [128, DI], BF16)
    ybar = SB(mctx, "ybar", [128, 16, 128], BF16)
    bmt = SB(mctx, "bmt", [NS, SEG], F32)
    cmt = SB(mctx, "cmt", [NS, SEG], F32)
    cmtb = SB(mctx, "cmtb", [NS, SEG], BF16)
    delt = SB(mctx, "delt", [NS, SEG], F32)
    lat = SB(mctx, "lat", [NS, SEG], F32)
    dbt = SB(mctx, "dbt", [NS, SEG], F32)
    latm = SB(mctx, "latm", [128, 2, NS], F32)
    dbtm = SB(mctx, "dbtm", [128, 2, NS], F32)
    cumtm = SB(mctx, "cumtm", [128, 2, NS], F32)
    negtm = SB(mctx, "negtm", [128, 2, NS], F32)
    revtm = SB(mctx, "revtm", [128, 2, NS], F32)
    erev = SB(mctx, "erev", [128, 2, NS], F32)
    dbbtm = SB(mctx, "dbbtm", [128, 2, NS], BF16)
    cumt = SB(mctx, "cumt", [NS, 2, 128], F32)
    ect = SB(mctx, "ect", [NS, 2, 128], F32)
    cbart = SB(mctx, "cbart", [NS, 2, 128], BF16)
    daq = SB(mctx, "daq", [NS, 2, 8], F32)
    cumfl = SB(mctx, "cumfl", [1, DI], F32)
    rowfl = SB(mctx, "rowfl", [1, DI], BF16)
    hfp = SB(mctx, "hfp", [NS, DI], F32)
    hbf = SB(mctx, "hbf", [NS, DI], BF16)

    WOUT = B("woutT")
    for ec in range(16):
        dma("gpsimd", "wout", woutT[:, ec, :],
            dwoutT[ec * 128:(ec + 1) * 128, :], [], [WOUT], tag="woutd")
    HB, HBB = B("hfp"), B("hbf")
    vop(lambda v: v.memset(hfp[:], 0.0), [], [HB], tag="h0")
    vop(lambda v: v.memset(hbf[:], 0.0), [], [HBB], tag="hbf0")
    XIB = B("xi")
    vop(lambda v: v.memset(xi[:], 0.0), [], [XIB], tag="xipad0")

    T1 = B("t1c")

    def emit_conv(ic):
        vop(lambda v, i=ic: v.tensor_scalar_mul(
            t1c[:], xi[:, i, 0:SEG], c_convw[:, 4 * i:4 * i + 1]),
            [XIB, CB], [T1], tag="cv0")
        for k in range(1, 4):
            vop(lambda v, i=ic, kk=k: v.scalar_tensor_tensor(
                t1c[:], xi[:, i, kk:SEG + kk],
                c_convw[:, 4 * i + kk:4 * i + kk + 1], t1c[:],
                ALU.mult, ALU.add), [XIB, CB, T1], [T1], tag="cvk")
        act(ufm[:, ic, :], t1c[:], AF.Silu, [T1, CB], [UB_],
            bias=c_convb[:, ic:ic + 1], tag="silu_u")

    def emit_zsilu(ic):
        act(zfm[:, ic, :], zfm[:, ic, :], AF.Silu, [ZB], [ZB], tag="silu_z")

    XSEG, LT, LF = B("x_seg"), B("ln1tm"), B("ln1fm")
    ZB, UB_, UD = B("zfm"), B("ufm"), B("xi")
    LS, MV, RS = B("lnst"), B("lnmv"), B("lnrs")

    for s in range(RUN_NSEG):
        sbase = s * SEG
        for ch in range(NCH):
            dma("sync", "xin", x_seg[:, ch, :],
                dx[sbase + ch * 128: sbase + (ch + 1) * 128, :],
                [], [XSEG], tag=f"xin{s}_{ch}")
        for ch in range(NCH if RUN_LN >= 1 else 0):
            vop(lambda v, c=ch: v.bn_stats(lnst[:, 0:6], x_seg[:, c, 0:512]),
                [XSEG], [LS], tag="bn1", small=True)
            vop(lambda v, c=ch: v.bn_stats(lnst[:, 6:12],
                                           x_seg[:, c, 512:1024]),
                [XSEG], [LS], tag="bn2", small=True)
            vop(lambda v: v.bn_aggr(lnmv[:], lnst[:]), [LS], [MV], tag="agg", small=True)
            vop(lambda v: v.tensor_scalar_add(
                lnrs[:], lnmv[:, 1:2].broadcast_to([128, 8]), EPS),
                [MV], [RS], tag="veps", small=True)
            act(lnrs[:], lnrs[:], AF.Ln, [RS], [RS], tag="lnv", small=True)
            vop(lambda v: v.tensor_scalar_mul(lnrs[:], lnrs[:], -0.5),
                [RS], [RS], tag="lnvm", small=True)
            act(lnrs[:], lnrs[:], AF.Exp, [RS], [RS], tag="rstd", small=True)
            if RUN_LN < 2:
                continue
            vop(lambda v, c=ch: v.tensor_scalar(
                ln1tm[:, c % 2, :], x_seg[:, c, :], lnmv[:, 0:1], None,
                ALU.subtract), [XSEG, MV], [LT], tag="ln1sub")
            vop(lambda v, c=ch: v.tensor_scalar_mul(
                ln1tm[:, c % 2, :], ln1tm[:, c % 2, :], lnrs[:, 0:1]),
                [LT, RS], [LT], tag="ln1mul")
            for w in range(2 if RUN_LN >= 3 else 0):
                c0 = w * 512
                pb = pb_A(c0, c0 + 512)
                for j in range(4):
                    dc = w * 4 + j
                    mm(psA[:, c0 + j * 128:c0 + (j + 1) * 128],
                       ln1tm[:, ch % 2, dc * 128:(dc + 1) * 128], c_i128b[:],
                       pb, [LT, CB], start=(j == 0), stop=(j == 3),
                       tag="ln1T")
                vop(lambda v, c=ch, ww=w, cc0=c0: v.tensor_copy(
                    ln1fm[:, ww * 4:(ww + 1) * 4, c * 128:(c + 1) * 128],
                    psA[:, cc0:cc0 + 512].rearrange("p (a n) -> p a n", a=4)),
                    pb, [LF], tag="ln1ev")
        for g in range(16 if RUN_WIN else 0):
            sl = g % 3
            WS = B(f"winSl{sl}")
            dma("gpsimd", "winS", winSl[:, sl, :, :],
                dwinS[g].rearrange("(a p) n -> p a n", p=128),
                [], [WS], tag=f"winS{s}_{g}")
            b0 = (g % 2) * 2
            pb0 = pb_A(b0 * 512, b0 * 512 + 512)
            pb1 = pb_A((b0 + 1) * 512, (b0 + 1) * 512 + 512)
            for dc in range(8):
                mm(psA[:, b0 * 512:(b0 + 1) * 512], winSl[:, sl, dc, 0:128],
                   ln1fm[:, dc, :], pb0, [WS, LF],
                   start=(dc == 0), stop=(dc == 7), tag="xzA")
                mm(psA[:, (b0 + 1) * 512:(b0 + 2) * 512],
                   winSl[:, sl, dc, 128:256], ln1fm[:, dc, :], pb1, [WS, LF],
                   start=(dc == 0), stop=(dc == 7), tag="xzB")
            if s > 0 and g == 0:
                vop(lambda v: v.tensor_copy(xi[:, :, 0:3], xi_sv[:]),
                    [B("xi_sv")], [XIB], tag="xipad", small=True)
            if g < 8:
                act(xi[:, 2 * g, 3:SEG + 3], psA[:, b0 * 512:(b0 + 1) * 512],
                    AF.Copy, pb0, [XIB], tag="evxi")
                act(xi[:, 2 * g + 1, 3:SEG + 3],
                    psA[:, (b0 + 1) * 512:(b0 + 2) * 512], AF.Copy, pb1,
                    [XIB], tag="evxi")
            else:
                act(zfm[:, 2 * (g - 8), :], psA[:, b0 * 512:(b0 + 1) * 512],
                    AF.Copy, pb0, [ZB], tag="evz")
                act(zfm[:, 2 * (g - 8) + 1, :],
                    psA[:, (b0 + 1) * 512:(b0 + 2) * 512], AF.Copy, pb1,
                    [ZB], tag="evz")
            if RUN_WIN and 1 <= g <= 8:
                emit_conv(2 * (g - 1))
                emit_conv(2 * (g - 1) + 1)
            if RUN_WIN and g >= 9:
                emit_zsilu(2 * (g - 9))
                emit_zsilu(2 * (g - 9) + 1)
        if RUN_WIN:
            emit_zsilu(14)
            emit_zsilu(15)
            vop(lambda v: v.tensor_copy(xi_sv[:], xi[:, :, SEG:SEG + 3]),
                [XIB], [B("xi_sv")], tag="xisave", small=True)

        for ic in range(16 if RUN_WIN else 0):
            vop(lambda v, i=ic: v.tensor_scalar_mul(
                xi[:, i, 3:SEG + 3], ufm[:, i, :], c_dvec[:, i:i + 1]),
                [UB_, CB], [UD], tag="uD")
        pWx = pb_A(0, 1536)
        for ec in range(16 if RUN_WIN else 0):
            for j in range(3):
                mm(psA[0:16, j * 512:(j + 1) * 512],
                   c_wxdt[:, ec, j * 16:(j + 1) * 16], ufm[:, ec, :], pWx,
                   [CB, UB_], start=(ec == 0), stop=(ec == 15), tag="wxdt")
        BM, CM, CMB = B("bmt"), B("cmt"), B("cmtb")
        DL, LA, DB = B("delt"), B("lat"), B("dbt")
        if not RUN_WIN:
            continue
        vop(lambda v: v.tensor_copy(bmt[:], psA[0:16, 0:512]), pWx, [BM],
            tag="evbm")
        vop(lambda v: v.tensor_copy(cmt[:], psA[0:16, 512:1024]), pWx, [CM],
            tag="evcm")
        vop(lambda v: v.tensor_copy(cmtb[:], cmt[:]), [CM], [CMB], tag="cmtb")
        act(delt[:], psA[0:16, 1024:1536], AF.Exp, pWx + [CB], [DL],
            bias=c_bdt[:], tag="d_exp")
        vop(lambda v: v.tensor_scalar_add(delt[:], delt[:], 1.0), [DL], [DL],
            tag="d_p1")
        act(delt[:], delt[:], AF.Ln, [DL], [DL], tag="d_ln")
        vop(lambda v: v.tensor_scalar_mul(lat[:], delt[:], c_acoef[:, 0:1]),
            [DL, ACF], [LA], tag="la")
        vop(lambda v: v.tensor_mul(dbt[:], delt[:], bmt[:]), [DL, BM], [DB],
            tag="db")

        for cc in range(min(NCH, RUN_CHUNKS)):
            ci = cc % 2
            ut = cc % 2
            gch = s * NCH + cc
            pb3 = pb_B(1536, 2048)
            mm(psB[:, 1536:1552], lat[:, cc * 128:(cc + 1) * 128], c_i16f[:],
               pb3, [LA, CB], start=True, stop=True, tag="latmM")
            mm(psB[:, 1552:1568], dbt[:, cc * 128:(cc + 1) * 128], c_i16f[:],
               pb3, [DB, CB], start=True, stop=True, tag="dbtmM")
            LTM, DTM = B("latm"), B("dbtm")
            vop(lambda v, i=ci: v.tensor_copy(latm[:, i, :], psB[:, 1536:1552]),
                pb3, [LTM], tag="latmev", small=True)
            vop(lambda v, i=ci: v.tensor_copy(dbtm[:, i, :], psB[:, 1552:1568]),
                pb3, [DTM], tag="dbtmev", small=True)
            mm(psB[:, 1568:1584], c_m1[:], latm[:, ci, :], pb3, [CB, LTM],
               start=True, stop=True, tag="cumM")
            mm(psB[:, 1584:1600], c_m2[:], latm[:, ci, :], pb3, [CB, LTM],
               start=True, stop=True, tag="revM")
            mm(psB[0:16, 1600:1728], latm[:, ci, :], c_m1[:], pb3, [LTM, CB],
               start=True, stop=True, tag="cumTM")
            CUM, REV, CT = B("cumtm"), B("revtm"), B("cumt")
            vop(lambda v, i=ci: v.tensor_copy(cumtm[:, i, :], psB[:, 1568:1584]),
                pb3, [CUM], tag="cumev", small=True)
            NEG = B("negtm")
            vop(lambda v, i=ci: v.tensor_scalar_mul(
                negtm[:, i, :], cumtm[:, i, :], -1.0), [CUM], [NEG],
                tag="negtm", small=True)
            vop(lambda v, i=ci: v.tensor_copy(revtm[:, i, :], psB[:, 1584:1600]),
                pb3, [REV], tag="revev", small=True)
            vop(lambda v, i=ci: v.tensor_copy(cumt[:, i, :],
                                              psB[0:16, 1600:1728]),
                pb3, [CT], tag="cumtev")
            ER, ECT, DAQ = B("erev"), B("ect"), B("daq")
            act(erev[:, ci, :], revtm[:, ci, :], AF.Exp, [REV], [ER], tag="er")
            act(ect[:, ci, :], cumt[:, ci, :], AF.Exp, [CT], [ECT], tag="ec")
            act(daq[:, ci, :], cumt[:, ci, 127:128].broadcast_to([NS, 8]),
                AF.Exp, [CT], [DAQ], tag="daq", small=True)
            DBB, CBT = B("dbbtm"), B("cbart")
            vop(lambda v, i=ci: v.tensor_mul(
                dbbtm[:, i, :], dbtm[:, i, :], erev[:, i, :]), [DTM, ER],
                [DBB], tag="dbb", small=True)
            vop(lambda v, i=ci, c=cc: v.tensor_mul(
                cbart[:, i, :], cmt[:, c * 128:(c + 1) * 128], ect[:, i, :]),
                [CM, ECT], [CBT], tag="cbart")
            CFL, RFL = B("cumfl"), B("rowfl")
            dma("gpsimd", "cfl", cumfl[:], cumt[:, ci, :], [CT], [CFL],
                tag="cfl")
            dma("gpsimd", "rfl", rowfl[:], cmtb[:, cc * 128:(cc + 1) * 128],
                [CMB], [RFL], tag="rfl")
            UT = B(f"utm{ut}")
            for w in range(4):
                c0 = 1024 + (w % 2) * 512
                pbu = pb_B(c0, c0 + 512)
                for j in range(4):
                    ic = w * 4 + j
                    mm(psB[:, c0 + j * 128:c0 + (j + 1) * 128],
                       ufm[:, ic, cc * 128:(cc + 1) * 128], c_i128b[:], pbu,
                       [UB_, CB], start=(j == 0), stop=(j == 3), tag="uT")
                vop(lambda v, u=ut, ww=w, cc0=c0: v.tensor_copy(
                    utm[:, u, ww * 512:(ww + 1) * 512], psB[:, cc0:cc0 + 512]),
                    pbu, [UT], tag="uTev")
            pD = pb_A(0, 2048)
            for j in range(4):
                mm(psA[:, j * 512:(j + 1) * 512], c_onesf[:],
                   cumfl[:, j * 512:(j + 1) * 512], pD, [CB, CFL],
                   start=True, stop=True, tag="diff")
            EB = B("ebuf")
            for n in range(NS):
                act(ebuf[:, n * 128:(n + 1) * 128],
                    psA[:, n * 128:(n + 1) * 128], AF.Relu, pD + [CUM], [EB],
                    bias=cumtm[:, ci, n:n + 1], scale=-1.0, tag="erelu")
            for n in range(NS):
                act(ebuf[:, n * 128:(n + 1) * 128],
                    ebuf[:, n * 128:(n + 1) * 128], AF.Exp, [EB], [EB],
                    scale=-1.0, tag="eexp")
            vop(lambda v: v.tensor_mul(ebuf[:], ebuf[:], c_mrep[:]),
                [EB, CB], [EB], tag="emask")
            for n in range(NS):
                vop(lambda v, nn=n, i=ci: v.tensor_scalar_mul(
                    ebuf[:, nn * 128:(nn + 1) * 128],
                    ebuf[:, nn * 128:(nn + 1) * 128],
                    dbtm[:, i, nn:nn + 1]), [EB, DTM], [EB], tag="edb")
            pC = pD
            for j in range(4):
                mm(psA[:, j * 512:(j + 1) * 512], c_onesb[:],
                   rowfl[:, j * 512:(j + 1) * 512], pC, [CB, RFL],
                   start=True, stop=True, tag="crowM")
            CR = B("crow")
            act(crow[:], psA[:, 0:2048], AF.Copy, pC, [CR], tag="crowev")
            vop(lambda v: v.tensor_mul(ebuf[:], ebuf[:], crow[:]), [EB, CR],
                [EB], tag="wmul")
            for lo, hi in ((1024, 2048), (512, 1024), (256, 512), (128, 256)):
                vop(lambda v, l=lo, h=hi: v.tensor_add(
                    ebuf[:, 0:l], ebuf[:, 0:l], ebuf[:, l:h]), [EB], [EB],
                    tag="fold")
            pY = pb_B(0, 2048)
            for ec in range(16):
                mm(psB[:, ec * 128:(ec + 1) * 128],
                   utm[:, ut, ec * 128:(ec + 1) * 128], ebuf[:, 0:128], pY,
                   [UT, EB], start=(ec % 4 == 0), stop=False, tag="intra")
            for ec in range(16):
                mm(psB[:, ec * 128:(ec + 1) * 128],
                   hbf[:, ec * 128:(ec + 1) * 128], cbart[:, ci, :], pY,
                   [HBB, CBT], start=False, stop=(ec % 4 == 3), tag="inter")
            T1E, YB = B("t1e"), B("ybar")
            vop(lambda v, c=cc: v.tensor_add(
                t1e[:].rearrange("p (a n) -> p a n", a=16),
                psB[:, 0:2048].rearrange("p (a n) -> p a n", a=16),
                xi[:, :, 3 + c * 128:3 + (c + 1) * 128]), pY + [UD], [T1E],
                tag="epi1")
            vop(lambda v, c=cc: v.tensor_mul(
                ybar[:], t1e[:].rearrange("p (a n) -> p a n", a=16),
                zfm[:, :, c * 128:(c + 1) * 128]), [T1E, ZB], [YB], tag="epi2")
            pS = pD
            for j in range(4):
                mm(psA[0:16, j * 512:(j + 1) * 512], dbbtm[:, ci, :],
                   utm[:, ut, j * 512:(j + 1) * 512], pS, [DBB, UT],
                   start=True, stop=True, tag="state")

            vop(lambda v, i=ci: v.scalar_tensor_tensor(
                hfp[:], hfp[:], daq[:, i, 0:1], psA[0:16, 0:2048],
                ALU.mult, ALU.add), [HB, DAQ] + pS, [HB], tag="hupd")

            vop(lambda v: v.tensor_copy(hbf[:], hfp[:]), [HB], [HBB],
                tag="hcast")
            pW = pb_B(0, 1024)
            for ec in range(16):
                for dh in range(2):
                    mm(psB[:, dh * 512:(dh + 1) * 512], ybar[:, ec, :],
                       woutT[:, ec, dh * 512:(dh + 1) * 512], pW, [YB, WOUT],
                       start=(ec == 0), stop=(ec == 15), tag="wout")
            X1C = B("x1t")
            sl = cc % 2
            vop(lambda v, c=cc, s_=sl: v.tensor_add(
                x1t[:, s_, :], psB[:, 0:1024], x_seg[:, c, :]),
                pW + [XSEG], [X1C], tag="resid")
            dma("sync", "x1w", dx1scr[gch * 128:(gch + 1) * 128, :],
                x1t[:, sl, :], [X1C], [B("x1scr")], tag="x1w")

    tap("ufm", ufm, UB_)
    tap("zfm", zfm, ZB)
    tap("lat", lat, B("lat"))
    tap("dbt", dbt, B("dbt"))
    tap("cmt", cmt, B("cmt"))
    tap("cumt", cumt, B("cumt"))
    tap("latm", latm, B("latm"))
    tap("dbbtm", dbbtm, B("dbbtm"))
    tap("cbart", cbart, B("cbart"))
    tap("daq", daq, B("daq"))
    tap("utm", utm, B("utm0"))
    tap("acoef", c_acoef, B("acoef"))
    tap("crow", crow, B("crow"))
    tap("delt", delt, B("delt"))
    tap("hfp", hfp, HB)
    tap("ebuf", ebuf, B("ebuf"))
    tap("ybar", ybar, B("ybar"))
    tap("x1scr", dx1scr, B("x1scr"))
    P.barrier(B)
    mctx.close()

    fctx = ExitStack()
    x1t2 = SB(fctx, "x1t2", [128, 2, DM], F32)
    outf = SB(fctx, "outf", [128, 2, DM], F32)
    ln2fm = SB(fctx, "ln2fm", [128, 8, L], BF16) if RUN_FFN >= 1 else None
    ln2tm = SB(fctx, "ln2tm", [128, 2, DM], BF16)
    logt = SB(fctx, "logt", [8, L], F32)
    lgc = SB(fctx, "lgc", [128, 8], F32)
    g_m1 = SB(fctx, "g_m1", [128, 1], F32)
    g_m2 = SB(fctx, "g_m2", [128, 1], F32)
    g_e1 = SB(fctx, "g_e1", [128, 8], F32)
    g_e2 = SB(fctx, "g_e2", [128, 8], F32)
    g_msk = SB(fctx, "g_msk", [128, 8], F32)
    g_w = SB(fctx, "g_w", [128, 32], F32)
    g_t = SB(fctx, "g_t", [128, 8], F32)
    gatec = SB(fctx, "gatec", [128, 8], BF16)
    gatet = SB(fctx, "gatet", [8, L], BF16)
    gfl = SB(fctx, "gfl", [1, L], BF16)
    if RUN_FFN >= 4:
        grep = SB(fctx, "grep", [128, L], BF16)
        w1fc = SB(fctx, "w1fc", [128, 3, 8, 128], BF16)
        w2dc = SB(fctx, "w2dc", [128, 3, 16, 128], BF16)
        hscl = SB(fctx, "hscl", [128, 16, L], BF16)
        htmp = SB(fctx, "htmp", [128, L], BF16)
        outac = SB(fctx, "outac", [128, 8, L], BF16)
    else:
        grep = w1fc = w2dc = hscl = htmp = outac = None

    if RUN_FFN >= 4:
        P.op("gpsimd", None, reads=[B("x1scr")], writes=[], tag="gfence")
    X1R, LT2, LN2 = B("x1r"), B("ln2tm"), B("ln2fm")
    for tc in range(16 if RUN_FFN >= 1 else 0):
        sl = tc % 2
        dma("sync", "x1r", x1t2[:, sl, :], dx1scr[tc * 128:(tc + 1) * 128, :],
            [B("x1scr")], [X1R], tag="x1r")
        if RUN_LN < 1:
            continue
        vop(lambda v, s_=sl: v.bn_stats(lnst[:, 0:6], x1t2[:, s_, 0:512]),
            [X1R], [LS], tag="bn1f", small=True)
        vop(lambda v, s_=sl: v.bn_stats(lnst[:, 6:12], x1t2[:, s_, 512:1024]),
            [X1R], [LS], tag="bn2f", small=True)
        vop(lambda v: v.bn_aggr(lnmv[:], lnst[:]), [LS], [MV], tag="aggf", small=True)
        vop(lambda v: v.tensor_scalar_add(
            lnrs[:], lnmv[:, 1:2].broadcast_to([128, 8]), EPS),
            [MV], [RS], tag="vepsf", small=True)
        act(lnrs[:], lnrs[:], AF.Ln, [RS], [RS], tag="lnvf", small=True)
        vop(lambda v: v.tensor_scalar_mul(lnrs[:], lnrs[:], -0.5),
            [RS], [RS], tag="lnvmf", small=True)
        act(lnrs[:], lnrs[:], AF.Exp, [RS], [RS], tag="rstdf", small=True)
        if RUN_LN < 2:
            continue
        vop(lambda v, s_=sl: v.tensor_scalar(
            ln2tm[:, s_, :], x1t2[:, s_, :], lnmv[:, 0:1], None,
            ALU.subtract), [X1R, MV], [LT2], tag="ln2sub")
        vop(lambda v, s_=sl: v.tensor_scalar_mul(
            ln2tm[:, s_, :], ln2tm[:, s_, :], lnrs[:, 0:1]),
            [LT2, RS], [LT2], tag="ln2mul")
        for w in range(2 if (RUN_LN >= 3 and tc < RUN_LNT) else 0):
            c0 = w * 512
            pb = pb_A(c0, c0 + 512)
            for j in range(4):
                dc = w * 4 + j
                mm(psA[:, c0 + j * 128:c0 + (j + 1) * 128],
                   ln2tm[:, sl, dc * 128:(dc + 1) * 128], c_i128b[:], pb,
                   [LT2, CB], start=(j == 0), stop=(j == 3), tag="ln2T")
            vop(lambda v, t_=tc, ww=w, cc0=c0: v.tensor_copy(
                ln2fm[:, ww * 4:(ww + 1) * 4, t_ * 128:(t_ + 1) * 128],
                psA[:, cc0:cc0 + 512].rearrange("p (a n) -> p a n", a=4)),
                pb, [LN2], tag="ln2ev")
    pR = pb_A(0, 2048)
    for dc in range(8 if RUN_FFN >= 2 else 0):
        for ts_ in range(4):
            mm(psA[0:8, ts_ * 512:(ts_ + 1) * 512], c_wrS[:, dc, :],
               ln2fm[:, dc, ts_ * 512:(ts_ + 1) * 512], pR, [CB, LN2],
               start=(dc == 0), stop=(dc == 7), tag="rout")
    LOG = B("logt")
    if RUN_FFN >= 2:
        act(logt[:], psA[0:8, 0:2048], AF.Copy, pR, [LOG], tag="routev")
    if RUN_FFN >= 2:
        vop(lambda v: v.tensor_scalar_add(logt[:], logt[:], c_br[:]),
            [LOG, CB], [LOG], tag="routbr")
    GT = B("gatet")
    for tc in range(16 if RUN_FFN >= 3 else 0):
        pb3 = pb_B(1536, 2048)
        mm(psB[0:128, 1536:1544], logt[:, tc * 128:(tc + 1) * 128],
           c_i16f[0:8, 0:8], pb3, [LOG, CB], start=True, stop=True,
           tag="gT")
        LGC = B("lgc")
        vop(lambda v: v.tensor_copy(lgc[:], psB[:, 1536:1544]), pb3, [LGC],
            tag="lgcev", small=True)
        M1_, M2_ = B("g_m1"), B("g_m2")
        E1, E2, MSK = B("g_e1"), B("g_e2"), B("g_msk")
        W_, T_, GC = B("g_w"), B("g_t"), B("gatec")
        vop(lambda v: v.reduce_max(g_m1[:], lgc[:], axis=AXX), [LGC], [M1_],
            tag="gm1", small=True)
        vop(lambda v: v.tensor_scalar(g_e1[:], lgc[:], g_m1[:], None,
                                      ALU.is_equal), [LGC, M1_], [E1],
            tag="ge1", small=True)
        vop(lambda v: v.scalar_tensor_tensor(g_msk[:], g_e1[:], -BIG, lgc[:],
                                             ALU.mult, ALU.add), [E1, LGC],
            [MSK], tag="gmsk", small=True)
        vop(lambda v: v.reduce_max(g_m2[:], g_msk[:], axis=AXX), [MSK],
            [M2_], tag="gm2", small=True)
        vop(lambda v: v.tensor_scalar(g_e2[:], lgc[:], g_m2[:], None,
                                      ALU.is_equal), [LGC, M2_], [E2],
            tag="ge2", small=True)
        vop(lambda v: v.tensor_tensor(
            g_w[:, 0:8], g_m2[:].broadcast_to([128, 8]),
            g_m1[:].broadcast_to([128, 8]), ALU.subtract), [M1_, M2_], [W_],
            tag="gd", small=True)
        act(g_w[:, 8:16], g_w[:, 0:8], AF.Exp, [W_], [W_], scale=-1.0,
            tag="gexp", small=True)
        vop(lambda v: v.tensor_scalar_add(g_w[:, 8:16], g_w[:, 8:16], 1.0),
            [W_], [W_], tag="gp1", small=True)
        vop(lambda v: v.reciprocal(g_w[:, 16:24], g_w[:, 8:16]), [W_], [W_],
            tag="grec", small=True)
        vop(lambda v: v.tensor_scalar(g_w[:, 24:32], g_w[:, 16:24], -1.0,
                                      1.0, ALU.mult, ALU.add), [W_], [W_],
            tag="gw1", small=True)
        vop(lambda v: v.tensor_scalar_mul(g_t[:], g_e2[:], g_w[:, 16:17]),
            [E2, W_], [T_], tag="gt2", small=True)
        vop(lambda v: v.scalar_tensor_tensor(gatec[:], g_e1[:], g_w[:, 24:25],
                                             g_t[:], ALU.mult, ALU.add),
            [E1, W_, T_], [GC], tag="gate", small=True)
        mm(psB[0:8, 1544:1672], gatec[:], c_i128b[:], pb3, [GC, CB],
           start=True, stop=True, tag="gTT")
        vop(lambda v, t_=tc: v.tensor_copy(gatet[:, t_ * 128:(t_ + 1) * 128],
                                           psB[0:8, 1544:1672]), pb3, [GT],
            tag="gatetev")
    tap("logt", logt, LOG)
    tap("gatet", gatet, GT)
    OUT = B("outac")
    GFL, GR, HS, HT = B("gfl"), B("grep"), B("hscl"), B("htmp")
    for e in range(NE if RUN_FFN >= 4 else 0):
        dma("gpsimd", "gfl", gfl[:], gatet[e:e + 1, :], [GT], [GFL],
            tag="gfl")
        pG = pb_A(0, 2048)
        for j in range(4):
            mm(psA[:, j * 512:(j + 1) * 512], c_onesb[:],
               gfl[:, j * 512:(j + 1) * 512], pG, [CB, GFL],
               start=True, stop=True, tag="grepM")
        act(grep[:], psA[:, 0:2048], AF.Copy, pG, [GR], tag="grepev")
        for fc in range(16):
            sl = fc % 3
            WF = B(f"w1fc{sl}")
            dma("gpsimd", "w1", w1fc[:, sl, :, :],
                dw1S[e, fc].rearrange("(a p) n -> p a n", p=128), [], [WF],
                tag="w1d")
            ps, pbank = (psA, pb_A) if fc % 2 == 0 else (psB, pb_B)
            pH = pbank(0, 2048)
            for dc in range(8):
                for ts_ in range(4):
                    mm(ps[:, ts_ * 512:(ts_ + 1) * 512], w1fc[:, sl, dc, :],
                       ln2fm[:, dc, ts_ * 512:(ts_ + 1) * 512], pH, [WF, LN2],
                       start=(dc == 0), stop=(dc == 7), tag="w1mm")
            act(htmp[:], ps[:, 0:2048], AF.Relu, pH + [CB], [HT],
                bias=c_b1h[:, e * 16 + fc:e * 16 + fc + 1], tag="hrelu")
            vop(lambda v, f_=fc: v.tensor_mul(hscl[:, f_, :], htmp[:],
                                              grep[:]), [HT, GR], [HS],
                tag="hscale")
        for dc in range(8):
            sl = dc % 3
            WD = B(f"w2dc{sl}")
            dma("gpsimd", "w2", w2dc[:, sl, :, :],
                dw2S[e, dc].rearrange("(a p) n -> p a n", p=128), [], [WD],
                tag="w2d")
            ps, pbank = (psA, pb_A) if dc % 2 == 0 else (psB, pb_B)
            pO = pbank(0, 2048)
            if e == 0:
                for ts_ in range(4):
                    mm(ps[:, ts_ * 512:(ts_ + 1) * 512],
                       c_b2[:, dc * 128:(dc + 1) * 128],
                       gatet[:, ts_ * 512:(ts_ + 1) * 512], pO, [CB, GT],
                       start=True, stop=False, tag="b2mm")
            for fc in range(16):
                for ts_ in range(4):
                    mm(ps[:, ts_ * 512:(ts_ + 1) * 512], w2dc[:, sl, fc, :],
                       hscl[:, fc, ts_ * 512:(ts_ + 1) * 512], pO, [WD, HS],
                       start=(e != 0 and fc == 0), stop=(fc == 15),
                       tag="w2mm")
            if e == 0:
                act(outac[:, dc, :], ps[:, 0:2048], AF.Copy, pO, [OUT],
                    tag="oev")
            else:
                vop(lambda v, d=dc, p=ps: v.tensor_add(
                    outac[:, d, :], outac[:, d, :], p[:, 0:2048]),
                    pO + [OUT], [OUT], tag="oadd")
    tap("hscl", hscl, HS)
    tap("outac", outac, OUT)
    OUTS, FO = B("outstore"), B("outf")
    for tc in range(16 if RUN_FFN >= 5 else 0):
        sl = tc % 2
        dma("sync", "x1r2", x1t2[:, sl, :], dx1scr[tc * 128:(tc + 1) * 128, :],
            [X1R, FO], [X1R], tag="x1rr")
        ps, pbank = (psA, pb_A) if tc % 2 == 0 else (psB, pb_B)
        pF = pbank(0, 1024)
        for dc in range(8):
            mm(ps[:, dc * 128:(dc + 1) * 128],
               outac[:, dc, tc * 128:(tc + 1) * 128], c_i128b[:], pF,
               [OUT, CB], start=(dc % 4 == 0), stop=(dc % 4 == 3),
               tag="foutT")
        vop(lambda v, s_=sl, p=ps: v.tensor_add(
            outf[:, s_, :], p[:, 0:1024], x1t2[:, s_, :]), pF + [X1R], [FO],
            tag="fadd")
        dma("sync", "outw", dout[tc * 128:(tc + 1) * 128, :], outf[:, sl, :],
            [FO], [OUTS], tag="outw")
    P.op("sync", None, reads=[OUTS], writes=[], tag="fence")
    if dtaps:
        P.op("sync", None, reads=[B("tapout")], writes=[], tag="tapfence")

    P.finalize(nc, gctx)
    with nc.Block() as block:
        block.tensor(lambda t: P.emit_engine("tensor", t))
        block.vector(lambda v: P.emit_engine("vector", v))
        block.scalar(lambda s: P.emit_engine("scalar", s))
        block.gpsimd(lambda g: P.emit_engine("gpsimd", g))
        block.sync(lambda q: P.emit_engine("sync", q))
    fctx.close()
    gctx.close()
    return nc


_tap_shapes = {
    "ufm": ([128, 16, SEG], BF16),
    "lat": ([NS, SEG], F32),
    "dbt": ([NS, SEG], F32),
    "cmt": ([NS, SEG], F32),
    "cumt": ([NS, 2, 128], F32),
    "latm": ([128, 2, NS], F32),
    "dbbtm": ([128, 2, NS], BF16),
    "cbart": ([NS, 2, 128], BF16),
    "daq": ([NS, 2, 8], F32),
    "utm": ([128, 2, DI], BF16),
    "acoef": ([NS, 8], F32),
    "crow": ([128, DI], BF16),
    "zfm": ([128, 16, SEG], BF16),
    "delt": ([NS, SEG], F32),
    "hfp": ([NS, DI], F32),
    "ebuf": ([128, DI], BF16),
    "ybar": ([128, 16, 128], BF16),
    "x1scr": ([L, DM], F32),
    "logt": ([8, L], F32),
    "gatet": ([8, L], BF16),
    "hscl": ([128, 16, L], BF16),
    "outac": ([128, 8, L], BF16),
}


def kernel(**inputs):
    x_full = np.asarray(inputs["L0_attn_x"], np.float32)
    W = host_prep(inputs)
    taps = [(nm, *(_tap_shapes[nm])) for nm in TAPS]
    nc = build_program(taps=taps)
    in_maps = []
    for c in range(8):
        m = dict(x=np.ascontiguousarray(x_full[c]))
        m.update(W)
        in_maps.append(m)
    res = run_bass_kernel_spmd(nc, in_maps, core_ids=list(range(8)))
    out = np.stack([np.asarray(r["out"], np.float32) for r in res.results])
    if TAPS:
        kernel.tap_results = [{nm: np.asarray(r["tap_" + nm]) for nm in TAPS}
                              for r in res.results]
    return out
